# version 06
# baseline (speedup 1.0000x reference)
import contextlib
import numpy as np
import ml_dtypes
import concourse.bass as bass
import concourse.mybir as mybir

F32 = mybir.dt.float32
BF16 = mybir.dt.bfloat16
AF = mybir.ActivationFunctionType
ALU = mybir.AluOpType
AX = mybir.AxisListType

ENGINES = ("tensor", "vector", "scalar", "gpsimd", "sync")
FFN_BIAS = True
FFN_STT = True

T_FULL = 16384
D = 1024
N_CORES = 8
NBLK = 16
BLK = 1024
SHARD = 2048
NTILE = 16


def core_blocks(c):
    return (c, NBLK - 1 - c)


CHUNK = 256


class Prog:
    def __init__(self, name="prog"):
        self.name = name
        self.nc = bass.Bass("TRN2", target_bir_lowering=False)
        self.es = contextlib.ExitStack()
        self.ops = {e: [] for e in ENGINES}
        self.sems = {}
        self.cnt = {}
        self.sb = {}
        self.ps = {}
        self.dram = {}
        self.n_emitted = 0
        self._consts = {}
        self._scopes = []
        self._dead = set()
        self.din("cpool", [128, 32], F32)
        self.sbuf("cpool_s", [128, 32], F32)

    def push_scope(self):
        self._scopes.append((contextlib.ExitStack(), []))

    def pop_scope(self):
        es, names = self._scopes.pop()
        es.close()
        for n in names:
            self._dead.add(n)

    def sem(self, name):
        if name not in self.sems:
            self.sems[name] = self.es.enter_context(self.nc.semaphore(name))
            self.cnt[name] = 0
        return self.sems[name]

    def sbuf(self, name, shape, dtype=F32):
        if name in self.sb:
            assert name in self._dead, f"sbuf {name} still live"
            self._dead.discard(name)
        if self._scopes:
            es, names = self._scopes[-1]
            self.sb[name] = es.enter_context(self.nc.sbuf_tensor(name, list(shape), dtype, allow_name_mangling=True))
            names.append(name)
        else:
            self.sb[name] = self.es.enter_context(self.nc.sbuf_tensor(name, list(shape), dtype))
        return self.sb[name]

    def psum(self, name, shape, dtype=F32):
        assert name not in self.ps
        self.ps[name] = self.es.enter_context(self.nc.psum_tensor(name, list(shape), dtype))
        return self.ps[name]

    def din(self, name, shape, dtype=F32):
        if name in self.dram:
            return self.dram[name]
        t = self.nc.dram_tensor(name, list(shape), dtype, kind="ExternalInput")
        self.dram[name] = t
        return t

    def dout(self, name, shape, dtype=F32):
        t = self.nc.dram_tensor(name, list(shape), dtype, kind="ExternalOutput")
        self.dram[name] = t
        return t

    def dint(self, name, shape, dtype=F32):
        if name in self.dram:
            return self.dram[name]
        t = self.nc.dram_tensor(name, list(shape), dtype)
        self.dram[name] = t
        return t

    def const_col(self, value, rows=128):
        v = float(value)
        if v not in self._consts:
            assert len(self._consts) < 32
            self._consts[v] = len(self._consts)
        j = self._consts[v]
        return self.sb["cpool_s"][0:rows, j:j + 1]

    def cpool_host(self):
        a = np.zeros((128, 32), dtype=np.float32)
        for v, j in self._consts.items():
            a[:, j] = v
        return a

    def emit(self, eng, fn):
        self.ops[eng].append(fn)
        self.n_emitted += 1

    def inc(self, eng, fn, sem_name, by=1):
        sem = self.sem(sem_name)
        if by == 1:
            self.ops[eng].append(lambda e, f=fn, s=sem: f(e).then_inc(s))
        else:
            self.ops[eng].append(lambda e, f=fn, s=sem, b=by: f(e).then_inc(s, b))
        self.cnt[sem_name] += by
        self.n_emitted += 1
        return self.cnt[sem_name]

    def bump(self, eng, sem_name, by=1):
        sem = self.sem(sem_name)
        self.ops[eng].append(lambda e, s=sem, b=by: e.nop().then_inc(s, b))
        self.cnt[sem_name] += by
        self.n_emitted += 1
        return self.cnt[sem_name]

    def wait(self, eng, sem_name, value):
        sem = self.sem(sem_name)
        self.ops[eng].append(lambda e, s=sem, v=value: e.wait_ge(s, v))
        self.n_emitted += 1

    def dma(self, eng, out, in_, sem_name):
        sem = self.sem(sem_name)
        self.ops[eng].append(lambda e, o=out, i=in_, s=sem: e.dma_start(out=o, in_=i).then_inc(s, 16))
        self.cnt[sem_name] += 16
        self.n_emitted += 1
        return self.cnt[sem_name]

    def collective(self, kind, op, ins, outs, sem_name="CC"):
        sem = self.sem(sem_name)
        rg = [list(range(N_CORES))]
        self.ops["gpsimd"].append(
            lambda e, k=kind, o=op, i=ins, u=outs, s=sem, r=rg:
                e.collective_compute(k, o, replica_groups=r, ins=i, outs=u).then_inc(s))
        self.cnt[sem_name] += 1
        self.n_emitted += 1
        return self.cnt[sem_name]

    def finalize(self):
        nc = self.nc
        with self.es:
            block = self.es.enter_context(nc.Block())
            engs = {
                "tensor": block.tensor, "vector": block.vector,
                "scalar": block.scalar, "gpsimd": block.gpsimd, "sync": block.sync,
            }
            for name in ENGINES:
                if not self.ops[name]:
                    continue
                dec = engs[name]

                def make(fns):
                    def body(e):
                        for f in fns:
                            f(e)
                    return body
                dec(make(self.ops[name]))
        return nc



def shard_rows(x_full, c):
    return np.concatenate([x_full[(8 * j + c) * CHUNK:(8 * j + c + 1) * CHUNK] for j in range(8)], axis=0)


def unshard_rows(shards):
    T = N_CORES * SHARD
    out = np.empty((T, shards[0].shape[1]), dtype=shards[0].dtype)
    for c in range(N_CORES):
        for j in range(8):
            out[(8 * j + c) * CHUNK:(8 * j + c + 1) * CHUNK] = shards[c][j * CHUNK:(j + 1) * CHUNK]
    return out


def emit_load_xres(p: Prog, xin_ap):
    xres = p.sb["xres"]
    src = xin_ap.rearrange("(t p) f -> t p f", p=128)
    ld = 0
    for i in range(NTILE):
        ld = p.dma("sync", xres[:, i * D:(i + 1) * D], src[i], "LD")
    return ld


def emit_store_xres(p: Prog, xout_ap):
    xres = p.sb["xres"]
    dst = xout_ap.rearrange("(t p) f -> t p f", p=128)
    st = 0
    for i in range(NTILE):
        st = p.dma("gpsimd", dst[i], xres[:, i * D:(i + 1) * D], "ST")
    return st


def emit_ln_tile(p: Prog, i, dst_name="xlnb", dst_col=0, group_slot=0):
    xres = p.sb["xres"]
    sq = p.sb["sq"]
    st = p.sb["stat"]
    xt = xres[:, i * D:(i + 1) * D]
    sqt = sq[:, group_slot * D:(group_slot + 1) * D]
    negm = st[:, group_slot * 4 + 0:group_slot * 4 + 1]
    sd = st[:, group_slot * 4 + 1:group_slot * 4 + 2]
    rstd = st[:, group_slot * 4 + 2:group_slot * 4 + 3]
    dst = p.sb[dst_name][:, dst_col:dst_col + D]

    p.emit("vector", lambda v, a=negm, b=xt: v.reduce_sum(a, b, axis=AX.X, negate=True))
    vc = p.inc("vector", lambda v, a=negm, s=p.const_col(1.0 / D): v.tensor_scalar_mul(a, a, s), "VV")
    p.wait("scalar", "VV", vc)
    sc = p.inc("scalar", lambda s, a=sqt, b=xt, m=negm: s.activation(a, b, AF.Square, bias=m, scale=np.float32(1.0)), "SS")
    p.wait("vector", "SS", sc)
    p.emit("vector", lambda v, a=sd, b=sqt: v.reduce_sum(a, b, axis=AX.X))
    vc = p.inc("vector", lambda v, a=sd, s=p.const_col(1.0 / D): v.tensor_scalar_mul(a, a, s), "VV")
    p.wait("scalar", "VV", vc)
    sc = p.inc("scalar", lambda s, a=sd: s.activation(a, a, AF.Sqrt, bias=np.float32(1e-5), scale=np.float32(1.0)), "SS")
    p.wait("vector", "SS", sc)
    rc = p.inc("vector", lambda v, a=rstd, b=sd: v.reciprocal(a, b), "VV")
    p.wait("vector", "VV", rc)
    vc = p.inc("vector", lambda v, a=dst, b=xt, m=negm, r=rstd:
               v.tensor_scalar(a, b, m, r, op0=ALU.add, op1=ALU.mult), "VV")
    return vc


def emit_transpose_tile(p: Prog, src_ap, dst_view8, ready_sem, ready_val, done_sem="TT",
                        ptr_name="ptr", evac_engine="scalar"):
    ptrs = [p.ps[ptr_name], p.ps[ptr_name + "2"]]
    identb = p.sb["identb_s"]
    eng = evac_engine
    sem = {"scalar": "SS", "vector": "VV"}[eng]
    p.wait("tensor", ready_sem, ready_val)
    p.wait("tensor", sem, p.cnt[sem])
    ec = 0
    for f in range(8):
        bank = ptrs[f % 2]
        if f >= 2:
            p.wait("tensor", sem, ec - 1)
        tc = p.inc("tensor", lambda t, o=bank[:, 0:128],
                   s=src_ap[:, f * 128:(f + 1) * 128], i=identb[:]:
                   t.transpose(o, s, i), done_sem)
        p.wait(eng, done_sem, tc)
        dst = dst_view8[:, f, :]
        if eng == "scalar":
            ec = p.inc(eng, lambda s, a=dst, b=bank[:, 0:128]:
                       s.activation(a, b, AF.Copy), sem)
        else:
            ec = p.inc(eng, lambda v, a=dst, b=bank[:, 0:128]:
                       v.tensor_copy(a, b), sem)
    return ec



def l0ffn_host_inputs(G1R, G2R, G1I, G2I, bias_r, bias_i):
    d = {}
    for nm, G1, G2 in (("r", G1R, G2R), ("i", G1I, G2I)):
        for c in range(4):
            d[f"g1{nm}{c}"] = np.ascontiguousarray(G1[0, :, c, :], dtype=np.float32)
            d[f"g2{nm}{c}"] = np.ascontiguousarray(G2[c, :, 0, :], dtype=np.float32)
    EB = np.zeros((32, 512), dtype=np.float32)
    for b in range(32):
        for e in range(16):
            EB[b, b * 16 + e] = 1.0
    EE = np.zeros((16, 512), dtype=np.float32)
    for b in range(32):
        for e in range(16):
            EE[e, b * 16 + e] = 1.0
    d["EB"], d["EE"] = EB, EE
    for q in range(4):
        DQ = np.zeros((32, 128), dtype=np.float32)
        for pp in range(128):
            DQ[8 * q + pp // 16, pp] = 1.0
        d[f"DQ{q}"] = DQ
    GQ = np.zeros((16, 128), dtype=np.float32)
    for pp in range(128):
        GQ[pp % 16, pp] = 1.0
    d["GQ"] = GQ
    d["brow"] = np.concatenate([np.asarray(bias_r), np.asarray(bias_i)]).astype(ml_dtypes.bfloat16).reshape(1, 1024)
    return d


def decl_l0ffn(p: Prog):
    for nm in ("r", "i"):
        for c in range(4):
            p.din(f"g1{nm}{c}", [32, 32], F32)
            p.din(f"g2{nm}{c}", [16, 16], F32)
    p.din("EB", [32, 512], F32)
    p.din("EE", [16, 512], F32)
    for q in range(4):
        p.din(f"DQ{q}", [32, 128], F32)
    p.din("GQ", [16, 128], F32)
    p.din("brow", [1, 1024], BF16)

    p.sbuf("g1s", [32, 8 * 32], F32)
    p.sbuf("g2s", [16, 8 * 16], F32)
    p.sbuf("EBs", [32, 512], F32)
    p.sbuf("EEs", [16, 512], F32)
    p.sbuf("DQs", [32, 4 * 128], F32)
    p.sbuf("GQs", [16, 128], F32)
    p.sbuf("brows", [1, 1024], BF16)
    p.sbuf("t1s", [32, 512], F32)
    p.sbuf("t2s", [16, 512], F32)
    p.sbuf("aexp", [128, 512], F32)
    p.sbuf("tmpw", [128, 512], F32)
    p.sbuf("wr_acc", [128, 4 * 512], F32)
    p.sbuf("wi_acc", [128, 4 * 512], F32)
    p.sbuf("wrb", [128, 4 * 512], BF16)
    p.sbuf("wib", [128, 4 * 512], BF16)
    p.sbuf("nwib", [128, 4 * 512], BF16)


def emit_build_w(p: Prog):
    nc = p.nc
    sb = p.sb
    ld = 0
    for nm_i, nm in enumerate(("r", "i")):
        for c in range(4):
            ld = p.dma("sync", sb["g1s"][:, (nm_i * 4 + c) * 32:(nm_i * 4 + c + 1) * 32], p.dram[f"g1{nm}{c}"].ap(), "LD")
            ld = p.dma("sync", sb["g2s"][:, (nm_i * 4 + c) * 16:(nm_i * 4 + c + 1) * 16], p.dram[f"g2{nm}{c}"].ap(), "LD")
    ld = p.dma("sync", sb["EBs"][:], p.dram["EB"].ap(), "LD")
    ld = p.dma("sync", sb["EEs"][:], p.dram["EE"].ap(), "LD")
    for q in range(4):
        ld = p.dma("sync", sb["DQs"][:, q * 128:(q + 1) * 128], p.dram[f"DQ{q}"].ap(), "LD")
    ld = p.dma("sync", sb["GQs"][:], p.dram["GQ"].ap(), "LD")
    ld = p.dma("sync", sb["brows"][:], p.dram["brow"].ap(), "LD")

    pT1 = p.ps["pwa"]
    pT2 = p.ps["pwb"]
    p.wait("tensor", "LD", ld)
    for nm_i, nm in enumerate(("r", "i")):
        acc = sb["wr_acc"] if nm == "r" else sb["wi_acc"]
        for c in range(4):
            g1 = sb["g1s"][:, (nm_i * 4 + c) * 32:(nm_i * 4 + c + 1) * 32]
            g2 = sb["g2s"][:, (nm_i * 4 + c) * 16:(nm_i * 4 + c + 1) * 16]
            tc = p.inc("tensor", lambda t, o=pT1[0:32, :], l=g1, r=sb["EBs"][:]:
                       t.matmul(o, l, r, start=True, stop=True), "TT")
            tc = p.inc("tensor", lambda t, o=pT2[0:16, :], l=g2, r=sb["EEs"][:]:
                       t.matmul(o, l, r, start=True, stop=True), "TT")
            p.wait("vector", "TT", tc)
            vc = p.inc("vector", lambda v, a=sb["t1s"][:], b=pT1[0:32, :]: v.tensor_copy(a, b), "VV")
            vc = p.inc("vector", lambda v, a=sb["t2s"][:], b=pT2[0:16, :]: v.tensor_copy(a, b), "VV")
            p.wait("tensor", "VV", vc)
            for q in range(4):
                tc = p.inc("tensor", lambda t, o=pT1[:], l=sb["DQs"][:, q * 128:(q + 1) * 128], r=sb["t1s"][:]:
                           t.matmul(o, l, r, start=True, stop=True), "TT")
                tc = p.inc("tensor", lambda t, o=pT2[:], l=sb["GQs"][:], r=sb["t2s"][:]:
                           t.matmul(o, l, r, start=True, stop=True), "TT")
                p.wait("vector", "TT", tc)
                vc = p.inc("vector", lambda v, a=sb["aexp"][:], b=pT1[:]: v.tensor_copy(a, b), "VV")
                vc = p.inc("vector", lambda v, a=sb["tmpw"][:], b=sb["aexp"][:], c2=pT2[:]:
                           v.tensor_tensor(a, b, c2, op=ALU.mult), "VV")
                dst = acc[:, q * 512:(q + 1) * 512]
                if c == 0:
                    vc = p.inc("vector", lambda v, a=dst, b=sb["tmpw"][:]: v.tensor_copy(a, b), "VV")
                else:
                    vc = p.inc("vector", lambda v, a=dst, b=sb["tmpw"][:]:
                               v.tensor_tensor(a, a, b, op=ALU.add), "VV")
                p.wait("tensor", "VV", vc)
    p.wait("scalar", "VV", p.cnt["VV"])
    sc = p.inc("scalar", lambda s: s.activation(p.sb["wrb"][:], p.sb["wr_acc"][:], AF.Copy), "SS")
    sc = p.inc("scalar", lambda s: s.activation(p.sb["wib"][:], p.sb["wi_acc"][:], AF.Copy), "SS")
    sc = p.inc("scalar", lambda s: s.activation(p.sb["nwib"][:], p.sb["wi_acc"][:], AF.Copy, scale=np.float32(-1.0)), "SS")
    return sc


def emit_l0ffn(p: Prog, w_ready_sc, x_ready_ld, ntiles=NTILE):
    sb = p.sb
    po = [p.ps["pA"], p.ps["pB"]]
    xlt8 = sb["xlt"][:].rearrange("p (f t) -> p f t", f=8)

    p.wait("vector", "LD", x_ready_ld)
    p.wait("tensor", "SS", w_ready_sc)
    last_sq_sc = [0, 0, 0, 0]
    last_tr_tt = [0, 0, 0, 0]
    last_evac_vv = 0
    last_ptr_ec = 0
    for i in range(ntiles):
        slot = i % 4
        if last_sq_sc[slot]:
            p.wait("vector", "SS", last_sq_sc[slot])
        if last_tr_tt[slot]:
            p.wait("vector", "TT", last_tr_tt[slot])
        vc_ln = emit_ln_tile(p, i, "xlnb", slot * D, slot)
        last_sq_sc[slot] = p.cnt["SS"] - 1
        if last_ptr_ec:
            p.wait("tensor", "SS", last_ptr_ec)
        ec = emit_transpose_tile(p, sb["xlnb"][:, slot * D:(slot + 1) * D],
                                 xlt8[:, :, i * 128:(i + 1) * 128], "VV", vc_ln)
        last_tr_tt[slot] = p.cnt["TT"]
        last_ptr_ec = ec
        p.wait("tensor", "SS", ec)
        if last_evac_vv:
            p.wait("tensor", "VV", last_evac_vv)
        t_half = []
        for half, (w_a, w_b, boff) in enumerate((("wrb", "nwib", 0), ("wib", "wrb", 512))):
            pp = po[half]
            for f in range(4):
                p.emit("tensor", lambda t, o=pp[:], l=xlt8[:, f, i * 128:(i + 1) * 128],
                       r=sb[w_a][:, f * 512:(f + 1) * 512], st=(f == 0):
                       t.matmul(o, l, r, start=st, stop=False))
            if FFN_BIAS:
                p.emit("tensor", lambda t, o=pp[:], l=sb["ones1b"][0:1, :], r=sb["brows"][0:1, boff:boff + 512]:
                       t.matmul(o, l, r, start=False, stop=False))
            for f in range(3):
                p.emit("tensor", lambda t, o=pp[:], l=xlt8[:, 4 + f, i * 128:(i + 1) * 128],
                       r=sb[w_b][:, f * 512:(f + 1) * 512]:
                       t.matmul(o, l, r, start=False, stop=False))
            t_half.append(p.inc("tensor", lambda t, o=pp[:], l=xlt8[:, 7, i * 128:(i + 1) * 128],
                                r=sb[w_b][:, 3 * 512:4 * 512]:
                                t.matmul(o, l, r, start=False, stop=True), "TT"))
        for half in range(2):
            xslice = sb["xres"][:, i * D + half * 512: i * D + (half + 1) * 512]
            p.wait("vector", "TT", t_half[half])
            if FFN_STT:
                last_evac_vv = p.inc("vector", lambda v, a=xslice, b=po[half][:], s=p.const_col(0.5):
                                     v.scalar_tensor_tensor(a, b, s, a, op0=ALU.mult, op1=ALU.add), "VV")
            else:
                last_evac_vv = p.inc("vector", lambda v, a=xslice, b=po[half][:], h=half:
                                     v.tensor_copy(a, b), "VV")
    return last_evac_vv



def decl_l0attn(p: Prog):
    p.din("wq", [1024, 1024], BF16)
    p.din("wk", [1024, 1024], BF16)
    p.din("wv", [1024, 1024], BF16)
    p.din("maskset", [2048, 256], BF16)
    p.sbuf("wqkv_b", [128, 8 * 1024], BF16)
    p.sbuf("qts", [128, 8 * 2048], BF16)
    p.sbuf("ktstage", [128, 2 * 8 * 512], BF16)
    p.sbuf("vstage", [128, 2 * 1024], BF16)
    p.sbuf("masks", [128, 16 * 256], BF16)
    p.sbuf("onesb", [128, 1], BF16)
    p.sbuf("rzcol", [128, 16], F32)
    p.dint("agk_in", [1024, 2048], BF16)
    p.dint("agk_out", [8192, 2048], BF16)
    p.dint("agv_in", [2048, 1024], BF16)
    p.dint("agv_out", [16384, 1024], BF16)
    p.dint("bar_in", [1, 8], F32)
    p.dint("bar_out", [1, 8], F32)


def l0attn_host_inputs(W_q, W_k, W_v, core):
    d = {"wq": np.ascontiguousarray(np.asarray(W_q, dtype=np.float32).astype(ml_dtypes.bfloat16)),
         "wk": np.ascontiguousarray(np.asarray(W_k, dtype=np.float32).astype(ml_dtypes.bfloat16)),
         "wv": np.ascontiguousarray(np.asarray(W_v, dtype=np.float32).astype(ml_dtypes.bfloat16))}
    kv = np.arange(128)[:, None]
    q = np.arange(256)[None, :]
    ms = np.zeros((16, 128, 256), dtype=ml_dtypes.bfloat16)
    for r in range(16):
        ms[r] = (q + (2 * core - r) * 128 >= kv).astype(ml_dtypes.bfloat16)
    d["maskset"] = ms.reshape(2048, 256)
    return d


def emit_load_weight_1024(p: Prog, dram_name, xlt_busy_sc=0):
    sb = p.sb
    ld = 0
    for hf in range(2):
        src = p.dram[dram_name].ap()[hf * 512:(hf + 1) * 512, :].rearrange("(f p) o -> p f o", p=128)
        ld = p.dma("sync", sb["wqkv_b"][:, hf * 4096:(hf + 1) * 4096].rearrange("p (f o) -> p f o", f=4), src, "LD")
    return ld


L0A_DEBUG = 0


def emit_l0attn(p: Prog, x_ready_ld):
    sb = p.sb
    xlt8 = sb["xlt"][:].rearrange("p (f t) -> p f t", f=8)

    ld = p.dma("sync", sb["masks"][:].rearrange("p (r q) -> p r q", r=16),
               p.dram["maskset"].ap().rearrange("(r p) q -> p r q", p=128), "LD")
    p.wait("vector", "LD", ld)
    p.inc("vector", lambda v: v.memset(p.sb["onesb"][:], 1.0), "VV")

    p.wait("vector", "LD", x_ready_ld)
    last_sq = [0] * 4
    last_tr = [0] * 4
    last_ec = 0
    for i in range(NTILE):
        slot = i % 4
        if last_sq[slot]:
            p.wait("vector", "SS", last_sq[slot])
        if last_tr[slot]:
            p.wait("vector", "TT", last_tr[slot])
        vc_ln = emit_ln_tile(p, i, "xlnb", slot * D, slot)
        last_sq[slot] = p.cnt["SS"] - 1
        ec = emit_transpose_tile(p, sb["xlnb"][:, slot * D:(slot + 1) * D],
                                 xlt8[:, :, i * 128:(i + 1) * 128], "VV", vc_ln)
        last_tr[slot] = p.cnt["TT"]
        last_ec = ec

    pq = [p.ps["pA"], p.ps["pB"]]
    bank_ev = [0, 0]
    kst_store = [0, 0]
    vst_store = [0, 0]
    st = 0

    def mm_group(bank, lhs_list, rhs_list):
        p.wait("tensor", "SS", bank_ev[bank])
        n = len(lhs_list)
        tcn = 0
        for j, (l, r) in enumerate(zip(lhs_list, rhs_list)):
            fn = (lambda t, o=pq[bank][:], l=l, r=r, s0=(j == 0), s1=(j == n - 1):
                  t.matmul(o, l, r, start=s0, stop=s1))
            if j == n - 1:
                tcn = p.inc("tensor", fn, "TT")
            else:
                p.emit("tensor", fn)
        return tcn

    for proj, wname in ((0, "wq"), (1, "wk")):
        if proj:
            p.wait("sync", "TT", p.cnt["TT"])
        wv = emit_load_weight_1024(p, wname)
        p.wait("tensor", "LD", wv)
        if proj == 0:
            p.wait("tensor", "SS", last_ec)
        for tch in range(4):
            if proj == 1:
                p.wait("scalar", "ST", kst_store[tch % 2])
            for ob in range(8):
                bank = ob % 2
                lhs = [sb["wqkv_b"][:, f * 1024 + ob * 128: f * 1024 + (ob + 1) * 128] for f in range(8)]
                rhs = [xlt8[:, f, tch * 512:(tch + 1) * 512] for f in range(8)]
                tcn = mm_group(bank, lhs, rhs)
                p.wait("scalar", "TT", tcn)
                if proj == 0:
                    dst = sb["qts"][:, ob * 2048 + tch * 512: ob * 2048 + (tch + 1) * 512]
                else:
                    dst = sb["ktstage"][:, (tch % 2) * 4096 + ob * 512:(tch % 2) * 4096 + (ob + 1) * 512]
                bank_ev[bank] = p.inc("scalar", lambda s, a=dst, b=pq[bank][:]: s.activation(a, b, AF.Copy), "SS")
            if proj == 1:
                p.wait("sync", "SS", bank_ev[1])
                for ob in range(8):
                    src = sb["ktstage"][:, (tch % 2) * 4096 + ob * 512:(tch % 2) * 4096 + (ob + 1) * 512]
                    dst = p.dram["agk_in"].ap()[ob * 128:(ob + 1) * 128, tch * 512:(tch + 1) * 512]
                    st = p.dma("sync", dst, src, "ST")
                kst_store[tch % 2] = st

    p.wait("gpsimd", "ST", st)
    cc = p.collective("AllGather", ALU.bypass, [p.dram["agk_in"].ap().opt()], [p.dram["agk_out"].ap().opt()])
    p.wait("sync", "TT", p.cnt["TT"])
    wvv = emit_load_weight_1024(p, "wv")
    p.wait("tensor", "LD", wvv)
    for tt in range(NTILE):
        vb = tt % 2
        p.wait("scalar", "ST", vst_store[vb])
        for half in range(2):
            bank = half
            lhs = [xlt8[:, f, tt * 128:(tt + 1) * 128] for f in range(8)]
            rhs = [sb["wqkv_b"][:, f * 1024 + half * 512: f * 1024 + (half + 1) * 512] for f in range(8)]
            tcn = mm_group(bank, lhs, rhs)
            p.wait("scalar", "TT", tcn)
            bank_ev[bank] = p.inc("scalar", lambda s,
                                  a=sb["vstage"][:, vb * 1024 + half * 512: vb * 1024 + (half + 1) * 512],
                                  b=pq[bank][:]: s.activation(a, b, AF.Copy), "SS")
        p.wait("sync", "SS", bank_ev[1])
        dst = p.dram["agv_in"].ap()[tt * 128:(tt + 1) * 128, :]
        st = p.dma("sync", dst, sb["vstage"][:, vb * 1024:(vb + 1) * 1024], "ST")
        vst_store[vb] = st

    p.wait("gpsimd", "ST", st)
    cc = p.collective("AllGather", ALU.bypass, [p.dram["agv_in"].ap().opt()], [p.dram["agv_out"].ap().opt()])
    cc = p.collective("AllReduce", ALU.add, [p.dram["bar_in"].ap().opt()], [p.dram["bar_out"].ap().opt()])
    p.wait("sync", "CC", cc)
    p.wait("gpsimd", "CC", cc)

    agk = p.dram["agk_out"].ap().rearrange("(r f p) t -> r f p t", f=8, p=128)
    agv = p.dram["agv_out"].ap()
    pc = [[p.ps["pc00"], p.ps["pc01"]], [p.ps["pc10"], p.ps["pc11"]]]
    psb = [p.ps["pA"], p.ps["ptr2"].bitcast(F32)]
    pz = p.ps["pB"]
    p.wait("tensor", "SS", p.cnt["SS"])
    p.wait("tensor", "VV", p.cnt["VV"])

    esc_hist = [0, 0]
    tsc_hist = [0, 0]
    pair_lk = pair_lv = 0
    last_ctx_tt = 0
    lvv = 0
    for j in range(8):
        tloc0 = j * 256
        nkv = 16 * (j + 1)

        def emit_ctx(kt, wbuf, bufp, esc, mvc, lv):
            nonlocal last_ctx_tt
            first = (kt == 0)
            last = (kt == nkv - 1)
            if mvc:
                p.wait("tensor", "VV", mvc)
            else:
                p.wait("tensor", "SS", esc)
            p.wait("tensor", "LV", lv)
            for sub in range(2):
                lhs = wbuf[:, sub * 128:(sub + 1) * 128]
                p.emit("tensor", lambda t, o=pz[:, 2 * sub:2 * sub + 1], l=lhs, r=sb["onesb"][:],
                       s0=(first and sub == 0), s1=(last and sub == 1): t.matmul(o, l, r, start=s0, stop=s1))
                for half in range(2):
                    vb0 = 4096 + bufp * 2048 + (kt % 2) * 1024
                    fn = (lambda t, o=pc[sub][half][:], l=lhs,
                          r=sb["ktstage"][:, vb0 + half * 512: vb0 + (half + 1) * 512],
                          s0=first, s1=last: t.matmul(o, l, r, start=s0, stop=s1))
                    if sub == 1 and half == 1:
                        last_ctx_tt = p.inc("tensor", fn, "TT")
                    else:
                        p.emit("tensor", fn)

        pend = None
        for kt in range(nkv):
            bufp = (kt // 2) % 2
            r = (kt // 2) % 8
            if kt % 2 == 0:
                col0 = (kt // 16) * 256
                p.wait("sync", "TT", tsc_hist[1])
                p.wait("sync", "SS", esc_hist[0])
                dst_k = sb["ktstage"][:, bufp * 2048:(bufp + 1) * 2048].rearrange("p (f t) -> p f t", f=8)
                lk = p.dma("sync", dst_k[:, 0:4, :], agk[r, 0:4, :, col0:col0 + 256].rearrange("f p t -> p f t"), "LK")
                lk = p.dma("sync", dst_k[:, 4:8, :], agk[r, 4:8, :, col0:col0 + 256].rearrange("f p t -> p f t"), "LK")
                row0 = r * 2048 + col0
                lv = p.dma("sync",
                           sb["ktstage"][:, 4096 + bufp * 2048: 4096 + (bufp + 1) * 2048].rearrange("p (s c) -> p s c", s=2),
                           agv[row0:row0 + 256, :].rearrange("(s p) c -> p s c", p=128), "LV")
                pair_lk, pair_lv = lk, lv
            lk, lv = pair_lk, pair_lv
            ps_s = psb[kt % 2]
            p.wait("tensor", "LK", lk)
            p.wait("tensor", "SS", esc_hist[1])
            kbp = sb["ktstage"][:, bufp * 2048:(bufp + 1) * 2048].rearrange("p (f t) -> p f t", f=8)
            tsc = 0
            for f in range(8):
                fn = (lambda t, o=ps_s[:, 0:256], l=kbp[:, f, (kt % 2) * 128:(kt % 2) * 128 + 128],
                      r2=sb["qts"][:, f * 2048 + tloc0: f * 2048 + tloc0 + 256],
                      s0=(f == 0), s1=(f == 7): t.matmul(o, l, r2, start=s0, stop=s1))
                if f == 7:
                    tsc = p.inc("tensor", fn, "TT")
                else:
                    p.emit("tensor", fn)
            tsc_hist = [tsc, tsc_hist[0]]
            if pend is not None:
                emit_ctx(*pend)
                pend = None
            wbuf = sb["vstage"][:, (kt % 4) * 256:(kt % 4) * 256 + 256]
            p.wait("scalar", "TT", tsc)
            esc = p.inc("scalar", lambda s, a=wbuf, b=ps_s[:, 0:256]:
                        s.activation(a, b, AF.Exp, scale=np.float32(1.0 / 32.0)), "SS")
            esc_hist = [esc, esc_hist[0]]
            rel = kt - 16 * j
            mvc = 0
            if rel >= 0:
                p.wait("vector", "SS", esc)
                mvc = p.inc("vector", lambda v, a=wbuf, m=sb["masks"][:, rel * 256:(rel + 1) * 256]:
                            v.tensor_tensor(a, a, m, op=ALU.mult), "VV")
            pend = (kt, wbuf, bufp, esc, mvc, lv)
        emit_ctx(*pend)
        pend = None
        p.wait("vector", "TT", last_ctx_tt)
        if L0A_DEBUG and j == 0:
            p.inc("vector", lambda v, a=sb["sq"][:, 3088:3600], b=pc[0][0][:]:
                  v.tensor_copy(a, b), "VV")
            p.inc("vector", lambda v, a=sb["sq"][:, 3600:3728], b=sb["xres"][:, 0:128]:
                  v.tensor_copy(a, b), "VV")
        for sub in range(2):
            rz = sb["rzcol"][:, j * 2 + sub: j * 2 + sub + 1]
            rc = p.inc("vector", lambda v, a=rz, b=pz[:, 2 * sub:2 * sub + 1]: v.reciprocal(a, b), "VV")
            p.wait("vector", "VV", rc)
            tile = j * 2 + sub
            for half in range(2):
                xsl = sb["xres"][:, tile * D + half * 512: tile * D + (half + 1) * 512]
                lvv = p.inc("vector", lambda v, a=xsl, b=pc[sub][half][:], s=rz:
                            v.scalar_tensor_tensor(a, b, s, a, op0=ALU.mult, op1=ALU.add), "VV")
        p.wait("tensor", "VV", lvv)
    return lvv



MOE_C = 5120.0


def decl_moe(p: Prog):
    p.din("rwt", [1024, 8], F32)
    p.din("w1b", [8, 1024, 4096], BF16)
    p.din("w2b", [8, 4096, 1024], BF16)
    p.din("tri", [128, 128], F32)
    p.din("selmat", [16 * 128, 128], F32)
    p.din("onesrow", [1, 128], F32)
    p.din("identf", [128, 128], F32)
    p.dint("cnt_in", [1, 256], F32)
    p.dint("cnt_out", [8, 256], F32)

    p.sbuf("rwt_s", [128, 64], BF16)
    p.sbuf("rwt_f", [128, 64], F32)
    p.sbuf("sel_s", [128, 16 * 128], F32)
    p.sbuf("onesrow_s", [1, 128], F32)
    p.sbuf("identf_s", [128, 128], F32)
    p.sbuf("tri_s", [128, 128], F32)
    p.sbuf("lgt", [128, 8], F32)
    p.sbuf("prb", [128, 24], F32)
    p.sbuf("rstat", [128, 8], F32)
    p.sbuf("msk", [128, 256], F32)
    p.sbuf("gts", [128, 32], F32)
    p.sbuf("cntrow", [1, 256], F32)
    p.sbuf("cagT", [128, 16], F32)
    p.sbuf("cofs", [1, 128], F32)
    p.sbuf("rwbuf", [1, 8], F32)
    p.sbuf("keepb", [128, 8], F32)
    p.sbuf("coef", [128, 128], F32)
    p.sbuf("w1qb", [128, 2 * 8192], BF16)
    p.sbuf("w2qb", [128, 8192], BF16)
    p.sbuf("hTq", [128, 8 * 1024], BF16)


_MOE_WCACHE = {}


def moe_host_inputs(router_w, w1, w2, core):
    key = (id(w1), id(w2))
    if key not in _MOE_WCACHE:
        _MOE_WCACHE.clear()
        _MOE_WCACHE[key] = (np.ascontiguousarray(np.asarray(w1, dtype=np.float32).astype(ml_dtypes.bfloat16)),
                            np.ascontiguousarray(np.asarray(w2, dtype=np.float32).astype(ml_dtypes.bfloat16)))
    w1b, w2b = _MOE_WCACHE[key]
    d = {"rwt": np.ascontiguousarray(router_w.T, dtype=np.float32),
         "w1b": w1b,
         "w2b": w2b}
    t_ = np.arange(128)
    d["tri"] = (t_[:, None] < t_[None, :]).astype(np.float32)
    d["onesrow"] = np.ones((1, 128), dtype=np.float32)
    d["identf"] = np.eye(128, dtype=np.float32)
    sel = np.zeros((8, 2, 128, 128), dtype=np.float32)
    for cp in range(8):
        for half in range(2):
            for tl in range(8):
                tp = tl + 8 * half
                jp = tp // 2
                for j in range(8):
                    if 8 * jp + cp < 8 * j + core:
                        for k in range(2):
                            for e in range(8):
                                sel[cp, half, tl * 16 + k * 8 + e, j * 16 + k * 8 + e] = 1.0
    d["selmat"] = sel.reshape(16, 128, 128).reshape(2048, 128)
    return d


def emit_moe_weight_quarter(p: Prog, src_ap, dst_region, prev_user_tt):
    p.wait("sync", "TT", prev_user_tt)
    ld = 0
    for hf in range(2):
        src = src_ap[hf * 512:(hf + 1) * 512, :].rearrange("(f p) c -> p f c", p=128)
        ld = p.dma("sync", dst_region[:, hf * 4096:(hf + 1) * 4096].rearrange("p (f c) -> p f c", f=4), src, "LD")
    return ld


def emit_moe(p: Prog, x_ready_ld):
    sb = p.sb
    xlt8 = sb["xlt"][:].rearrange("p (f t) -> p f t", f=8)
    one_col = p.const_col(1.0)
    half_col = p.const_col(0.5)
    cap_col = p.const_col(MOE_C)
    pA, pB = p.ps["pA"], p.ps["pB"]

    ldr = p.dma("sync", sb["rwt_f"][:].rearrange("p (f e) -> p f e", f=8),
                p.dram["rwt"].ap().rearrange("(f p) e -> p f e", p=128), "LD")
    p.wait("vector", "LD", ldr)
    rwc = p.inc("vector", lambda v: v.tensor_copy(p.sb["rwt_s"][:], p.sb["rwt_f"][:]), "VV")
    for piece in range(2):
        ld = p.dma("sync", sb["sel_s"][:, piece * 1024:(piece + 1) * 1024].rearrange("p (c x) -> p c x", c=8),
                   p.dram["selmat"].ap()[piece * 1024:(piece + 1) * 1024].rearrange("(c p) x -> p c x", p=128), "LD")
    ld = p.dma("sync", sb["onesrow_s"][:], p.dram["onesrow"].ap(), "LD")
    ld = p.dma("sync", sb["identf_s"][:], p.dram["identf"].ap(), "LD")
    ld = p.dma("sync", sb["tri_s"][:], p.dram["tri"].ap(), "LD")
    ld_aux = ld

    p.wait("vector", "LD", x_ready_ld)
    last_sq = [0] * 4
    last_tr = [0] * 4
    last_ec = 0
    for i in range(NTILE):
        slot = i % 4
        if last_sq[slot]:
            p.wait("vector", "SS", last_sq[slot])
        if last_tr[slot]:
            p.wait("vector", "TT", last_tr[slot])
        vc_ln = emit_ln_tile(p, i, "xlnb", slot * D, slot)
        last_sq[slot] = p.cnt["SS"] - 1
        ec = emit_transpose_tile(p, sb["xlnb"][:, slot * D:(slot + 1) * D],
                                 xlt8[:, :, i * 128:(i + 1) * 128], "VV", vc_ln)
        last_tr[slot] = p.cnt["TT"]
        last_ec = ec

    p.wait("tensor", "LD", ld_aux)
    p.wait("tensor", "VV", rwc)
    p.wait("tensor", "SS", last_ec)
    cnt_mm_ev = 0
    lgt_ev = 0
    for t in range(NTILE):
        p.wait("tensor", "SS", lgt_ev)
        tcn = 0
        for f in range(8):
            fn = (lambda tt, o=pA[:, 0:8], l=xlt8[:, f, t * 128:(t + 1) * 128],
                  r=sb["rwt_s"][:, f * 8:(f + 1) * 8],
                  s0=(f == 0), s1=(f == 7): tt.matmul(o, l, r, start=s0, stop=s1))
            if f == 7:
                tcn = p.inc("tensor", fn, "TT")
            else:
                p.emit("tensor", fn)
        p.wait("scalar", "TT", tcn)
        lgt_ev = p.inc("scalar", lambda s: s.activation(p.sb["lgt"][:], p.ps["pA"][:, 0:8], AF.Copy), "SS")
        p.wait("vector", "SS", lgt_ev)
        m0 = sb["msk"][:, (t * 2) * 8:(t * 2) * 8 + 8]
        m1 = sb["msk"][:, (t * 2 + 1) * 8:(t * 2 + 1) * 8 + 8]

        def vstep(fn):
            c = p.inc("vector", fn, "VV")
            p.wait("vector", "VV", c)
            return c

        vstep(lambda v: v.reduce_max(p.sb["rstat"][:, 6:7], p.sb["lgt"][:], axis=AX.X))
        svc = vstep(lambda v: v.tensor_scalar(p.sb["prb"][:, 8:16], p.sb["lgt"][:], p.sb["rstat"][:, 6:7], None, op0=ALU.subtract))
        p.wait("scalar", "VV", svc)
        sexp = p.inc("scalar", lambda s: s.activation(p.sb["prb"][:, 0:8], p.sb["prb"][:, 8:16], AF.Exp), "SS")
        p.wait("vector", "SS", sexp)
        vstep(lambda v: v.reduce_sum(p.sb["rstat"][:, 7:8], p.sb["prb"][:, 0:8], axis=AX.X))
        vstep(lambda v: v.reciprocal(p.sb["rstat"][:, 5:6], p.sb["rstat"][:, 7:8]))
        vstep(lambda v: v.tensor_scalar_mul(p.sb["prb"][:, 0:8], p.sb["prb"][:, 0:8], p.sb["rstat"][:, 5:6]))
        vstep(lambda v: v.reduce_max(p.sb["rstat"][:, 0:1], p.sb["prb"][:, 0:8], axis=AX.X))
        vstep(lambda v, a=m0: v.tensor_scalar(a, p.sb["prb"][:, 0:8], p.sb["rstat"][:, 0:1], None, op0=ALU.is_ge))
        vstep(lambda v, a=m0: v.tensor_tensor(p.sb["prb"][:, 16:24], p.sb["prb"][:, 0:8], a, op=ALU.mult))
        vstep(lambda v: v.tensor_tensor(p.sb["prb"][:, 8:16], p.sb["prb"][:, 0:8], p.sb["prb"][:, 16:24], op=ALU.subtract))
        vstep(lambda v: v.reduce_max(p.sb["rstat"][:, 1:2], p.sb["prb"][:, 8:16], axis=AX.X))
        vstep(lambda v, a=m1: v.tensor_scalar(a, p.sb["prb"][:, 8:16], p.sb["rstat"][:, 1:2], None, op0=ALU.is_ge))
        vstep(lambda v: v.tensor_tensor(p.sb["rstat"][:, 2:3], p.sb["rstat"][:, 0:1], p.sb["rstat"][:, 1:2], op=ALU.add))
        vstep(lambda v: v.reciprocal(p.sb["rstat"][:, 3:4], p.sb["rstat"][:, 2:3]))
        vstep(lambda v, s=half_col: v.tensor_scalar_mul(p.sb["rstat"][:, 3:4], p.sb["rstat"][:, 3:4], s))
        vstep(lambda v, a=sb["gts"][:, t * 2:t * 2 + 1]: v.tensor_tensor(a, p.sb["rstat"][:, 0:1], p.sb["rstat"][:, 3:4], op=ALU.mult))
        mvc = vstep(lambda v, a=sb["gts"][:, t * 2 + 1:t * 2 + 2]: v.tensor_tensor(a, p.sb["rstat"][:, 1:2], p.sb["rstat"][:, 3:4], op=ALU.mult))
        p.wait("tensor", "VV", mvc)
        p.wait("tensor", "SS", cnt_mm_ev)
        for k in range(2):
            mk = sb["msk"][:, (t * 2 + k) * 8:(t * 2 + k) * 8 + 8]
            tcn2 = p.inc("tensor", lambda tt, m=mk, oc=one_col: tt.matmul(p.ps["pB"][0:1, 0:8], oc, m, start=True, stop=True), "TT")
            p.wait("scalar", "TT", tcn2)
            cnt_mm_ev = p.inc("scalar", lambda s, t_=t, k_=k: s.activation(
                p.sb["cntrow"][0:1, (t_ * 2 + k_) * 8:(t_ * 2 + k_) * 8 + 8], p.ps["pB"][0:1, 0:8], AF.Copy), "SS")

    p.wait("gpsimd", "SS", cnt_mm_ev)
    stc = p.dma("gpsimd", p.dram["cnt_in"].ap(), sb["cntrow"][:], "ST")
    p.wait("gpsimd", "ST", stc)
    cc = p.collective("AllGather", ALU.bypass, [p.dram["cnt_in"].ap().opt()], [p.dram["cnt_out"].ap().opt()])
    p.wait("sync", "CC", cc)
    p.wait("sync", "SS", p.cnt["SS"])
    ldc = p.dma("sync", sb["sq"][0:8, 0:256], p.dram["cnt_out"].ap(), "LD")
    p.wait("tensor", "LD", ldc)
    p.wait("tensor", "SS", p.cnt["SS"])
    tt1 = p.inc("tensor", lambda t: t.transpose(p.ps["pA"][:, 0:8], p.sb["sq"][0:8, 0:128], p.sb["identf_s"][0:8, 0:8]), "TT")
    tt1 = p.inc("tensor", lambda t: t.transpose(p.ps["pA"][:, 8:16], p.sb["sq"][0:8, 128:256], p.sb["identf_s"][0:8, 0:8]), "TT")
    p.wait("vector", "TT", tt1)
    vcag = p.inc("vector", lambda v: v.tensor_copy(p.sb["cagT"][:], p.ps["pA"][:, 0:16]), "VV")
    p.wait("tensor", "VV", vcag)
    ncc = 0
    for cp in range(8):
        for half in range(2):
            blk = cp * 2 + half
            fn = (lambda t, l=sb["cagT"][:, half * 8 + cp: half * 8 + cp + 1],
                  r=sb["sel_s"][:, blk * 128:(blk + 1) * 128],
                  s0=(blk == 0), s1=(blk == 15): t.matmul(p.ps["pB"][0:1, 0:128], l, r, start=s0, stop=s1))
            if blk == 15:
                ncc = p.inc("tensor", fn, "TT")
            else:
                p.emit("tensor", fn)
    p.wait("scalar", "TT", ncc)
    ofs_ev = p.inc("scalar", lambda s: s.activation(p.sb["cofs"][:], p.ps["pB"][0:1, 0:128], AF.Copy), "SS")

    p.wait("vector", "SS", ofs_ev)
    pos_ev = 0
    coef_last = 0
    for t in range(NTILE):
        j = t // 2
        for k in range(2):
            osl = sb["cofs"][0:1, j * 16 + k * 8: j * 16 + k * 8 + 8]
            if t % 2 == 0:
                rb = p.inc("vector", lambda v, o=osl: v.tensor_copy(p.sb["rwbuf"][:], o), "VV")
            else:
                rb = p.inc("vector", lambda v, o=osl, c=sb["cntrow"][0:1, ((t - 1) * 2 + k) * 8: ((t - 1) * 2 + k) * 8 + 8]:
                           v.tensor_tensor(p.sb["rwbuf"][:], o, c, op=ALU.add), "VV")
            p.wait("tensor", "VV", rb)
            p.wait("tensor", "VV", coef_last)
            mk = sb["msk"][:, (t * 2 + k) * 8:(t * 2 + k) * 8 + 8]
            p.emit("tensor", lambda tt, m=mk: tt.matmul(p.ps["pA"][:, 0:8], p.sb["tri_s"][:], m, start=True, stop=False))
            pos_ev = p.inc("tensor", lambda tt: tt.matmul(p.ps["pA"][:, 0:8], p.sb["onesrow_s"][:], p.sb["rwbuf"][:], start=False, stop=True), "TT")
            p.wait("vector", "TT", pos_ev)

            def vstep2(fn):
                c = p.inc("vector", fn, "VV")
                p.wait("vector", "VV", c)
                return c

            vstep2(lambda v, s=cap_col: v.tensor_scalar(p.sb["keepb"][:], p.ps["pA"][:, 0:8], s, None, op0=ALU.is_lt))
            vstep2(lambda v, m=mk: v.tensor_tensor(p.sb["keepb"][:], p.sb["keepb"][:], m, op=ALU.mult))
            cdst = sb["coef"][:, t * 8:(t + 1) * 8]
            if k == 0:
                coef_last = vstep2(lambda v, a=cdst, g=sb["gts"][:, t * 2:t * 2 + 1]:
                                   v.tensor_scalar_mul(a, p.sb["keepb"][:], g))
            else:
                vstep2(lambda v, g=sb["gts"][:, t * 2 + 1:t * 2 + 2]:
                       v.tensor_scalar_mul(p.sb["keepb"][:], p.sb["keepb"][:], g))
                coef_last = vstep2(lambda v, a=cdst:
                                   v.tensor_tensor(a, a, p.sb["keepb"][:], op=ALU.add))

    p.wait("tensor", "VV", p.cnt["VV"])
    p.wait("sync", "TT", p.cnt["TT"])
    wq_user_tt = [0, 0]
    w2_user_tt = 0
    last_stt = 0
    for e in range(8):
        for qr in range(4):
            wb = (e * 4 + qr) % 2
            w1r = sb["w1qb"][:, wb * 8192:(wb + 1) * 8192]
            w2r = sb["w2qb"][:]
            vw1 = emit_moe_weight_quarter(p, p.dram["w1b"].ap()[e, :, qr * 1024:(qr + 1) * 1024], w1r, wq_user_tt[wb])
            p.wait("tensor", "LD", vw1)
            vw2 = None
            pbank = [pA, pB]
            pev = [p.cnt["SS"], p.cnt["SS"]]
            pc2 = [p.ps["pc00"], p.ps["pc01"]]
            pev2 = [p.cnt["VV"], p.cnt["VV"]]
            for th in range(2):
                p.wait("tensor", "VV", pev2[0])
                p.wait("tensor", "VV", pev2[1])
                for fb in range(8):
                    for tch in range(2):
                        bk = (fb * 2 + tch) % 2
                        p.wait("tensor", "SS", pev[bk])
                        tcn = 0
                        for f in range(8):
                            fn = (lambda tt, o=pbank[bk][:], l=w1r[:, f * 1024 + fb * 128: f * 1024 + (fb + 1) * 128],
                                  r=xlt8[:, f, th * 1024 + tch * 512: th * 1024 + (tch + 1) * 512],
                                  s0=(f == 0), s1=(f == 7): tt.matmul(o, l, r, start=s0, stop=s1))
                            if f == 7:
                                tcn = p.inc("tensor", fn, "TT")
                            else:
                                p.emit("tensor", fn)
                        p.wait("scalar", "TT", tcn)
                        pev[bk] = p.inc("scalar", lambda s, a=sb["hTq"][:, fb * 1024 + tch * 512: fb * 1024 + (tch + 1) * 512],
                                        b=pbank[bk][:]: s.activation(a, b, AF.Gelu), "SS")
                ph1_all = p.cnt["SS"]
                if th == 0:
                    vw2 = emit_moe_weight_quarter(p, p.dram["w2b"].ap()[e, qr * 1024:(qr + 1) * 1024, :], w2r, w2_user_tt)
                p.wait("tensor", "SS", ph1_all)
                if th == 0:
                    p.wait("tensor", "LD", vw2)
                for tk8 in range(8):
                    tk = th * 8 + tk8
                    for ch in range(2):
                        bk = (tk8 * 2 + ch) % 2
                        p.wait("tensor", "VV", pev2[bk])
                        tcn = 0
                        for fb in range(8):
                            fn = (lambda tt, o=pc2[bk][:], l=sb["hTq"][:, fb * 1024 + tk8 * 128: fb * 1024 + (tk8 + 1) * 128],
                                  r=w2r[:, fb * 1024 + ch * 512: fb * 1024 + (ch + 1) * 512],
                                  s0=(fb == 0), s1=(fb == 7): tt.matmul(o, l, r, start=s0, stop=s1))
                            if fb == 7:
                                tcn = p.inc("tensor", fn, "TT")
                            else:
                                p.emit("tensor", fn)
                        p.wait("vector", "TT", tcn)
                        xsl = sb["xres"][:, tk * D + ch * 512: tk * D + (ch + 1) * 512]
                        pev2[bk] = p.inc("vector", lambda v, a=xsl, b=pc2[bk][:], s=sb["coef"][:, tk * 8 + e: tk * 8 + e + 1]:
                                         v.scalar_tensor_tensor(a, b, s, a, op0=ALU.mult, op1=ALU.add), "VV")
                        last_stt = pev2[bk]
            wq_user_tt[wb] = p.cnt["TT"]
            w2_user_tt = p.cnt["TT"]
    return last_stt



L1_L = 64
L1_DEBUG_STOP = 0
L1_NH = 4
L1_DK = 128
L1_DV = 256


def decl_l1attn(p: Prog):
    p.din("xpm", [128, 8], F32)
    p.din("xlw1t", [1024, 32], F32)
    p.din("xlw2t", [32, 160], F32)
    p.din("xlb2", [32, 5], F32)
    p.din("xowt", [160, 1024], F32)
    p.din("xbt", [128, 40], F32)
    p.din("rwt_l", [1024, 512], F32)
    p.din("kwt_l", [1024, 512], F32)
    p.din("vwt_l", [1024, 1024], F32)
    p.din("gwt_l", [1024, 1024], F32)
    p.din("ww1t", [1024, 64], F32)
    p.din("ww2t", [64, 512], F32)
    p.din("wbrow", [1, 512], F32)
    p.din("owt_l", [1024, 1024], F32)
    p.din("bonusrow", [1, 512], F32)
    p.din("gnwrow", [1, 1024], F32)
    p.din("gnbrow", [1, 1024], F32)
    p.din("tri64", [64, 64], F32)
    p.din("sel63", [64, 1], F32)
    p.din("mask64", [64, 64], F32)
    p.din("ident64", [64, 64], BF16)
    p.din("onesrow64", [1, 64], F32)

    p.sbuf("xpm_s", [128, 8], F32)
    p.sbuf("xlw1t_s", [128, 8 * 32], BF16)
    p.sbuf("xlw2t_s", [32, 160], BF16)
    p.sbuf("xlb2_s", [32, 5], F32)
    p.sbuf("xow_s", [32, 1024], BF16)
    p.sbuf("xbt_s", [128, 5 * 8], F32)
    p.sbuf("ww2t_s", [64, 512], BF16)
    p.sbuf("wbrow_s", [1, 512], BF16)
    p.sbuf("bonus_b", [128, 512], BF16)
    p.sbuf("gnw_b", [128, 1024], BF16)
    p.sbuf("gnb_b", [128, 1024], BF16)
    p.sbuf("tri64_s", [64, 64], F32)
    p.sbuf("sel63_s", [64, 1], F32)
    p.sbuf("mask64_s", [64, 64], F32)
    p.sbuf("ident64_s", [64, 64], BF16)
    p.sbuf("onesrow64_s", [1, 64], F32)
    p.sbuf("wbuf_l", [128, 8 * 1024], BF16)
    p.sbuf("lerp_l", [128, 8 * 512], BF16)
    p.sbuf("dlt_l", [128, 8 * 512], BF16)
    p.sbuf("x0t_l", [32, 512], BF16)
    p.sbuf("x1t_l", [32, 512], BF16)
    p.sbuf("hprevT", [128, 8 * 8], BF16)
    p.sbuf("rtmp", [128, 512], BF16)
    p.sbuf("ktmp", [128, 512], BF16)
    p.sbuf("wexpc", [128, 2 * 512], F32)
    p.sbuf("diag_l", [128, 16 * 4], F32)
    p.sbuf("S0", [128, 4 * 256], F32)
    p.sbuf("Stmp", [128, 4 * 256], BF16)
    if "mrow" not in p.sb:
        p.sbuf("mrow", [1, 1024], F32)
    p.sbuf("mt_sb", [64, 512], F32)
    p.sbuf("fac", [64, 3 * 512], BF16)
    p.sbuf("facf", [64, 2 * 512], F32)
    p.sbuf("rt_sc", [128, 512], BF16)
    p.sbuf("at_sb", [64, 4 * 64], BF16)
    p.sbuf("dcol", [128, 8], F32)
    p.sbuf("bchunk", [128, 4 * 256], BF16)
    p.sbuf("arow_st", [1, 512], F32)
    p.din("indc", [128, 8], F32)
    p.din("indc8", [8, 1], BF16)
    p.din("identf1", [1, 1], F32)
    p.din("identf64", [64, 64], F32)
    p.sbuf("indc_s", [128, 8], F32)
    p.sbuf("indc8_s", [8, 1], BF16)
    p.sbuf("scW", [64, 512], F32)
    p.sbuf("scR", [64, 512], BF16)
    p.sbuf("scK", [64, 512], BF16)
    p.sbuf("scV", [64, 1024], BF16)
    p.sbuf("scDg", [64, 4], F32)
    p.sbuf("scDm", [64, 64], BF16)
    p.sbuf("sc1", [64, 1536], BF16)
    p.sbuf("ofin", [128, 1024], BF16)
    p.sbuf("oft", [128, 8 * 128], BF16)
    p.sbuf("sgb", [128, 1024], BF16)
    p.dint("halo_pad", [65, 1024], BF16)
    p.dint("sin_d", [64 * 512, 256], BF16)
    p.dint("vdram", [SHARD, 1024], BF16)
    p.dint("kdram", [SHARD, 512], BF16)
    p.dint("rdram", [SHARD, 512], BF16)
    p.sbuf("mub", [128, 512], BF16)
    p.sbuf("tnh", [128, 64], BF16)
    p.sbuf("tnhT", [64, 128], BF16)
    p.sbuf("wstg", [128, 512], F32)
    p.sbuf("ones1bl", [1, 128], BF16)
    p.sbuf("identf1_s", [1, 1], F32)
    p.sb["bload"] = p.sb["lerp_l"]
    p.sb["sload"] = p.sb["dlt_l"]
    p.sbuf("identf_l", [64, 64], F32)
    p.sbuf("acol", [128, 256], F32)

    p.dint("gdram", [SHARD, 1024], BF16)
    p.dint("wexp_d", [SHARD, 512], F32)
    p.dint("halo_in", [8, 1024], BF16)
    p.dint("halo_out", [64, 1024], BF16)
    p.dint("agA_in", [8, 512], F32)
    p.dint("agA_out", [64, 512], F32)
    p.dint("agB_in", [8 * 512, 256], BF16)
    p.dint("agB_out", [64 * 512, 256], BF16)


def l1attn_host_inputs(z):
    g = lambda k: np.asarray(z["in_L1_attn_" + k], dtype=np.float32)
    d = {}
    d["xpm"] = np.ascontiguousarray(g("x_proj_mu").reshape(8, 128).T)
    d["xlw1t"] = np.ascontiguousarray(g("xl_w1").T)
    d["xlw2t"] = np.ascontiguousarray(g("xl_w2").T)
    d["xlb2"] = np.ascontiguousarray(g("xl_b2").reshape(5, 32).T)
    d["xowt"] = np.ascontiguousarray(g("x_out_w").T)
    d["xbt"] = np.ascontiguousarray(g("x_bias").T.reshape(8, 128, 5).transpose(1, 0, 2).reshape(128, 40))
    d["rwt_l"] = np.ascontiguousarray(g("r_w").T)
    d["kwt_l"] = np.ascontiguousarray(g("k_w").T)
    d["vwt_l"] = np.ascontiguousarray(g("v_w").T)
    d["gwt_l"] = np.ascontiguousarray(g("g_w").T)
    d["ww1t"] = np.ascontiguousarray(g("w_w1").T)
    d["ww2t"] = np.ascontiguousarray(g("w_w2").T)
    d["wbrow"] = g("w_b").reshape(1, 512)
    d["owt_l"] = np.ascontiguousarray(g("o_w").T)
    d["bonusrow"] = g("bonus").reshape(1, 512)
    d["gnwrow"] = g("gn_weight").reshape(1, 1024)
    d["gnbrow"] = g("gn_bias").reshape(1, 1024)
    t_ = np.arange(64)
    d["tri64"] = (t_[:, None] <= t_[None, :]).astype(np.float32)
    s63 = np.zeros((64, 1), dtype=np.float32)
    s63[63, 0] = 1.0
    d["sel63"] = s63
    d["mask64"] = (np.exp(40.0) * (t_[:, None] < t_[None, :])).astype(np.float32)
    d["ident64"] = np.eye(64, dtype=ml_dtypes.bfloat16)
    d["onesrow64"] = np.ones((1, 64), dtype=np.float32)
    return d


def emit_l1_weight(p: Prog, dram_name, ncols, dst_region):
    sb = p.sb
    vc = 0
    for hf in range(4):
        p.wait("sync", "VV", p.cnt["VV"])
        p.wait("sync", "SS", p.cnt["SS"])
        src = p.dram[dram_name].ap()[hf * 256:(hf + 1) * 256, :].rearrange("(f p) c -> p f c", p=128)
        ld = p.dma("sync", sb["sq"][:, 0:2 * ncols].rearrange("p (f c) -> p f c", f=2), src, "LD")
        p.wait("vector", "LD", ld)
        vc = p.inc("vector", lambda v, a=dst_region[:, hf * 2 * ncols:(hf + 1) * 2 * ncols],
                   b=sb["sq"][:, 0:2 * ncols]: v.tensor_copy(a, b), "VV")
    return vc


def emit_l1attn(p: Prog, x_ready_ld):
    sb = p.sb
    nc = p.nc
    xlt8 = sb["xlt"][:].rearrange("p (f t) -> p f t", f=8)
    pA, pB = p.ps["pA"], p.ps["pB"]
    pc00, pc01, pc10, pc11 = p.ps["pc00"], p.ps["pc01"], p.ps["pc10"], p.ps["pc11"]
    ptrs = [p.ps["ptr"], p.ps["ptr2"]]

    def vstep(fn):
        c = p.inc("vector", fn, "VV")
        p.wait("vector", "VV", c)
        return c

    def vfast(fn):
        return p.inc("vector", fn, "VV")

    ld = p.dma("sync", sb["xpm_s"][:], p.dram["xpm"].ap(), "LD")
    ld = p.dma("sync", sb["xbt_s"][:], p.dram["xbt"].ap(), "LD")
    ld = p.dma("sync", sb["xlb2_s"][0:32, :], p.dram["xlb2"].ap(), "LD")
    ld = p.dma("sync", sb["tri64_s"][0:64, :], p.dram["tri64"].ap(), "LD")
    ld = p.dma("sync", sb["sel63_s"][0:64, :], p.dram["sel63"].ap(), "LD")
    ld = p.dma("sync", sb["mask64_s"][0:64, :], p.dram["mask64"].ap(), "LD")
    ld = p.dma("sync", sb["ident64_s"][0:64, :], p.dram["ident64"].ap(), "LD")
    ld = p.dma("sync", sb["onesrow64_s"][:], p.dram["onesrow64"].ap(), "LD")
    ld = p.dma("sync", sb["onesrow_l"][:], p.dram["onesrow"].ap(), "LD")
    ld = p.dma("sync", sb["indc_s"][:], p.dram["indc"].ap(), "LD")
    ld = p.dma("sync", sb["indc8_s"][0:8, :], p.dram["indc8"].ap(), "LD")
    ld = p.dma("sync", sb["identf1_s"][0:1, :], p.dram["identf1"].ap(), "LD")
    ld = p.dma("sync", sb["identf_l"][0:64, :], p.dram["identf64"].ap(), "LD")
    ld = p.dma("sync", sb["mrow"][0:1, 0:512], p.dram["wbrow"].ap(), "LD")
    ld = p.dma("sync", sb["mrow"][0:1, 512:1024], p.dram["bonusrow"].ap(), "LD")
    ld_aux = ld
    ldw = p.dma("sync", sb["sq"][:, 0:256].rearrange("p (f c) -> p f c", f=8),
                p.dram["xlw1t"].ap().rearrange("(f p) c -> p f c", p=128), "LD")
    p.wait("vector", "LD", ldw)
    vc = p.inc("vector", lambda v: v.tensor_copy(p.sb["xlw1t_s"][:], p.sb["sq"][:, 0:256]), "VV")
    p.wait("sync", "VV", vc)
    ldw = p.dma("sync", sb["sq"][0:32, 0:160], p.dram["xlw2t"].ap(), "LD")
    p.wait("vector", "LD", ldw)
    vc = p.inc("vector", lambda v: v.tensor_copy(p.sb["xlw2t_s"][:], p.sb["sq"][0:32, 0:160]), "VV")
    p.wait("sync", "VV", vc)
    ldw = p.dma("sync", sb["sq"][0:64, 0:512], p.dram["ww2t"].ap(), "LD")
    p.wait("vector", "LD", ldw)
    vc = p.inc("vector", lambda v: v.tensor_copy(p.sb["ww2t_s"][:], p.sb["sq"][0:64, 0:512]), "VV")
    p.wait("sync", "VV", vc)
    vc = p.inc("vector", lambda v: v.tensor_copy(p.sb["wbrow_s"][:], p.sb["mrow"][0:1, 0:512]), "VV")
    vc = p.inc("vector", lambda v: v.memset(p.sb["ones1bl"][:], 1.0), "VV")
    vc_small = vc

    p.wait("vector", "LD", x_ready_ld)
    last_sq = [0] * 4
    last_tr = [0] * 4
    last_ec = 0
    for i in range(NTILE):
        slot = i % 2
        if last_sq[slot]:
            p.wait("vector", "SS", last_sq[slot])
        if last_tr[slot]:
            p.wait("vector", "TT", last_tr[slot])
        vc_ln = emit_ln_tile(p, i, "xlnb", slot * D, slot)
        last_sq[slot] = p.cnt["SS"] - 1
        if i % 2 == 1:
            j = i // 2
            p.wait("gpsimd", "VV", vc_ln)
            hs = p.dma("gpsimd", p.dram["halo_in"].ap()[j:j + 1, :],
                       sb["xlnb"][127:128, slot * D:(slot + 1) * D], "ST")
            p.wait("vector", "ST", hs)
        ec = emit_transpose_tile(p, sb["xlnb"][:, slot * D:(slot + 1) * D],
                                 xlt8[:, :, i * 128:(i + 1) * 128], "VV", vc_ln)
        last_tr[slot] = p.cnt["TT"]
        last_ec = ec

    p.wait("gpsimd", "SS", p.cnt["SS"])
    cc = p.collective("AllGather", ALU.bypass, [p.dram["halo_in"].ap().opt()], [p.dram["halo_out"].ap().opt()])
    p.wait("gpsimd", "CC", cc)
    zv = p.inc("vector", lambda v: v.memset(p.sb["lerp_l"][0:1, 0:1024], 0.0), "VV")
    p.wait("gpsimd", "VV", zv)
    st = p.dma("gpsimd", p.dram["halo_pad"].ap()[0:1, :], sb["lerp_l"][0:1, 0:1024], "ST")
    for r_ in range(64):
        c_ = r_ // 8
        j_ = r_ % 8
        m_ = 8 * j_ + c_
        st = p.dma("gpsimd", p.dram["halo_pad"].ap()[1 + m_:2 + m_, :], p.dram["halo_out"].ap()[r_:r_ + 1, :], "ST")
    p.wait("sync", "ST", st)
    hld = 0
    for j in range(8):
        hld = p.dma("sync", sb["bload"][0:8, 0:1024], p.dram["halo_pad"].ap()[8 * j:8 * j + 8, :], "LK")
        p.wait("tensor", "LK", hld)
        p.wait("tensor", "SS", p.cnt["SS"])
        t1 = p.inc("tensor", lambda t: t.matmul(p.ps["pB"][0:1, 0:512], p.sb["indc8_s"][0:8, :], p.sb["bload"][0:8, 0:512], start=True, stop=True), "TT")
        t1 = p.inc("tensor", lambda t: t.matmul(p.ps["pc11"][0:1, 0:512], p.sb["indc8_s"][0:8, :], p.sb["bload"][0:8, 512:1024], start=True, stop=True), "TT")
        p.wait("scalar", "TT", t1)
        e1 = p.inc("scalar", lambda s: s.activation(p.sb["sgb"][0:1, 0:512], p.ps["pB"][0:1, 0:512], AF.Copy), "SS")
        e1 = p.inc("scalar", lambda s: s.activation(p.sb["sgb"][0:1, 512:1024], p.ps["pc11"][0:1, 0:512], AF.Copy), "SS")
        p.wait("tensor", "SS", e1)
        tf = 0
        for f in range(8):
            tf = p.inc("tensor", lambda t, f_=f: t.transpose(p.ps["ptr"][:, 2 * f_:2 * f_ + 1],
                       p.sb["sgb"][0:1, f_ * 128:(f_ + 1) * 128], p.sb["identb_s"][0:1, 0:1]), "TT")
        p.wait("scalar", "TT", tf)
        e2 = p.inc("scalar", lambda s, j_=j: s.activation(
            p.sb["hprevT"][:, j_ * 8:(j_ + 1) * 8].rearrange("p (f o) -> p f o", o=1),
            p.ps["ptr"][:, 0:16].rearrange("p (f t) -> p f t", t=2)[:, :, 0:1], AF.Copy), "SS")
        p.wait("tensor", "SS", e2)
    hprev_ss = p.cnt["SS"]
    p.wait("sync", "SS", p.cnt["SS"])
    ldg = p.dma("sync", sb["sq"][0:1, 0:1024], p.dram["gnwrow"].ap(), "LD")
    ldg = p.dma("sync", sb["sq"][0:1, 1024:2048], p.dram["gnbrow"].ap(), "LD")
    p.wait("tensor", "LD", ldg)
    p.wait("tensor", "SS", hprev_ss)
    tb = p.inc("tensor", lambda t: t.matmul(p.ps["pc00"][:, 0:512], p.sb["onesrow_l"][0:1, 0:128], p.sb["mrow"][0:1, 512:1024], start=True, stop=True), "TT")
    tb = p.inc("tensor", lambda t: t.matmul(p.ps["pc01"][:, 0:512], p.sb["onesrow_l"][0:1, 0:128], p.sb["sq"][0:1, 0:512], start=True, stop=True), "TT")
    tb = p.inc("tensor", lambda t: t.matmul(p.ps["pc10"][:, 0:512], p.sb["onesrow_l"][0:1, 0:128], p.sb["sq"][0:1, 512:1024], start=True, stop=True), "TT")
    tb = p.inc("tensor", lambda t: t.matmul(p.ps["pc11"][:, 0:512], p.sb["onesrow_l"][0:1, 0:128], p.sb["sq"][0:1, 1024:1536], start=True, stop=True), "TT")
    tb = p.inc("tensor", lambda t: t.matmul(p.ps["pA"][:, 0:512], p.sb["onesrow_l"][0:1, 0:128], p.sb["sq"][0:1, 1536:2048], start=True, stop=True), "TT")
    p.wait("scalar", "TT", tb)
    eb = p.inc("scalar", lambda s: s.activation(p.sb["bonus_b"][:], p.ps["pc00"][:, 0:512], AF.Copy), "SS")
    eb = p.inc("scalar", lambda s: s.activation(p.sb["gnw_b"][:, 0:512], p.ps["pc01"][:, 0:512], AF.Copy), "SS")
    eb = p.inc("scalar", lambda s: s.activation(p.sb["gnw_b"][:, 512:1024], p.ps["pc10"][:, 0:512], AF.Copy), "SS")
    eb = p.inc("scalar", lambda s: s.activation(p.sb["gnb_b"][:, 0:512], p.ps["pc11"][:, 0:512], AF.Copy), "SS")
    eb = p.inc("scalar", lambda s: s.activation(p.sb["gnb_b"][:, 512:1024], p.ps["pA"][:, 0:512], AF.Copy), "SS")
    p.wait("tensor", "SS", eb)

    pq = [pA, pB]
    bank_ev = [p.cnt["SS"], p.cnt["SS"]]

    def emit_xchain(g_):
        p.wait("vector", "TT", p.cnt["TT"])
        for f in range(8):
            c0 = 512 * g_
            dsl = sb["dlt_l"][:, f * 512:(f + 1) * 512]
            xf = xlt8[:, f, :]
            j0, j1 = 2 * g_, 2 * g_ + 1
            vfast(lambda v, a=dsl[:, 1:256], i0=xf[:, c0 + 0:c0 + 255], i1=xf[:, c0 + 1:c0 + 256]:
                  v.tensor_tensor(a, i0, i1, op=ALU.subtract))
            vfast(lambda v, a=dsl[:, 257:512], i0=xf[:, c0 + 256:c0 + 511], i1=xf[:, c0 + 257:c0 + 512]:
                  v.tensor_tensor(a, i0, i1, op=ALU.subtract))
            vfast(lambda v, a=dsl[:, 0:1], i0=sb["hprevT"][:, j0 * 8 + f:j0 * 8 + f + 1], i1=xf[:, c0:c0 + 1]:
                  v.tensor_tensor(a, i0, i1, op=ALU.subtract))
            vfast(lambda v, a=dsl[:, 256:257], i0=sb["hprevT"][:, j1 * 8 + f:j1 * 8 + f + 1], i1=xf[:, c0 + 256:c0 + 257]:
                  v.tensor_tensor(a, i0, i1, op=ALU.subtract))
        for f in range(8):
            vfast(lambda v, a=sb["lerp_l"][:, f * 512:(f + 1) * 512], d=sb["dlt_l"][:, f * 512:(f + 1) * 512],
                  s=sb["xpm_s"][:, f:f + 1], x=xlt8[:, f, 512 * g_:512 * (g_ + 1)]:
                  v.scalar_tensor_tensor(a, d, s, x, op0=ALU.mult, op1=ALU.add))
        xl_vv = p.cnt["VV"]
        p.wait("tensor", "VV", xl_vv)
        p.wait("tensor", "SS", bank_ev[0])
        tcn = 0
        for f in range(8):
            fn = (lambda t, o=pA[0:32, 0:512], l=sb["xlw1t_s"][:, f * 32:(f + 1) * 32],
                  r=sb["lerp_l"][:, f * 512:(f + 1) * 512],
                  s0=(f == 0), s1=(f == 7): t.matmul(o, l, r, start=s0, stop=s1))
            if f == 7:
                tcn = p.inc("tensor", fn, "TT")
            else:
                p.emit("tensor", fn)
        p.wait("scalar", "TT", tcn)
        bank_ev[0] = p.inc("scalar", lambda s: s.activation(p.sb["x0t_l"][:], pA[0:32, 0:512], AF.Tanh), "SS")
        return bank_ev[0]

    def emit_mu_lerp(n, g_, x0_ss):
        p.wait("tensor", "SS", x0_ss)
        p.wait("tensor", "SS", bank_ev[1])
        t1 = p.inc("tensor", lambda t, n_=n: t.matmul(pB[0:32, 0:512], p.sb["xlw2t_s"][:, n_ * 32:(n_ + 1) * 32],
                   p.sb["x0t_l"][:], start=True, stop=True), "TT")
        p.wait("scalar", "TT", t1)
        bank_ev[1] = p.inc("scalar", lambda s, n_=n: s.activation(p.sb["x1t_l"][:], pB[0:32, 0:512], AF.Tanh,
                           bias=p.sb["xlb2_s"][:, n_:n_ + 1], scale=np.float32(1.0)), "SS")
        x1_ss = bank_ev[1]
        for f in range(8):
            bk = f % 2
            p.wait("tensor", "SS", x1_ss if f == 0 else 0)
            p.wait("tensor", "SS", bank_ev[bk])
            t2 = p.inc("tensor", lambda t, f_=f, o=pq[bk][:, 0:512]: t.matmul(
                o, p.sb["xow_s"][:, f_ * 128:(f_ + 1) * 128], p.sb["x1t_l"][:], start=True, stop=True), "TT")
            p.wait("scalar", "TT", t2)
            p.wait("scalar", "VV", p.cnt["VV"])
            bank_ev[bk] = p.inc("scalar", lambda s, f_=f, n_=n, o=pq[bk][:, 0:512]: s.activation(
                p.sb["mub"][:], o, AF.Identity, bias=p.sb["xbt_s"][:, f_ * 5 + n_:f_ * 5 + n_ + 1], scale=np.float32(1.0)), "SS")
            p.wait("vector", "SS", bank_ev[bk])
            vfast(lambda v, f_=f: v.tensor_tensor(p.sb["mub"][:], p.sb["dlt_l"][:, f_ * 512:(f_ + 1) * 512],
                  p.sb["mub"][:], op=ALU.mult))
            vfast(lambda v, f_=f, g2=g_: v.tensor_tensor(p.sb["lerp_l"][:, f_ * 512:(f_ + 1) * 512],
                  xlt8[:, f_, 512 * g2:512 * (g2 + 1)], p.sb["mub"][:], op=ALU.add))
        return p.cnt["VV"]

    PROJS = [("r", 0, "rwt_l", 512), ("w", 1, None, None), ("k", 2, "kwt_l", 512),
             ("v", 3, "vwt_l", 1024), ("g", 4, "gwt_l", 1024)]
    st_g = 0
    for pname, n, wdram, ncols in PROJS:
        p.wait("vector", "TT", p.cnt["TT"])
        if pname == "w":
            wv = emit_l1_weight(p, "ww1t", 64, sb["wbuf_l"])
        else:
            wv = emit_l1_weight(p, wdram, ncols, sb["wbuf_l"])
        p.wait("sync", "VV", p.cnt["VV"])
        p.wait("sync", "SS", p.cnt["SS"])
        ldw2 = p.dma("sync", sb["sq"][0:32, 0:1024], p.dram["xowt"].ap()[n * 32:(n + 1) * 32, :], "LD")
        p.wait("vector", "LD", ldw2)
        wv2 = p.inc("vector", lambda v: v.tensor_copy(p.sb["xow_s"][:], p.sb["sq"][0:32, 0:1024]), "VV")
        p.wait("tensor", "VV", wv2)
        for g_ in range(4):
            x0_ss = emit_xchain(g_)
            lerp_vv = emit_mu_lerp(n, g_, x0_ss)
            p.wait("tensor", "VV", lerp_vv)
            for tk in range(4):
                tt = g_ * 4 + tk
                if pname == "w":
                    p.wait("tensor", "SS", bank_ev[0])
                    tcn = 0
                    for f in range(8):
                        fn = (lambda t, o=pA[:, 0:64], l=sb["lerp_l"][:, f * 512 + tk * 128: f * 512 + (tk + 1) * 128],
                              r=sb["wbuf_l"][:, f * 64:(f + 1) * 64],
                              s0=(f == 0), s1=(f == 7): t.matmul(o, l, r, start=s0, stop=s1))
                        if f == 7:
                            tcn = p.inc("tensor", fn, "TT")
                        else:
                            p.emit("tensor", fn)
                    p.wait("scalar", "TT", tcn)
                    bank_ev[0] = p.inc("scalar", lambda s: s.activation(p.sb["tnh"][:], pA[:, 0:64], AF.Tanh), "SS")
                    p.wait("tensor", "SS", bank_ev[0])
                    ttr = p.inc("tensor", lambda t: t.transpose(p.ps["ptr"][0:64, 0:128],
                                p.sb["tnh"][:], p.sb["identb_s"][:]), "TT")
                    p.wait("scalar", "TT", ttr)
                    etr = p.inc("scalar", lambda s: s.activation(p.sb["tnhT"][:], p.ps["ptr"][0:64, 0:128], AF.Copy), "SS")
                    p.wait("tensor", "SS", etr)
                    p.wait("tensor", "SS", bank_ev[1])
                    p.emit("tensor", lambda t: t.matmul(pB[:, 0:512], p.sb["tnhT"][:], p.sb["ww2t_s"][:], start=True, stop=False))
                    tcn = p.inc("tensor", lambda t: t.matmul(pB[:, 0:512], p.sb["ones1bl"][:], p.sb["wbrow_s"][:], start=False, stop=True), "TT")
                    p.wait("scalar", "TT", tcn)
                    p.wait("scalar", "ST", st_g)
                    bank_ev[1] = p.inc("scalar", lambda s: s.activation(p.sb["wstg"][:], pB[:, 0:512], AF.Exp), "SS")
                    p.wait("gpsimd", "SS", bank_ev[1])
                    st_g = p.dma("gpsimd", p.dram["wexp_d"].ap()[tt * 128:(tt + 1) * 128, :], sb["wstg"][:], "ST")
                else:
                    nch = ncols // 512
                    for ch in range(nch):
                        bk = ch % 2
                        p.wait("tensor", "SS", bank_ev[bk])
                        tcn = 0
                        for f in range(8):
                            fn = (lambda t, o=pq[bk][:, 0:512], l=sb["lerp_l"][:, f * 512 + tk * 128: f * 512 + (tk + 1) * 128],
                                  r=sb["wbuf_l"][:, f * ncols + ch * 512: f * ncols + (ch + 1) * 512],
                                  s0=(f == 0), s1=(f == 7): t.matmul(o, l, r, start=s0, stop=s1))
                            if f == 7:
                                tcn = p.inc("tensor", fn, "TT")
                            else:
                                p.emit("tensor", fn)
                        p.wait("scalar", "TT", tcn)
                        dst = sb["sgb"][:, ch * 512:(ch + 1) * 512]
                        if ch == 0:
                            p.wait("scalar", "ST", st_g)
                        bank_ev[bk] = p.inc("scalar", lambda s, a=dst, o=pq[bk][:, 0:512]: s.activation(a, o, AF.Copy), "SS")
                    p.wait("gpsimd", "SS", bank_ev[(nch - 1) % 2])
                    dd = {"v": p.dram["vdram"], "g": p.dram["gdram"], "k": p.dram["kdram"], "r": p.dram["rdram"]}[pname]
                    width = 1024 if pname in ("v", "g") else 512
                    st_g = p.dma("gpsimd", dd.ap()[tt * 128:(tt + 1) * 128, :], sb["sgb"][:, 0:width], "ST")
    proj_done_ss = p.cnt["SS"]
    p.wait("gpsimd", "ST", p.cnt["ST"])

    p.wait("vector", "SS", proj_done_ss)
    for t in range(NTILE):
        p.wait("sync", "VV", p.cnt["VV"])
        lk2 = p.dma("sync", sb["ktmp"][:], p.dram["kdram"].ap()[t * 128:(t + 1) * 128, :], "LK")
        lk2 = p.dma("sync", sb["rtmp"][:], p.dram["rdram"].ap()[t * 128:(t + 1) * 128, :], "LK")
        p.wait("vector", "LK", lk2)
        vfast(lambda v: v.tensor_tensor(p.sb["mub"][:], p.sb["rtmp"][:], p.sb["bonus_b"][:], op=ALU.mult))
        vfast(lambda v: v.tensor_tensor(p.sb["mub"][:], p.sb["mub"][:], p.sb["ktmp"][:], op=ALU.mult))
        for h in range(4):
            vstep(lambda v, t_=t, h_=h: v.reduce_sum(p.sb["diag_l"][:, t_ * 4 + h_:t_ * 4 + h_ + 1],
                  p.sb["mub"][:, h_ * 128:(h_ + 1) * 128], axis=AX.X))
    diag_vv = p.cnt["VV"]

    sc_ld = {"n": 0}
    sc_ss = {"n": p.cnt["SS"]}
    sc_vv = {"n": p.cnt["VV"]}

    def load_subchunk(j, u, with_rk):
        tt = 2 * j + u // 2
        b = (u % 2) * 64
        p.wait("sync", "VV", p.cnt["VV"])
        p.wait("sync", "SS", p.cnt["SS"])
        ll = p.dma("sync", sb["scW"][0:64, :], sb["wexpc"][:, (u // 2) * 512:(u // 2 + 1) * 512][b:b + 64, :], "LL")
        ll = p.dma("sync", sb["scV"][0:64, :], p.dram["vdram"].ap()[tt * 128 + b: tt * 128 + b + 64, :], "LL")
        ll = p.dma("sync", sb["scK"][0:64, :], p.dram["kdram"].ap()[tt * 128 + b: tt * 128 + b + 64, :], "LL")
        if with_rk:
            ll = p.dma("sync", sb["scR"][0:64, :], p.dram["rdram"].ap()[tt * 128 + b: tt * 128 + b + 64, :], "LL")
            ll = p.dma("sync", sb["scDg"][0:64, :], sb["diag_l"][:, tt * 4:(tt + 1) * 4][b:b + 64, :], "LL")
        return ll

    def subchunk_m(ll, parity):
        p.wait("tensor", "LL", ll)
        p.wait("tensor", "SS", sc_ss["n"])
        p.wait("tensor", "VV", sc_vv["n"])
        t1 = p.inc("tensor", lambda t: t.matmul(pA[0:64, 0:512], p.sb["tri64_s"][0:64, :], p.sb["scW"][0:64, :], start=True, stop=True), "TT")
        p.wait("scalar", "TT", t1)
        e1 = p.inc("scalar", lambda s: s.activation(p.sb["mt_sb"][0:64, :], pA[0:64, 0:512], AF.Copy), "SS")
        p.wait("tensor", "SS", e1)
        t2 = p.inc("tensor", lambda t: t.matmul(pB[0:1, 0:512], p.sb["sel63_s"][0:64, :], p.sb["mt_sb"][0:64, :], start=True, stop=True), "TT")
        p.wait("scalar", "TT", t2)
        e2 = p.inc("scalar", lambda s: s.activation(p.sb["mrow"][0:1, 0:512], pB[0:1, 0:512], AF.Copy), "SS")
        e2 = p.inc("scalar", lambda s: s.activation(p.sb["mrow"][0:1, 512:1024], pB[0:1, 0:512], AF.Copy, scale=np.float32(0.5)), "SS")
        p.wait("tensor", "SS", e2)
        t3 = p.inc("tensor", lambda t: t.matmul(p.ps["pc11"][0:64, 0:512], p.sb["onesrow64_s"][:], p.sb["mrow"][0:1, 0:512], start=True, stop=True), "TT")
        t3 = p.inc("tensor", lambda t: t.matmul(p.ps["pc10"][0:64, 0:512], p.sb["onesrow64_s"][:], p.sb["mrow"][0:1, 512:1024], start=True, stop=True), "TT")
        return e2, t3

    def subchunk_factors(bc_tt, need_ar):
        p.wait("vector", "TT", bc_tt)
        vfast(lambda v: v.tensor_tensor(p.sb["facf"][0:64, 0:512], p.sb["mt_sb"][0:64, :], p.sb["scW"][0:64, :], op=ALU.subtract))
        if need_ar:
            vfast(lambda v: v.tensor_tensor(p.sb["facf"][0:64, 512:1024], p.sb["facf"][0:64, 0:512], p.ps["pc10"][0:64, 0:512], op=ALU.subtract))
            svc = vfast(lambda v, s=p.const_col(-40.0, 64): v.tensor_scalar_max(p.sb["facf"][0:64, 512:1024], p.sb["facf"][0:64, 512:1024], s))
            p.wait("scalar", "VV", svc)
            sA = p.inc("scalar", lambda s: s.activation(p.sb["fac"][0:64, 0:512], p.sb["facf"][0:64, 512:1024], AF.Exp, scale=np.float32(-1.0), bias=np.float32(-20.0)), "SS")
            p.wait("vector", "SS", sA)
            vfast(lambda v: v.tensor_tensor(p.sb["facf"][0:64, 512:1024], p.sb["mt_sb"][0:64, :], p.ps["pc10"][0:64, 0:512], op=ALU.subtract))
            svc = vfast(lambda v, s=p.const_col(40.0, 64): v.tensor_scalar_min(p.sb["facf"][0:64, 512:1024], p.sb["facf"][0:64, 512:1024], s))
            p.wait("scalar", "VV", svc)
            sA = p.inc("scalar", lambda s: s.activation(p.sb["fac"][0:64, 512:1024], p.sb["facf"][0:64, 512:1024], AF.Exp, scale=np.float32(1.0), bias=np.float32(-20.0)), "SS")
            p.wait("vector", "SS", sA)
        vfast(lambda v: v.tensor_tensor(p.sb["facf"][0:64, 512:1024], p.ps["pc11"][0:64, 0:512], p.sb["mt_sb"][0:64, :], op=ALU.subtract))
        svc = p.cnt["VV"]
        p.wait("scalar", "VV", svc)
        sA = p.inc("scalar", lambda s: s.activation(p.sb["fac"][0:64, 1024:1536], p.sb["facf"][0:64, 512:1024], AF.Exp, scale=np.float32(-1.0)), "SS")
        p.wait("vector", "SS", sA)
        if need_ar:
            vfast(lambda v: v.tensor_tensor(p.sb["sc1"][0:64, 0:512], p.sb["scR"][0:64, :], p.sb["fac"][0:64, 0:512], op=ALU.mult))
            vfast(lambda v: v.tensor_tensor(p.sb["sc1"][0:64, 512:1024], p.sb["scK"][0:64, :], p.sb["fac"][0:64, 512:1024], op=ALU.mult))
        vfast(lambda v: v.tensor_tensor(p.sb["sc1"][0:64, 1024:1536], p.sb["scK"][0:64, :], p.sb["fac"][0:64, 1024:1536], op=ALU.mult))
        return p.cnt["VV"], sA

    def subchunk_dcols(parity, with_emid):
        tdc = 0
        p.wait("tensor", "SS", sc_ss["n"])
        for h in range(4):
            tdc = p.inc("tensor", lambda t, h_=h: t.transpose(pB[:, 8 + h_:9 + h_],
                        p.sb["mrow"][0:1, h_ * 128:(h_ + 1) * 128], p.sb["identf1_s"][0:1, 0:1]), "TT")
        p.wait("scalar", "TT", tdc)
        edc = 0
        for h in range(4):
            edc = p.inc("scalar", lambda s, h_=h: s.activation(p.sb["dcol"][:, h_:h_ + 1], pB[:, 8 + h_:9 + h_], AF.Exp, scale=np.float32(-1.0)), "SS")
            if with_emid:
                edc = p.inc("scalar", lambda s, h_=h: s.activation(p.sb["dcol"][:, 4 + h_:5 + h_], pB[:, 8 + h_:9 + h_], AF.Exp, scale=np.float32(-0.5), bias=np.float32(20.0)), "SS")
        return edc

    def subchunk_bsub_supdate(prod_vv, edc):
        p.wait("vector", "SS", edc)
        p.wait("tensor", "SS", p.cnt["SS"])
        for h in range(4):
            p.wait("tensor", "VV", p.cnt["VV"])
            tb2 = p.inc("tensor", lambda t, h_=h: t.matmul(p.ps["pc00"][:, 0:256], p.sb["sc1"][0:64, 1024 + h_ * 128:1024 + (h_ + 1) * 128],
                        p.sb["scV"][0:64, h_ * 256:(h_ + 1) * 256], start=True, stop=True), "TT")
            vstep(lambda v, h_=h: v.tensor_scalar_mul(p.sb["S0"][:, h_ * 256:(h_ + 1) * 256],
                  p.sb["S0"][:, h_ * 256:(h_ + 1) * 256], p.sb["dcol"][:, h_:h_ + 1]))
            p.wait("vector", "TT", tb2)
            vstep(lambda v, h_=h: v.tensor_tensor(p.sb["S0"][:, h_ * 256:(h_ + 1) * 256],
                  p.sb["S0"][:, h_ * 256:(h_ + 1) * 256], p.ps["pc00"][:, 0:256], op=ALU.add))
        sc_vv["n"] = p.cnt["VV"]
        sc_ss["n"] = p.cnt["SS"]

    one_col = p.const_col(1.0)
    stA = 0
    for j in range(8):
        p.wait("sync", "TT", p.cnt["TT"])
        lw = p.dma("sync", sb["wexpc"][:, 0:512], p.dram["wexp_d"].ap()[(2 * j) * 128:(2 * j + 1) * 128, :], "LW")
        lw = p.dma("sync", sb["wexpc"][:, 512:1024], p.dram["wexp_d"].ap()[(2 * j + 1) * 128:(2 * j + 2) * 128, :], "LW")
        p.wait("sync", "LW", lw)
        p.wait("tensor", "LW", lw)
        p.wait("tensor", "SS", sc_ss["n"])
        tA = 0
        for tt2 in range(2):
            tA = p.inc("tensor", lambda t, t2=tt2: t.matmul(pB[0:1, 0:512], one_col, p.sb["wexpc"][:, t2 * 512:(t2 + 1) * 512],
                       start=(t2 == 0), stop=(t2 == 1)), "TT")
        p.wait("scalar", "TT", tA)
        p.wait("scalar", "ST", stA)
        eA = p.inc("scalar", lambda s: s.activation(p.sb["arow_st"][:], pB[0:1, 0:512], AF.Exp, scale=np.float32(-1.0)), "SS")
        p.wait("gpsimd", "SS", eA)
        stA = p.dma("gpsimd", p.dram["agA_in"].ap()[j:j + 1, :], sb["arow_st"][:], "ST")
        sc_ss["n"] = p.cnt["SS"]
        zz = vstep(lambda v: v.memset(p.sb["S0"][:], 0.0))
        sc_vv["n"] = zz
        for u in range(4):
            ll = load_subchunk(j, u, with_rk=False)
            mt_ss, bc_tt = subchunk_m(ll, u % 2)
            prod_vv, sA = subchunk_factors(bc_tt, need_ar=False)
            edc = subchunk_dcols(u % 2, with_emid=False)
            p.wait("tensor", "VV", prod_vv)
            subchunk_bsub_supdate(prod_vv, edc)
            p.wait("sync", "TT", p.cnt["TT"])
        bvv = p.cnt["VV"]
        p.wait("scalar", "VV", bvv)
        eB = 0
        for h in range(4):
            eB = p.inc("scalar", lambda s, h_=h: s.activation(p.sb["bchunk"][:, h_ * 256:(h_ + 1) * 256],
                       p.sb["S0"][:, h_ * 256:(h_ + 1) * 256], AF.Copy), "SS")
        p.wait("gpsimd", "SS", eB)
        stA = p.dma("gpsimd", p.dram["agB_in"].ap()[j * 512:(j + 1) * 512, :].rearrange("(h p) v -> p h v", p=128),
                    sb["bchunk"][:].rearrange("p (h v) -> p h v", h=4), "ST")
        sc_ss["n"] = p.cnt["SS"]

    p.wait("gpsimd", "ST", p.cnt["ST"])
    cc = p.collective("AllGather", ALU.bypass, [p.dram["agA_in"].ap().opt()], [p.dram["agA_out"].ap().opt()])
    cc = p.collective("AllGather", ALU.bypass, [p.dram["agB_in"].ap().opt()], [p.dram["agB_out"].ap().opt()])
    p.wait("sync", "CC", cc)
    p.wait("gpsimd", "CC", cc)

    la = p.dma("sync", sb["sq"][0:64, 0:512], p.dram["agA_out"].ap(), "LK")
    p.wait("tensor", "LK", la)
    p.wait("tensor", "SS", p.cnt["SS"])
    ta2 = 0
    for h in range(4):
        ta2 = p.inc("tensor", lambda t, h_=h: t.transpose(pB[:, h_ * 64:(h_ + 1) * 64],
                    p.sb["sq"][0:64, h_ * 128:(h_ + 1) * 128], p.sb["identf_l"][0:64, 0:64]), "TT")
    p.wait("scalar", "TT", ta2)
    ea2 = p.inc("scalar", lambda s: s.activation(p.sb["acol"][:], pB[:, 0:256], AF.Copy), "SS")
    p.wait("tensor", "SS", ea2)
    p.wait("vector", "SS", ea2)

    zz = vstep(lambda v: v.memset(p.sb["S0"][:], 0.0))
    bld_tt = [0, 0]
    stS = 0
    for m in range(64):
        r_ = 8 * (m % 8) + m // 8
        buf = m % 2
        p.wait("sync", "VV", bld_tt[buf])
        lb = p.dma("sync", sb["bload"][:, buf * 1024:(buf + 1) * 1024].rearrange("p (h v) -> p h v", h=4),
                   p.dram["agB_out"].ap()[r_ * 512:(r_ + 1) * 512, :].rearrange("(h p) v -> p h v", p=128), "LK")
        p.wait("scalar", "VV", p.cnt["VV"])
        p.wait("scalar", "ST", stS)
        eS = p.inc("scalar", lambda s: s.activation(p.sb["bchunk"][:], p.sb["S0"][:], AF.Copy), "SS")
        p.wait("gpsimd", "SS", eS)
        stS = p.dma("gpsimd", p.dram["sin_d"].ap()[m * 512:(m + 1) * 512, :].rearrange("(h p) v -> p h v", p=128),
                    sb["bchunk"][:].rearrange("p (h v) -> p h v", h=4), "ST")
        p.wait("vector", "SS", eS)
        p.wait("vector", "LK", lb)
        for h in range(4):
            vstep(lambda v, h_=h, r2=r_: v.tensor_scalar_mul(p.sb["S0"][:, h_ * 256:(h_ + 1) * 256],
                  p.sb["S0"][:, h_ * 256:(h_ + 1) * 256], p.sb["acol"][:, h_ * 64 + r2:h_ * 64 + r2 + 1]))
            vstep(lambda v, h_=h, b_=buf: v.tensor_tensor(p.sb["S0"][:, h_ * 256:(h_ + 1) * 256],
                  p.sb["S0"][:, h_ * 256:(h_ + 1) * 256], p.sb["bload"][:, b_ * 1024 + h_ * 256: b_ * 1024 + (h_ + 1) * 256], op=ALU.add))
        bld_tt[buf] = p.cnt["VV"]

    sld_vv = [0, 0]
    for j in range(8):
        zz = vstep(lambda v: v.memset(p.sb["S0"][:], 0.0))
        for c2 in range(8):
            sbuf_i = c2 % 2
            p.wait("sync", "VV", sld_vv[sbuf_i])
            ls = p.dma("sync", sb["sload"][:, sbuf_i * 1024:(sbuf_i + 1) * 1024].rearrange("p (h v) -> p h v", h=4),
                       p.dram["sin_d"].ap()[(8 * j + c2) * 512:(8 * j + c2 + 1) * 512, :].rearrange("(h p) v -> p h v", p=128), "LK")
            p.wait("vector", "LK", ls)
            sld_vv[sbuf_i] = vstep(lambda v, b_=sbuf_i, c3=c2: v.scalar_tensor_tensor(p.sb["S0"][:],
                                   p.sb["sload"][:, b_ * 1024:(b_ + 1) * 1024], p.sb["indc_s"][:, c3:c3 + 1],
                                   p.sb["S0"][:], op0=ALU.mult, op1=ALU.add))
        sc_vv["n"] = p.cnt["VV"]
        p.wait("sync", "TT", p.cnt["TT"])
        lw = p.dma("sync", sb["wexpc"][:, 0:512], p.dram["wexp_d"].ap()[(2 * j) * 128:(2 * j + 1) * 128, :], "LW")
        lw = p.dma("sync", sb["wexpc"][:, 512:1024], p.dram["wexp_d"].ap()[(2 * j + 1) * 128:(2 * j + 2) * 128, :], "LW")
        for u in range(4):
            tt = 2 * j + u // 2
            b = (u % 2) * 64
            ll = load_subchunk(j, u, with_rk=True)
            mt_ss, bc_tt = subchunk_m(ll, u % 2)
            prod_vv, sA = subchunk_factors(bc_tt, need_ar=True)
            edc = subchunk_dcols(u % 2, with_emid=True)
            ptb = ptrs[u % 2]
            p.wait("tensor", "VV", prod_vv)
            p.wait("tensor", "SS", sc_ss["n"])
            ttr = 0
            for h in range(4):
                ttr = p.inc("tensor", lambda t, h_=h, o=ptb: t.transpose(o[:, h_ * 64:(h_ + 1) * 64],
                            p.sb["sc1"][0:64, h_ * 128:(h_ + 1) * 128], p.sb["ident64_s"][0:64, :]), "TT")
            for h in range(4):
                ttr = p.inc("tensor", lambda t, h_=h, o=ptb: t.transpose(o[:, 256 + h_ * 64:256 + (h_ + 1) * 64],
                            p.sb["sc1"][0:64, 512 + h_ * 128:512 + (h_ + 1) * 128], p.sb["ident64_s"][0:64, :]), "TT")
            p.wait("scalar", "TT", ttr)
            etr = p.inc("scalar", lambda s, o=ptb: s.activation(p.sb["rt_sc"][:, 0:512], o[:, 0:512], AF.Copy), "SS")
            sc_ss["n"] = etr
            p.wait("vector", "SS", edc)
            for h in range(4):
                vstep(lambda v, h_=h: v.tensor_scalar_mul(p.sb["Stmp"][:, h_ * 256:(h_ + 1) * 256],
                      p.sb["S0"][:, h_ * 256:(h_ + 1) * 256], p.sb["dcol"][:, 4 + h_:5 + h_]))
            stmp_vv = p.cnt["VV"]
            p.wait("tensor", "SS", etr)
            p.wait("tensor", "VV", stmp_vv)
            tat = 0
            for h in range(4):
                tat = p.inc("tensor", lambda t, h_=h: t.matmul(pA[0:64, h_ * 64:(h_ + 1) * 64],
                            p.sb["rt_sc"][:, 256 + h_ * 64:256 + (h_ + 1) * 64],
                            p.sb["rt_sc"][:, h_ * 64:(h_ + 1) * 64], start=True, stop=True), "TT")
            p.wait("vector", "TT", tat)
            for h in range(4):
                vstep(lambda v, h_=h: v.tensor_tensor(p.sb["at_sb"][0:64, h_ * 64:(h_ + 1) * 64],
                      pA[0:64, h_ * 64:(h_ + 1) * 64], p.sb["mask64_s"][0:64, :], op=ALU.mult))
                vstep(lambda v, h_=h: v.tensor_scalar_mul(p.sb["scDm"][0:64, :], p.sb["ident64_s"][0:64, :],
                      p.sb["scDg"][0:64, h_:h_ + 1]))
                vstep(lambda v, h_=h: v.tensor_tensor(p.sb["at_sb"][0:64, h_ * 64:(h_ + 1) * 64],
                      p.sb["at_sb"][0:64, h_ * 64:(h_ + 1) * 64], p.sb["scDm"][0:64, :], op=ALU.add))
            at_vv = p.cnt["VV"]
            p.wait("tensor", "VV", at_vv)
            tO = 0
            for h in range(4):
                po = p.ps["pc00"] if h < 2 else p.ps["pc01"]
                off = (h % 2) * 256
                p.emit("tensor", lambda t, h_=h, o=po[0:64, off:off + 256]: t.matmul(
                    o, p.sb["at_sb"][0:64, h_ * 64:(h_ + 1) * 64], p.sb["scV"][0:64, h_ * 256:(h_ + 1) * 256],
                    start=True, stop=False))
                tO = p.inc("tensor", lambda t, h_=h, o=po[0:64, off:off + 256]: t.matmul(
                    o, p.sb["rt_sc"][:, h_ * 64:(h_ + 1) * 64], p.sb["Stmp"][:, h_ * 256:(h_ + 1) * 256],
                    start=False, stop=True), "TT")
            p.wait("scalar", "TT", tO)
            eO = p.inc("scalar", lambda s: s.activation(p.sb["ofin"][0:64, 0:512], p.ps["pc00"][0:64, 0:512], AF.Copy), "SS")
            eO = p.inc("scalar", lambda s: s.activation(p.sb["ofin"][0:64, 512:1024], p.ps["pc01"][0:64, 0:512], AF.Copy), "SS")
            p.wait("gpsimd", "SS", eO)
            stO = p.dma("gpsimd", sb["xlt"][:, tt * 1024:(tt + 1) * 1024][b:b + 64, :], sb["ofin"][0:64, :], "ST")
            sc_ss["n"] = p.cnt["SS"]
            subchunk_bsub_supdate(prod_vv, edc)
            p.wait("sync", "TT", p.cnt["TT"])
            p.wait("sync", "ST", stO)
            if L1_DEBUG_STOP == 1 and j == 0 and u == 0:
                p.wait("gpsimd", "ST", p.cnt["ST"])
                return p.cnt["VV"]
    o_done_st = p.cnt["ST"]
    if L1_DEBUG_STOP == 2:
        p.wait("gpsimd", "ST", p.cnt["ST"])
        return p.cnt["VV"]

    p.wait("vector", "TT", p.cnt["TT"])
    wvo = emit_l1_weight(p, "owt_l", 1024, sb["wbuf_l"])
    p.wait("tensor", "VV", wvo)
    st_fin = 0
    gb_ld = 0
    gb_guard = 0
    for t in range(NTILE):
        p.wait("sync", "VV", p.cnt["VV"])
        p.wait("sync", "SS", p.cnt["SS"])
        lg = p.dma("sync", sb["sgb"][:], p.dram["gdram"].ap()[t * 128:(t + 1) * 128, :], "LK")
        p.wait("scalar", "LK", lg)
        p.wait("scalar", "ST", o_done_st)
        sg_ss = p.inc("scalar", lambda s: s.activation(p.sb["sgb"][:], p.sb["sgb"][:], AF.Silu), "SS")
        gb_ld = sg_ss
        osl = sb["xlt"][:, t * 1024:(t + 1) * 1024]
        p.wait("vector", "ST", o_done_st)
        for h in range(4):
            p.wait("vector", "SS", p.cnt["SS"])
            oh = osl[:, h * 256:(h + 1) * 256]
            negm = sb["stat"][:, 0:1]
            sd = sb["stat"][:, 1:2]
            rstd = sb["stat"][:, 2:3]
            vstep(lambda v, a=negm, b=oh: v.reduce_sum(a, b, axis=AX.X, negate=True))
            nv = vstep(lambda v, a=negm, s=p.const_col(1.0 / 256.0): v.tensor_scalar_mul(a, a, s))
            p.wait("scalar", "VV", nv)
            sqs = p.inc("scalar", lambda s, a=sb["mub"][:, 0:256], b=oh, m=negm:
                        s.activation(a, b, AF.Square, bias=m, scale=np.float32(1.0)), "SS")
            p.wait("vector", "SS", sqs)
            vstep(lambda v, a=sd: v.reduce_sum(a, p.sb["mub"][:, 0:256], axis=AX.X))
            nv = vstep(lambda v, a=sd, s=p.const_col(1.0 / 256.0): v.tensor_scalar_mul(a, a, s))
            p.wait("scalar", "VV", nv)
            sqs = p.inc("scalar", lambda s, a=sd: s.activation(a, a, AF.Sqrt, bias=np.float32(1e-5), scale=np.float32(1.0)), "SS")
            p.wait("vector", "SS", sqs)
            vstep(lambda v, a=rstd, b=sd: v.reciprocal(a, b))
            vstep(lambda v, a=sb["ofin"][:, h * 256:(h + 1) * 256], b=oh, m=negm, r=rstd:
                  v.tensor_scalar(a, b, m, r, op0=ALU.add, op1=ALU.mult))
        vfast(lambda v: v.tensor_tensor(p.sb["ofin"][:], p.sb["ofin"][:], p.sb["gnw_b"][:], op=ALU.mult))
        vfast(lambda v: v.tensor_tensor(p.sb["ofin"][:], p.sb["ofin"][:], p.sb["gnb_b"][:], op=ALU.add))
        p.wait("vector", "SS", sg_ss)
        fvv = vstep(lambda v: v.tensor_tensor(p.sb["ofin"][:], p.sb["ofin"][:], p.sb["sgb"][:], op=ALU.mult))
        gb_guard = fvv
        ec2 = emit_transpose_tile(p, sb["ofin"][:], sb["oft"][:].rearrange("p (f c) -> p f c", f=8),
                                  "VV", fvv)
        p.wait("tensor", "SS", ec2)
        for ch in range(2):
            bk = ch % 2
            p.wait("tensor", "VV", p.cnt["VV"])
            tcn = 0
            for f in range(8):
                fn = (lambda t, o=pq[bk][:, 0:512], l=sb["oft"][:, f * 128:(f + 1) * 128],
                      r=sb["wbuf_l"][:, f * 1024 + ch * 512: f * 1024 + (ch + 1) * 512],
                      s0=(f == 0), s1=(f == 7): t.matmul(o, l, r, start=s0, stop=s1))
                if f == 7:
                    tcn = p.inc("tensor", fn, "TT")
                else:
                    p.emit("tensor", fn)
            p.wait("vector", "TT", tcn)
            if L1_DEBUG_STOP == 3:
                last = vstep(lambda v, a=sb["mub"][:], b=pq[bk][:, 0:512]: v.tensor_copy(a, b))
            else:
                last = vstep(lambda v, a=sb["xres"][:, t * D + ch * 512: t * D + (ch + 1) * 512], b=pq[bk][:, 0:512]:
                             v.tensor_tensor(a, a, b, op=ALU.add))
        if L1_DEBUG_STOP == 4 and t == 0:
            dxr = p.dram["dxr"].ap()
            p.wait("gpsimd", "VV", last)
            std = p.dma("gpsimd", dxr[0:128, :], sb["xres"][:, 0:1024], "ST")
            std = p.dma("gpsimd", dxr[128:256, :], sb["ofin"][:], "ST")
            std = p.dma("gpsimd", dxr[256:384, :], sb["sgb"][:], "ST")
            p.wait("sync", "ST", std)
    return last



def _barrier(p):
    for eng in ENGINES:
        for s2 in ("VV", "SS", "TT", "LD", "LK", "LV", "ST", "CC"):
            if s2 in p.cnt and p.cnt[s2] > 0:
                p.wait(eng, s2, p.cnt[s2])


def build_fused(nphases=6):
    p = Prog("mkfused")
    p.din("xin", [SHARD, D], F32)
    p.dout("xout", [SHARD, D], F32)
    p.din("identb", [128, 128], BF16)
    p.din("ones1", [1, 128], BF16)
    p.din("onesrow", [1, 128], F32)
    p.sbuf("xres", [128, NTILE * D], F32)
    p.sbuf("stat", [128, 16], F32)
    p.sbuf("xlt", [128, 16 * D], BF16)
    p.sbuf("identb_s", [128, 128], BF16)
    p.sbuf("ones1b", [1, 128], BF16)
    p.sbuf("onesrow_l", [1, 128], F32)
    p.psum("ptr", [128, 1024], BF16)
    p.psum("ptr2", [128, 1024], BF16)
    p.psum("pA", [128, 512], F32)
    p.psum("pB", [128, 512], F32)
    p.psum("pc00", [128, 512], F32)
    p.psum("pc01", [128, 512], F32)
    p.psum("pc10", [128, 512], F32)
    p.psum("pc11", [128, 512], F32)
    p.ps["pwa"] = p.ps["pc00"]
    p.ps["pwb"] = p.ps["pc01"]
    p.dma("sync", p.sb["cpool_s"][:], p.dram["cpool"].ap(), "LD")
    p.dma("sync", p.sb["identb_s"][:], p.dram["identb"].ap(), "LD")
    p.dma("sync", p.sb["ones1b"][:], p.dram["ones1"].ap(), "LD")
    p.dma("sync", p.sb["onesrow_l"][:], p.dram["onesrow"].ap(), "LD")
    ld_x = emit_load_xres(p, p.dram["xin"].ap())
    def scope4(p):
        p.sbuf("xlnb", [128, 4 * D], BF16)
        p.sbuf("sq", [128, 4 * D], F32)
    last = 0
    p.push_scope(); scope4(p); decl_l0ffn(p)
    sc_w = emit_build_w(p)
    last = emit_l0ffn(p, sc_w, ld_x)
    _barrier(p); p.pop_scope()
    if nphases >= 2:
        p.push_scope(); scope4(p); decl_l0attn(p)
        last = emit_l0attn(p, ld_x)
        _barrier(p); p.pop_scope()
    if nphases >= 3:
        p.push_scope(); scope4(p); decl_l0ffn(p)
        sc_w = emit_build_w(p)
        last = emit_l0ffn(p, sc_w, ld_x)
        _barrier(p); p.pop_scope()
    if nphases >= 4:
        p.push_scope(); scope4(p); decl_moe(p)
        last = emit_moe(p, ld_x)
        _barrier(p); p.pop_scope()
    if nphases >= 5:
        p.push_scope()
        p.sbuf("xlnb", [128, 2 * D], BF16)
        p.sbuf("mrow", [1, 1024], F32)
        p.sbuf("sq", [128, 2 * D], F32)
        decl_l1attn(p)
        last = emit_l1attn(p, ld_x)
        _barrier(p); p.pop_scope()
    if nphases >= 6:
        p.push_scope(); scope4(p); decl_moe(p)
        last = emit_moe(p, ld_x)
        _barrier(p); p.pop_scope()
    p.wait("gpsimd", "VV", last)
    st = emit_store_xres(p, p.dram["xout"].ap())
    p.wait("gpsimd", "ST", st)
    return p



_CACHE = {}


def _program():
    if "prog" not in _CACHE:
        pf = build_fused()
        _CACHE["prog"] = (pf, pf.finalize())
    return _CACHE["prog"]


def kernel(**inputs):
    from concourse.bass_utils import run_bass_kernel_spmd
    g = lambda k2: np.asarray(inputs[k2])
    pf, nc = _program()

    host = {}
    host.update(l0ffn_host_inputs(g("L0_ffn_G1_R"), g("L0_ffn_G2_R"), g("L0_ffn_G1_I"), g("L0_ffn_G2_I"),
                                  g("L0_ffn_bias_real"), g("L0_ffn_bias_imag")))
    host.update(l1attn_host_inputs({("in_L1_attn_" + k2): v for k2, v in
                                    [(k3[len("L1_attn_"):], g(k3)) for k3 in inputs if k3.startswith("L1_attn_")]}))
    host["identb"] = np.eye(128, dtype=ml_dtypes.bfloat16)
    host["ones1"] = np.ones((1, 128), dtype=ml_dtypes.bfloat16)
    host["onesrow"] = np.ones((1, 128), dtype=np.float32)
    host["identf1"] = np.ones((1, 1), dtype=np.float32)
    host["identf64"] = np.eye(64, dtype=np.float32)
    host["cpool"] = pf.cpool_host()

    x = np.ascontiguousarray(g("L0_attn_X")[0], dtype=np.float32)
    in_maps = []
    for c in range(N_CORES):
        m = dict(host)
        m.update(l0attn_host_inputs(g("L0_attn_W_q"), g("L0_attn_W_k"), g("L0_attn_W_v"), c))
        m.update(moe_host_inputs(g("L1_ffn_router_w"), g("L1_ffn_w1"), g("L1_ffn_w2"), c))
        m["identb"] = host["identb"]
        m["onesrow"] = host["onesrow"]
        ind = np.zeros((128, 8), dtype=np.float32)
        ind[:, c] = 1.0
        m["indc"] = ind
        i8 = np.zeros((8, 1), dtype=ml_dtypes.bfloat16)
        i8[c, 0] = 1.0
        m["indc8"] = i8
        m["xin"] = shard_rows(x, c)
        in_maps.append(m)
    res = run_bass_kernel_spmd(nc, in_maps, core_ids=list(range(N_CORES)))
    out = unshard_rows([np.asarray(res.results[c]["xout"]) for c in range(N_CORES)])
    return np.ascontiguousarray(out, dtype=np.float32).reshape(1, 16384, 1024)


# version 07 (final)
# speedup vs baseline: 1.0915x; 1.0915x over previous
import contextlib
import numpy as np
import ml_dtypes
import concourse.bass as bass
import concourse.mybir as mybir

F32 = mybir.dt.float32
BF16 = mybir.dt.bfloat16
AF = mybir.ActivationFunctionType
ALU = mybir.AluOpType
AX = mybir.AxisListType

ENGINES = ("tensor", "vector", "scalar", "gpsimd", "sync")
FFN_BIAS = True
FFN_STT = True

T_FULL = 16384
D = 1024
N_CORES = 8
NBLK = 16
BLK = 1024
SHARD = 2048
NTILE = 16


def core_blocks(c):
    return (c, NBLK - 1 - c)


CHUNK = 256


class Prog:
    def __init__(self, name="prog"):
        self.name = name
        self.nc = bass.Bass("TRN2", target_bir_lowering=False)
        self.es = contextlib.ExitStack()
        self.ops = {e: [] for e in ENGINES}
        self.sems = {}
        self.cnt = {}
        self.sb = {}
        self.ps = {}
        self.dram = {}
        self.n_emitted = 0
        self._consts = {}
        self._scopes = []
        self._dead = set()
        self.din("cpool", [128, 32], F32)
        self.sbuf("cpool_s", [128, 32], F32)

    def push_scope(self):
        self._scopes.append((contextlib.ExitStack(), []))

    def pop_scope(self):
        es, names = self._scopes.pop()
        es.close()
        for n in names:
            self._dead.add(n)

    def sem(self, name):
        if name not in self.sems:
            self.sems[name] = self.es.enter_context(self.nc.semaphore(name))
            self.cnt[name] = 0
        return self.sems[name]

    def sbuf(self, name, shape, dtype=F32, side=None):
        if name in self.sb:
            assert name in self._dead, f"sbuf {name} still live"
            self._dead.discard(name)
        kw = {"side": side} if side else {}
        if self._scopes:
            es, names = self._scopes[-1]
            self.sb[name] = es.enter_context(self.nc.sbuf_tensor(name, list(shape), dtype, allow_name_mangling=True, **kw))
            names.append(name)
        else:
            self.sb[name] = self.es.enter_context(self.nc.sbuf_tensor(name, list(shape), dtype, **kw))
        return self.sb[name]

    def psum(self, name, shape, dtype=F32):
        assert name not in self.ps
        self.ps[name] = self.es.enter_context(self.nc.psum_tensor(name, list(shape), dtype))
        return self.ps[name]

    def din(self, name, shape, dtype=F32):
        if name in self.dram:
            return self.dram[name]
        t = self.nc.dram_tensor(name, list(shape), dtype, kind="ExternalInput")
        self.dram[name] = t
        return t

    def dout(self, name, shape, dtype=F32):
        t = self.nc.dram_tensor(name, list(shape), dtype, kind="ExternalOutput")
        self.dram[name] = t
        return t

    def dint(self, name, shape, dtype=F32):
        if name in self.dram:
            return self.dram[name]
        t = self.nc.dram_tensor(name, list(shape), dtype)
        self.dram[name] = t
        return t

    def const_col(self, value, rows=128):
        v = float(value)
        if v not in self._consts:
            assert len(self._consts) < 32
            self._consts[v] = len(self._consts)
        j = self._consts[v]
        return self.sb["cpool_s"][0:rows, j:j + 1]

    def cpool_host(self):
        a = np.zeros((128, 32), dtype=np.float32)
        for v, j in self._consts.items():
            a[:, j] = v
        return a

    def emit(self, eng, fn):
        self.ops[eng].append(fn)
        self.n_emitted += 1

    def inc(self, eng, fn, sem_name, by=1):
        sem = self.sem(sem_name)
        if by == 1:
            self.ops[eng].append(lambda e, f=fn, s=sem: f(e).then_inc(s))
        else:
            self.ops[eng].append(lambda e, f=fn, s=sem, b=by: f(e).then_inc(s, b))
        self.cnt[sem_name] += by
        self.n_emitted += 1
        return self.cnt[sem_name]

    def bump(self, eng, sem_name, by=1):
        sem = self.sem(sem_name)
        self.ops[eng].append(lambda e, s=sem, b=by: e.nop().then_inc(s, b))
        self.cnt[sem_name] += by
        self.n_emitted += 1
        return self.cnt[sem_name]

    def wait(self, eng, sem_name, value):
        sem = self.sem(sem_name)
        self.ops[eng].append(lambda e, s=sem, v=value: e.wait_ge(s, v))
        self.n_emitted += 1

    def dma(self, eng, out, in_, sem_name):
        sem = self.sem(sem_name)
        self.ops[eng].append(lambda e, o=out, i=in_, s=sem: e.dma_start(out=o, in_=i).then_inc(s, 16))
        self.cnt[sem_name] += 16
        self.n_emitted += 1
        return self.cnt[sem_name]

    def collective(self, kind, op, ins, outs, sem_name="CC"):
        sem = self.sem(sem_name)
        rg = [list(range(N_CORES))]
        self.ops["gpsimd"].append(
            lambda e, k=kind, o=op, i=ins, u=outs, s=sem, r=rg:
                e.collective_compute(k, o, replica_groups=r, ins=i, outs=u).then_inc(s))
        self.cnt[sem_name] += 1
        self.n_emitted += 1
        return self.cnt[sem_name]

    def finalize(self):
        nc = self.nc
        with self.es:
            block = self.es.enter_context(nc.Block())
            engs = {
                "tensor": block.tensor, "vector": block.vector,
                "scalar": block.scalar, "gpsimd": block.gpsimd, "sync": block.sync,
            }
            for name in ENGINES:
                if not self.ops[name]:
                    continue
                dec = engs[name]

                def make(fns):
                    def body(e):
                        for f in fns:
                            f(e)
                    return body
                dec(make(self.ops[name]))
        return nc



def shard_rows(x_full, c):
    return np.concatenate([x_full[(8 * j + c) * CHUNK:(8 * j + c + 1) * CHUNK] for j in range(8)], axis=0)


def unshard_rows(shards):
    T = N_CORES * SHARD
    out = np.empty((T, shards[0].shape[1]), dtype=shards[0].dtype)
    for c in range(N_CORES):
        for j in range(8):
            out[(8 * j + c) * CHUNK:(8 * j + c + 1) * CHUNK] = shards[c][j * CHUNK:(j + 1) * CHUNK]
    return out


def emit_load_xres(p: Prog, xin_ap):
    xres = p.sb["xres"]
    src = xin_ap.rearrange("(t p) f -> t p f", p=128)
    ld = 0
    for i in range(NTILE):
        ld = p.dma("sync", xres[:, i * D:(i + 1) * D], src[i], "LD")
    return ld


def emit_store_xres(p: Prog, xout_ap):
    xres = p.sb["xres"]
    dst = xout_ap.rearrange("(t p) f -> t p f", p=128)
    st = 0
    for i in range(NTILE):
        st = p.dma("gpsimd", dst[i], xres[:, i * D:(i + 1) * D], "ST")
    return st


def emit_ln_tile(p: Prog, i, dst_name="xlnb", dst_col=0, group_slot=0):
    xres = p.sb["xres"]
    sq = p.sb["sq"]
    st = p.sb["stat"]
    xt = xres[:, i * D:(i + 1) * D]
    sqt = sq[:, group_slot * D:(group_slot + 1) * D]
    negm = st[:, group_slot * 4 + 0:group_slot * 4 + 1]
    sd = st[:, group_slot * 4 + 1:group_slot * 4 + 2]
    rstd = st[:, group_slot * 4 + 2:group_slot * 4 + 3]
    dst = p.sb[dst_name][:, dst_col:dst_col + D]

    p.emit("vector", lambda v, a=negm, b=xt: v.reduce_sum(a, b, axis=AX.X, negate=True))
    vc = p.inc("vector", lambda v, a=negm, s=p.const_col(1.0 / D): v.tensor_scalar_mul(a, a, s), "VV")
    p.wait("scalar", "VV", vc)
    sc = p.inc("scalar", lambda s, a=sqt, b=xt, m=negm: s.activation(a, b, AF.Square, bias=m, scale=np.float32(1.0)), "SS")
    p.wait("vector", "SS", sc)
    p.emit("vector", lambda v, a=sd, b=sqt: v.reduce_sum(a, b, axis=AX.X))
    vc = p.inc("vector", lambda v, a=sd, s=p.const_col(1.0 / D): v.tensor_scalar_mul(a, a, s), "VV")
    p.wait("scalar", "VV", vc)
    sc = p.inc("scalar", lambda s, a=sd: s.activation(a, a, AF.Sqrt, bias=np.float32(1e-5), scale=np.float32(1.0)), "SS")
    p.wait("vector", "SS", sc)
    rc = p.inc("vector", lambda v, a=rstd, b=sd: v.reciprocal(a, b), "VV")
    p.wait("vector", "VV", rc)
    vc = p.inc("vector", lambda v, a=dst, b=xt, m=negm, r=rstd:
               v.tensor_scalar(a, b, m, r, op0=ALU.add, op1=ALU.mult), "VV")
    return vc


def emit_transpose_tile(p: Prog, src_ap, dst_view8, ready_sem, ready_val, done_sem="TT",
                        ptr_name="ptr", evac_engine="scalar"):
    ptrs = [p.ps[ptr_name], p.ps[ptr_name + "2"]]
    identb = p.sb["identb_s"]
    eng = evac_engine
    sem = {"scalar": "SS", "vector": "VV"}[eng]
    p.wait("tensor", ready_sem, ready_val)
    p.wait("tensor", sem, p.cnt[sem])
    ec = 0
    for f in range(8):
        bank = ptrs[f % 2]
        if f >= 2:
            p.wait("tensor", sem, ec - 1)
        tc = p.inc("tensor", lambda t, o=bank[:, 0:128],
                   s=src_ap[:, f * 128:(f + 1) * 128], i=identb[:]:
                   t.transpose(o, s, i), done_sem)
        p.wait(eng, done_sem, tc)
        dst = dst_view8[:, f, :]
        if eng == "scalar":
            ec = p.inc(eng, lambda s, a=dst, b=bank[:, 0:128]:
                       s.activation(a, b, AF.Copy), sem)
        else:
            ec = p.inc(eng, lambda v, a=dst, b=bank[:, 0:128]:
                       v.tensor_copy(a, b), sem)
    return ec



def l0ffn_host_inputs(G1R, G2R, G1I, G2I, bias_r, bias_i):
    d = {}
    for nm, G1, G2 in (("r", G1R, G2R), ("i", G1I, G2I)):
        for c in range(4):
            d[f"g1{nm}{c}"] = np.ascontiguousarray(G1[0, :, c, :], dtype=np.float32)
            d[f"g2{nm}{c}"] = np.ascontiguousarray(G2[c, :, 0, :], dtype=np.float32)
    EB = np.zeros((32, 512), dtype=np.float32)
    for b in range(32):
        for e in range(16):
            EB[b, b * 16 + e] = 1.0
    EE = np.zeros((16, 512), dtype=np.float32)
    for b in range(32):
        for e in range(16):
            EE[e, b * 16 + e] = 1.0
    d["EB"], d["EE"] = EB, EE
    for q in range(4):
        DQ = np.zeros((32, 128), dtype=np.float32)
        for pp in range(128):
            DQ[8 * q + pp // 16, pp] = 1.0
        d[f"DQ{q}"] = DQ
    GQ = np.zeros((16, 128), dtype=np.float32)
    for pp in range(128):
        GQ[pp % 16, pp] = 1.0
    d["GQ"] = GQ
    d["brow"] = np.concatenate([np.asarray(bias_r), np.asarray(bias_i)]).astype(ml_dtypes.bfloat16).reshape(1, 1024)
    return d


def decl_l0ffn_w(p: Prog):
    p.sbuf("wrb", [128, 4 * 512], BF16, side="right")
    p.sbuf("wib", [128, 4 * 512], BF16, side="right")
    p.sbuf("nwib", [128, 4 * 512], BF16, side="right")


def decl_l0ffn(p: Prog):
    for nm in ("r", "i"):
        for c in range(4):
            p.din(f"g1{nm}{c}", [32, 32], F32)
            p.din(f"g2{nm}{c}", [16, 16], F32)
    p.din("EB", [32, 512], F32)
    p.din("EE", [16, 512], F32)
    for q in range(4):
        p.din(f"DQ{q}", [32, 128], F32)
    p.din("GQ", [16, 128], F32)
    p.din("brow", [1, 1024], BF16)

    p.sbuf("g1s", [32, 8 * 32], F32)
    p.sbuf("g2s", [16, 8 * 16], F32)
    p.sbuf("EBs", [32, 512], F32)
    p.sbuf("EEs", [16, 512], F32)
    p.sbuf("DQs", [32, 4 * 128], F32)
    p.sbuf("GQs", [16, 128], F32)
    p.sbuf("brows", [1, 1024], BF16)
    p.sbuf("t1s", [32, 512], F32)
    p.sbuf("t2s", [16, 512], F32)
    p.sbuf("aexp", [128, 512], F32)
    p.sbuf("tmpw", [128, 512], F32)
    p.sbuf("wr_acc", [128, 4 * 512], F32)
    p.sbuf("wi_acc", [128, 4 * 512], F32)


def emit_build_w(p: Prog):
    nc = p.nc
    sb = p.sb
    ld = 0
    for nm_i, nm in enumerate(("r", "i")):
        for c in range(4):
            ld = p.dma("sync", sb["g1s"][:, (nm_i * 4 + c) * 32:(nm_i * 4 + c + 1) * 32], p.dram[f"g1{nm}{c}"].ap(), "LD")
            ld = p.dma("sync", sb["g2s"][:, (nm_i * 4 + c) * 16:(nm_i * 4 + c + 1) * 16], p.dram[f"g2{nm}{c}"].ap(), "LD")
    ld = p.dma("sync", sb["EBs"][:], p.dram["EB"].ap(), "LD")
    ld = p.dma("sync", sb["EEs"][:], p.dram["EE"].ap(), "LD")
    for q in range(4):
        ld = p.dma("sync", sb["DQs"][:, q * 128:(q + 1) * 128], p.dram[f"DQ{q}"].ap(), "LD")
    ld = p.dma("sync", sb["GQs"][:], p.dram["GQ"].ap(), "LD")
    ld = p.dma("sync", sb["brows"][:], p.dram["brow"].ap(), "LD")

    pT1 = p.ps["pwa"]
    pT2 = p.ps["pwb"]
    p.wait("tensor", "LD", ld)
    for nm_i, nm in enumerate(("r", "i")):
        acc = sb["wr_acc"] if nm == "r" else sb["wi_acc"]
        for c in range(4):
            g1 = sb["g1s"][:, (nm_i * 4 + c) * 32:(nm_i * 4 + c + 1) * 32]
            g2 = sb["g2s"][:, (nm_i * 4 + c) * 16:(nm_i * 4 + c + 1) * 16]
            tc = p.inc("tensor", lambda t, o=pT1[0:32, :], l=g1, r=sb["EBs"][:]:
                       t.matmul(o, l, r, start=True, stop=True), "TT")
            tc = p.inc("tensor", lambda t, o=pT2[0:16, :], l=g2, r=sb["EEs"][:]:
                       t.matmul(o, l, r, start=True, stop=True), "TT")
            p.wait("vector", "TT", tc)
            vc = p.inc("vector", lambda v, a=sb["t1s"][:], b=pT1[0:32, :]: v.tensor_copy(a, b), "VV")
            vc = p.inc("vector", lambda v, a=sb["t2s"][:], b=pT2[0:16, :]: v.tensor_copy(a, b), "VV")
            p.wait("tensor", "VV", vc)
            for q in range(4):
                tc = p.inc("tensor", lambda t, o=pT1[:], l=sb["DQs"][:, q * 128:(q + 1) * 128], r=sb["t1s"][:]:
                           t.matmul(o, l, r, start=True, stop=True), "TT")
                tc = p.inc("tensor", lambda t, o=pT2[:], l=sb["GQs"][:], r=sb["t2s"][:]:
                           t.matmul(o, l, r, start=True, stop=True), "TT")
                p.wait("vector", "TT", tc)
                vc = p.inc("vector", lambda v, a=sb["aexp"][:], b=pT1[:]: v.tensor_copy(a, b), "VV")
                vc = p.inc("vector", lambda v, a=sb["tmpw"][:], b=sb["aexp"][:], c2=pT2[:]:
                           v.tensor_tensor(a, b, c2, op=ALU.mult), "VV")
                dst = acc[:, q * 512:(q + 1) * 512]
                if c == 0:
                    vc = p.inc("vector", lambda v, a=dst, b=sb["tmpw"][:]: v.tensor_copy(a, b), "VV")
                else:
                    vc = p.inc("vector", lambda v, a=dst, b=sb["tmpw"][:]:
                               v.tensor_tensor(a, a, b, op=ALU.add), "VV")
                p.wait("tensor", "VV", vc)
    p.wait("scalar", "VV", p.cnt["VV"])
    sc = p.inc("scalar", lambda s: s.activation(p.sb["wrb"][:], p.sb["wr_acc"][:], AF.Copy), "SS")
    sc = p.inc("scalar", lambda s: s.activation(p.sb["wib"][:], p.sb["wi_acc"][:], AF.Copy), "SS")
    sc = p.inc("scalar", lambda s: s.activation(p.sb["nwib"][:], p.sb["wi_acc"][:], AF.Copy, scale=np.float32(-1.0)), "SS")
    return sc


def emit_l0ffn(p: Prog, w_ready_sc, x_ready_ld, ntiles=NTILE):
    sb = p.sb
    po = [p.ps["pA"], p.ps["pB"]]
    xlt8 = sb["xlt"][:].rearrange("p (f t) -> p f t", f=8)

    p.wait("vector", "LD", x_ready_ld)
    p.wait("tensor", "SS", w_ready_sc)
    last_sq_sc = [0, 0, 0, 0]
    last_tr_tt = [0, 0, 0, 0]
    last_evac_vv = 0
    last_ptr_ec = 0
    for i in range(ntiles):
        slot = i % 4
        if last_sq_sc[slot]:
            p.wait("vector", "SS", last_sq_sc[slot])
        if last_tr_tt[slot]:
            p.wait("vector", "TT", last_tr_tt[slot])
        vc_ln = emit_ln_tile(p, i, "xlnb", slot * D, slot)
        last_sq_sc[slot] = p.cnt["SS"] - 1
        if last_ptr_ec:
            p.wait("tensor", "SS", last_ptr_ec)
        ec = emit_transpose_tile(p, sb["xlnb"][:, slot * D:(slot + 1) * D],
                                 xlt8[:, :, i * 128:(i + 1) * 128], "VV", vc_ln)
        last_tr_tt[slot] = p.cnt["TT"]
        last_ptr_ec = ec
        p.wait("tensor", "SS", ec)
        if last_evac_vv:
            p.wait("tensor", "VV", last_evac_vv)
        t_half = []
        for half, (w_a, w_b, boff) in enumerate((("wrb", "nwib", 0), ("wib", "wrb", 512))):
            pp = po[half]
            for f in range(4):
                p.emit("tensor", lambda t, o=pp[:], l=xlt8[:, f, i * 128:(i + 1) * 128],
                       r=sb[w_a][:, f * 512:(f + 1) * 512], st=(f == 0):
                       t.matmul(o, l, r, start=st, stop=False))
            if FFN_BIAS:
                p.emit("tensor", lambda t, o=pp[:], l=sb["ones1b"][0:1, :], r=sb["brows"][0:1, boff:boff + 512]:
                       t.matmul(o, l, r, start=False, stop=False))
            for f in range(3):
                p.emit("tensor", lambda t, o=pp[:], l=xlt8[:, 4 + f, i * 128:(i + 1) * 128],
                       r=sb[w_b][:, f * 512:(f + 1) * 512]:
                       t.matmul(o, l, r, start=False, stop=False))
            t_half.append(p.inc("tensor", lambda t, o=pp[:], l=xlt8[:, 7, i * 128:(i + 1) * 128],
                                r=sb[w_b][:, 3 * 512:4 * 512]:
                                t.matmul(o, l, r, start=False, stop=True), "TT"))
        for half in range(2):
            xslice = sb["xres"][:, i * D + half * 512: i * D + (half + 1) * 512]
            p.wait("vector", "TT", t_half[half])
            if FFN_STT:
                last_evac_vv = p.inc("vector", lambda v, a=xslice, b=po[half][:], s=p.const_col(0.5):
                                     v.scalar_tensor_tensor(a, b, s, a, op0=ALU.mult, op1=ALU.add), "VV")
            else:
                last_evac_vv = p.inc("vector", lambda v, a=xslice, b=po[half][:], h=half:
                                     v.tensor_copy(a, b), "VV")
    return last_evac_vv



def decl_l0attn(p: Prog):
    p.din("wq", [1024, 1024], BF16)
    p.din("wk", [1024, 1024], BF16)
    p.din("wv", [1024, 1024], BF16)
    p.din("maskset", [2048, 256], BF16)
    p.sbuf("wqkv_b", [128, 8 * 1024], BF16)
    p.sbuf("qts", [128, 8 * 2048], BF16)
    p.sbuf("ktstage", [128, 2 * 8 * 512], BF16)
    p.sbuf("vstage", [128, 2 * 1024], BF16)
    p.sbuf("masks", [128, 16 * 256], BF16)
    p.sbuf("onesb", [128, 1], BF16)
    p.sbuf("rzcol", [128, 16], F32)
    p.dint("agk_in", [1024, 2048], BF16)
    p.dint("agk_out", [8192, 2048], BF16)
    p.dint("agv_in", [2048, 1024], BF16)
    p.dint("agv_out", [16384, 1024], BF16)
    p.dint("bar_in", [1, 8], F32)
    p.dint("bar_out", [1, 8], F32)


def l0attn_host_inputs(W_q, W_k, W_v, core):
    d = {"wq": np.ascontiguousarray(np.asarray(W_q, dtype=np.float32).astype(ml_dtypes.bfloat16)),
         "wk": np.ascontiguousarray(np.asarray(W_k, dtype=np.float32).astype(ml_dtypes.bfloat16)),
         "wv": np.ascontiguousarray(np.asarray(W_v, dtype=np.float32).astype(ml_dtypes.bfloat16))}
    kv = np.arange(128)[:, None]
    q = np.arange(256)[None, :]
    ms = np.zeros((16, 128, 256), dtype=ml_dtypes.bfloat16)
    for r in range(16):
        ms[r] = (q + (2 * core - r) * 128 >= kv).astype(ml_dtypes.bfloat16)
    d["maskset"] = ms.reshape(2048, 256)
    return d


def emit_load_weight_1024(p: Prog, dram_name, xlt_busy_sc=0):
    sb = p.sb
    ld = 0
    for hf in range(2):
        src = p.dram[dram_name].ap()[hf * 512:(hf + 1) * 512, :].rearrange("(f p) o -> p f o", p=128)
        ld = p.dma("sync", sb["wqkv_b"][:, hf * 4096:(hf + 1) * 4096].rearrange("p (f o) -> p f o", f=4), src, "LD")
    return ld


L0A_DEBUG = 0


def emit_l0attn(p: Prog, x_ready_ld):
    sb = p.sb
    xlt8 = sb["xlt"][:].rearrange("p (f t) -> p f t", f=8)

    ld = p.dma("sync", sb["masks"][:].rearrange("p (r q) -> p r q", r=16),
               p.dram["maskset"].ap().rearrange("(r p) q -> p r q", p=128), "LD")
    p.wait("vector", "LD", ld)
    p.inc("vector", lambda v: v.memset(p.sb["onesb"][:], 1.0), "VV")

    p.wait("vector", "LD", x_ready_ld)
    last_sq = [0] * 4
    last_tr = [0] * 4
    last_ec = 0
    for i in range(NTILE):
        slot = i % 2
        if last_sq[slot]:
            p.wait("vector", "SS", last_sq[slot])
        if last_tr[slot]:
            p.wait("vector", "TT", last_tr[slot])
        vc_ln = emit_ln_tile(p, i, "xlnb", slot * D, slot)
        last_sq[slot] = p.cnt["SS"] - 1
        ec = emit_transpose_tile(p, sb["xlnb"][:, slot * D:(slot + 1) * D],
                                 xlt8[:, :, i * 128:(i + 1) * 128], "VV", vc_ln)
        last_tr[slot] = p.cnt["TT"]
        last_ec = ec

    pq = [p.ps["pA"], p.ps["pB"]]
    bank_ev = [0, 0]
    kst_store = [0, 0]
    vst_store = [0, 0]
    st = 0

    def mm_group(bank, lhs_list, rhs_list):
        p.wait("tensor", "SS", bank_ev[bank])
        n = len(lhs_list)
        tcn = 0
        for j, (l, r) in enumerate(zip(lhs_list, rhs_list)):
            fn = (lambda t, o=pq[bank][:], l=l, r=r, s0=(j == 0), s1=(j == n - 1):
                  t.matmul(o, l, r, start=s0, stop=s1))
            if j == n - 1:
                tcn = p.inc("tensor", fn, "TT")
            else:
                p.emit("tensor", fn)
        return tcn

    for proj, wname in ((0, "wq"), (1, "wk")):
        if proj:
            p.wait("sync", "TT", p.cnt["TT"])
        wv = emit_load_weight_1024(p, wname)
        p.wait("tensor", "LD", wv)
        if proj == 0:
            p.wait("tensor", "SS", last_ec)
        for tch in range(4):
            if proj == 1:
                p.wait("scalar", "ST", kst_store[tch % 2])
            for ob in range(8):
                bank = ob % 2
                lhs = [sb["wqkv_b"][:, f * 1024 + ob * 128: f * 1024 + (ob + 1) * 128] for f in range(8)]
                rhs = [xlt8[:, f, tch * 512:(tch + 1) * 512] for f in range(8)]
                tcn = mm_group(bank, lhs, rhs)
                p.wait("scalar", "TT", tcn)
                if proj == 0:
                    dst = sb["qts"][:, ob * 2048 + tch * 512: ob * 2048 + (tch + 1) * 512]
                else:
                    dst = sb["ktstage"][:, (tch % 2) * 4096 + ob * 512:(tch % 2) * 4096 + (ob + 1) * 512]
                bank_ev[bank] = p.inc("scalar", lambda s, a=dst, b=pq[bank][:]: s.activation(a, b, AF.Copy), "SS")
            if proj == 1:
                p.wait("sync", "SS", bank_ev[1])
                for ob in range(8):
                    src = sb["ktstage"][:, (tch % 2) * 4096 + ob * 512:(tch % 2) * 4096 + (ob + 1) * 512]
                    dst = p.dram["agk_in"].ap()[ob * 128:(ob + 1) * 128, tch * 512:(tch + 1) * 512]
                    st = p.dma("sync", dst, src, "ST")
                kst_store[tch % 2] = st

    p.wait("gpsimd", "ST", st)
    cc = p.collective("AllGather", ALU.bypass, [p.dram["agk_in"].ap().opt()], [p.dram["agk_out"].ap().opt()])
    p.wait("sync", "TT", p.cnt["TT"])
    wvv = emit_load_weight_1024(p, "wv")
    p.wait("tensor", "LD", wvv)
    for tt in range(NTILE):
        vb = tt % 2
        p.wait("scalar", "ST", vst_store[vb])
        for half in range(2):
            bank = half
            lhs = [xlt8[:, f, tt * 128:(tt + 1) * 128] for f in range(8)]
            rhs = [sb["wqkv_b"][:, f * 1024 + half * 512: f * 1024 + (half + 1) * 512] for f in range(8)]
            tcn = mm_group(bank, lhs, rhs)
            p.wait("scalar", "TT", tcn)
            bank_ev[bank] = p.inc("scalar", lambda s,
                                  a=sb["vstage"][:, vb * 1024 + half * 512: vb * 1024 + (half + 1) * 512],
                                  b=pq[bank][:]: s.activation(a, b, AF.Copy), "SS")
        p.wait("sync", "SS", bank_ev[1])
        dst = p.dram["agv_in"].ap()[tt * 128:(tt + 1) * 128, :]
        st = p.dma("sync", dst, sb["vstage"][:, vb * 1024:(vb + 1) * 1024], "ST")
        vst_store[vb] = st

    p.wait("gpsimd", "ST", st)
    cc = p.collective("AllGather", ALU.bypass, [p.dram["agv_in"].ap().opt()], [p.dram["agv_out"].ap().opt()])
    cc = p.collective("AllReduce", ALU.add, [p.dram["bar_in"].ap().opt()], [p.dram["bar_out"].ap().opt()])
    p.wait("sync", "CC", cc)
    p.wait("gpsimd", "CC", cc)

    agk = p.dram["agk_out"].ap().rearrange("(r f p) t -> r f p t", f=8, p=128)
    agv = p.dram["agv_out"].ap()
    pc = [[p.ps["pc00"], p.ps["pc01"]], [p.ps["pc10"], p.ps["pc11"]]]
    psb = [p.ps["pA"], p.ps["ptr2"].bitcast(F32)]
    pz = p.ps["pB"]
    p.wait("tensor", "SS", p.cnt["SS"])
    p.wait("tensor", "VV", p.cnt["VV"])

    esc_hist = [0, 0]
    tsc_hist = [0, 0]
    pair_lk = pair_lv = 0
    last_ctx_tt = 0
    lvv = 0
    for j in range(8):
        tloc0 = j * 256
        nkv = 16 * (j + 1)

        def emit_ctx(kt, wbuf, bufp, esc, mvc, lv):
            nonlocal last_ctx_tt
            first = (kt == 0)
            last = (kt == nkv - 1)
            if mvc:
                p.wait("tensor", "VV", mvc)
            else:
                p.wait("tensor", "SS", esc)
            p.wait("tensor", "LV", lv)
            for sub in range(2):
                lhs = wbuf[:, sub * 128:(sub + 1) * 128]
                p.emit("tensor", lambda t, o=pz[:, 2 * sub:2 * sub + 1], l=lhs, r=sb["onesb"][:],
                       s0=(first and sub == 0), s1=(last and sub == 1): t.matmul(o, l, r, start=s0, stop=s1))
                for half in range(2):
                    vb0 = 4096 + bufp * 2048 + (kt % 2) * 1024
                    fn = (lambda t, o=pc[sub][half][:], l=lhs,
                          r=sb["ktstage"][:, vb0 + half * 512: vb0 + (half + 1) * 512],
                          s0=first, s1=last: t.matmul(o, l, r, start=s0, stop=s1))
                    if sub == 1 and half == 1:
                        last_ctx_tt = p.inc("tensor", fn, "TT")
                    else:
                        p.emit("tensor", fn)

        pend = None
        for kt in range(nkv):
            bufp = (kt // 2) % 2
            r = (kt // 2) % 8
            if kt % 2 == 0:
                col0 = (kt // 16) * 256
                p.wait("sync", "TT", tsc_hist[1])
                p.wait("sync", "SS", esc_hist[0])
                dst_k = sb["ktstage"][:, bufp * 2048:(bufp + 1) * 2048].rearrange("p (f t) -> p f t", f=8)
                lk = p.dma("sync", dst_k[:, 0:4, :], agk[r, 0:4, :, col0:col0 + 256].rearrange("f p t -> p f t"), "LK")
                lk = p.dma("sync", dst_k[:, 4:8, :], agk[r, 4:8, :, col0:col0 + 256].rearrange("f p t -> p f t"), "LK")
                row0 = r * 2048 + col0
                lv = p.dma("sync",
                           sb["ktstage"][:, 4096 + bufp * 2048: 4096 + (bufp + 1) * 2048].rearrange("p (s c) -> p s c", s=2),
                           agv[row0:row0 + 256, :].rearrange("(s p) c -> p s c", p=128), "LV")
                pair_lk, pair_lv = lk, lv
            lk, lv = pair_lk, pair_lv
            ps_s = psb[kt % 2]
            p.wait("tensor", "LK", lk)
            p.wait("tensor", "SS", esc_hist[1])
            kbp = sb["ktstage"][:, bufp * 2048:(bufp + 1) * 2048].rearrange("p (f t) -> p f t", f=8)
            tsc = 0
            for f in range(8):
                fn = (lambda t, o=ps_s[:, 0:256], l=kbp[:, f, (kt % 2) * 128:(kt % 2) * 128 + 128],
                      r2=sb["qts"][:, f * 2048 + tloc0: f * 2048 + tloc0 + 256],
                      s0=(f == 0), s1=(f == 7): t.matmul(o, l, r2, start=s0, stop=s1))
                if f == 7:
                    tsc = p.inc("tensor", fn, "TT")
                else:
                    p.emit("tensor", fn)
            tsc_hist = [tsc, tsc_hist[0]]
            if pend is not None:
                emit_ctx(*pend)
                pend = None
            wbuf = sb["vstage"][:, (kt % 4) * 256:(kt % 4) * 256 + 256]
            p.wait("scalar", "TT", tsc)
            esc = p.inc("scalar", lambda s, a=wbuf, b=ps_s[:, 0:256]:
                        s.activation(a, b, AF.Exp, scale=np.float32(1.0 / 32.0)), "SS")
            esc_hist = [esc, esc_hist[0]]
            rel = kt - 16 * j
            mvc = 0
            if rel >= 0:
                p.wait("vector", "SS", esc)
                mvc = p.inc("vector", lambda v, a=wbuf, m=sb["masks"][:, rel * 256:(rel + 1) * 256]:
                            v.tensor_tensor(a, a, m, op=ALU.mult), "VV")
            pend = (kt, wbuf, bufp, esc, mvc, lv)
        emit_ctx(*pend)
        pend = None
        p.wait("vector", "TT", last_ctx_tt)
        if L0A_DEBUG and j == 0:
            p.inc("vector", lambda v, a=sb["sq"][:, 3088:3600], b=pc[0][0][:]:
                  v.tensor_copy(a, b), "VV")
            p.inc("vector", lambda v, a=sb["sq"][:, 3600:3728], b=sb["xres"][:, 0:128]:
                  v.tensor_copy(a, b), "VV")
        for sub in range(2):
            rz = sb["rzcol"][:, j * 2 + sub: j * 2 + sub + 1]
            rc = p.inc("vector", lambda v, a=rz, b=pz[:, 2 * sub:2 * sub + 1]: v.reciprocal(a, b), "VV")
            p.wait("vector", "VV", rc)
            tile = j * 2 + sub
            for half in range(2):
                xsl = sb["xres"][:, tile * D + half * 512: tile * D + (half + 1) * 512]
                lvv = p.inc("vector", lambda v, a=xsl, b=pc[sub][half][:], s=rz:
                            v.scalar_tensor_tensor(a, b, s, a, op0=ALU.mult, op1=ALU.add), "VV")
        p.wait("tensor", "VV", lvv)
    return lvv



MOE_C = 5120.0


def decl_moe(p: Prog):
    p.din("rwt", [1024, 8], F32)
    p.din("w1b", [8, 1024, 4096], BF16)
    p.din("w2b", [8, 4096, 1024], BF16)
    p.din("tri", [128, 128], F32)
    p.din("selmat", [16 * 128, 128], F32)
    p.din("onesrow", [1, 128], F32)
    p.din("identf", [128, 128], F32)
    p.dint("cnt_in", [1, 256], F32)
    p.dint("cnt_out", [8, 256], F32)

    p.sbuf("rwt_s", [128, 64], BF16)
    p.sbuf("rwt_f", [128, 64], F32)
    p.sbuf("sel_s", [128, 16 * 128], F32)
    p.sbuf("onesrow_s", [1, 128], F32)
    p.sbuf("identf_s", [128, 128], F32)
    p.sbuf("tri_s", [128, 128], F32)
    p.sbuf("lgt", [128, 8], F32)
    p.sbuf("prb", [128, 24], F32)
    p.sbuf("rstat", [128, 8], F32)
    p.sbuf("msk", [128, 256], F32)
    p.sbuf("gts", [128, 32], F32)
    p.sbuf("cntrow", [1, 256], F32)
    p.sbuf("cagT", [128, 16], F32)
    p.sbuf("cofs", [1, 128], F32)
    p.sbuf("rwbuf", [1, 8], F32)
    p.sbuf("keepb", [128, 8], F32)
    p.sbuf("coef", [128, 128], F32)
    p.sbuf("w1qb", [128, 2 * 8192], BF16)
    p.sbuf("w2qb", [128, 8192], BF16)
    p.sbuf("hTq", [128, 8 * 1024], BF16)


_MOE_WCACHE = {}


def moe_host_inputs(router_w, w1, w2, core):
    key = (id(w1), id(w2))
    if key not in _MOE_WCACHE:
        _MOE_WCACHE.clear()
        _MOE_WCACHE[key] = (np.ascontiguousarray(np.asarray(w1, dtype=np.float32).astype(ml_dtypes.bfloat16)),
                            np.ascontiguousarray(np.asarray(w2, dtype=np.float32).astype(ml_dtypes.bfloat16)))
    w1b, w2b = _MOE_WCACHE[key]
    d = {"rwt": np.ascontiguousarray(router_w.T, dtype=np.float32),
         "w1b": w1b,
         "w2b": w2b}
    t_ = np.arange(128)
    d["tri"] = (t_[:, None] < t_[None, :]).astype(np.float32)
    d["onesrow"] = np.ones((1, 128), dtype=np.float32)
    d["identf"] = np.eye(128, dtype=np.float32)
    sel = np.zeros((8, 2, 128, 128), dtype=np.float32)
    for cp in range(8):
        for half in range(2):
            for tl in range(8):
                tp = tl + 8 * half
                jp = tp // 2
                for j in range(8):
                    if 8 * jp + cp < 8 * j + core:
                        for k in range(2):
                            for e in range(8):
                                sel[cp, half, tl * 16 + k * 8 + e, j * 16 + k * 8 + e] = 1.0
    d["selmat"] = sel.reshape(16, 128, 128).reshape(2048, 128)
    return d


def emit_moe_weight_quarter(p: Prog, src_ap, dst_region, prev_user_tt):
    p.wait("sync", "TT", prev_user_tt)
    ld = 0
    for hf in range(2):
        src = src_ap[hf * 512:(hf + 1) * 512, :].rearrange("(f p) c -> p f c", p=128)
        ld = p.dma("sync", dst_region[:, hf * 4096:(hf + 1) * 4096].rearrange("p (f c) -> p f c", f=4), src, "LD")
    return ld


def emit_moe(p: Prog, x_ready_ld):
    sb = p.sb
    xlt8 = sb["xlt"][:].rearrange("p (f t) -> p f t", f=8)
    one_col = p.const_col(1.0)
    half_col = p.const_col(0.5)
    cap_col = p.const_col(MOE_C)
    pA, pB = p.ps["pA"], p.ps["pB"]

    ldr = p.dma("sync", sb["rwt_f"][:].rearrange("p (f e) -> p f e", f=8),
                p.dram["rwt"].ap().rearrange("(f p) e -> p f e", p=128), "LD")
    p.wait("vector", "LD", ldr)
    rwc = p.inc("vector", lambda v: v.tensor_copy(p.sb["rwt_s"][:], p.sb["rwt_f"][:]), "VV")
    for piece in range(2):
        ld = p.dma("sync", sb["sel_s"][:, piece * 1024:(piece + 1) * 1024].rearrange("p (c x) -> p c x", c=8),
                   p.dram["selmat"].ap()[piece * 1024:(piece + 1) * 1024].rearrange("(c p) x -> p c x", p=128), "LD")
    ld = p.dma("sync", sb["onesrow_s"][:], p.dram["onesrow"].ap(), "LD")
    ld = p.dma("sync", sb["identf_s"][:], p.dram["identf"].ap(), "LD")
    ld = p.dma("sync", sb["tri_s"][:], p.dram["tri"].ap(), "LD")
    ld_aux = ld

    p.wait("vector", "LD", x_ready_ld)
    last_sq = [0] * 4
    last_tr = [0] * 4
    last_ec = 0
    for i in range(NTILE):
        slot = i % 2
        if last_sq[slot]:
            p.wait("vector", "SS", last_sq[slot])
        if last_tr[slot]:
            p.wait("vector", "TT", last_tr[slot])
        vc_ln = emit_ln_tile(p, i, "xlnb", slot * D, slot)
        last_sq[slot] = p.cnt["SS"] - 1
        ec = emit_transpose_tile(p, sb["xlnb"][:, slot * D:(slot + 1) * D],
                                 xlt8[:, :, i * 128:(i + 1) * 128], "VV", vc_ln)
        last_tr[slot] = p.cnt["TT"]
        last_ec = ec

    p.wait("tensor", "LD", ld_aux)
    p.wait("tensor", "VV", rwc)
    p.wait("tensor", "SS", last_ec)
    cnt_mm_ev = 0
    lgt_ev = 0
    for t in range(NTILE):
        p.wait("tensor", "SS", lgt_ev)
        tcn = 0
        for f in range(8):
            fn = (lambda tt, o=pA[:, 0:8], l=xlt8[:, f, t * 128:(t + 1) * 128],
                  r=sb["rwt_s"][:, f * 8:(f + 1) * 8],
                  s0=(f == 0), s1=(f == 7): tt.matmul(o, l, r, start=s0, stop=s1))
            if f == 7:
                tcn = p.inc("tensor", fn, "TT")
            else:
                p.emit("tensor", fn)
        p.wait("scalar", "TT", tcn)
        lgt_ev = p.inc("scalar", lambda s: s.activation(p.sb["lgt"][:], p.ps["pA"][:, 0:8], AF.Copy), "SS")
        p.wait("vector", "SS", lgt_ev)
        m0 = sb["msk"][:, (t * 2) * 8:(t * 2) * 8 + 8]
        m1 = sb["msk"][:, (t * 2 + 1) * 8:(t * 2 + 1) * 8 + 8]

        def vstep(fn):
            c = p.inc("vector", fn, "VV")
            p.wait("vector", "VV", c)
            return c

        vstep(lambda v: v.reduce_max(p.sb["rstat"][:, 6:7], p.sb["lgt"][:], axis=AX.X))
        svc = vstep(lambda v: v.tensor_scalar(p.sb["prb"][:, 8:16], p.sb["lgt"][:], p.sb["rstat"][:, 6:7], None, op0=ALU.subtract))
        p.wait("scalar", "VV", svc)
        sexp = p.inc("scalar", lambda s: s.activation(p.sb["prb"][:, 0:8], p.sb["prb"][:, 8:16], AF.Exp), "SS")
        p.wait("vector", "SS", sexp)
        vstep(lambda v: v.reduce_sum(p.sb["rstat"][:, 7:8], p.sb["prb"][:, 0:8], axis=AX.X))
        vstep(lambda v: v.reciprocal(p.sb["rstat"][:, 5:6], p.sb["rstat"][:, 7:8]))
        vstep(lambda v: v.tensor_scalar_mul(p.sb["prb"][:, 0:8], p.sb["prb"][:, 0:8], p.sb["rstat"][:, 5:6]))
        vstep(lambda v: v.reduce_max(p.sb["rstat"][:, 0:1], p.sb["prb"][:, 0:8], axis=AX.X))
        vstep(lambda v, a=m0: v.tensor_scalar(a, p.sb["prb"][:, 0:8], p.sb["rstat"][:, 0:1], None, op0=ALU.is_ge))
        vstep(lambda v, a=m0: v.tensor_tensor(p.sb["prb"][:, 16:24], p.sb["prb"][:, 0:8], a, op=ALU.mult))
        vstep(lambda v: v.tensor_tensor(p.sb["prb"][:, 8:16], p.sb["prb"][:, 0:8], p.sb["prb"][:, 16:24], op=ALU.subtract))
        vstep(lambda v: v.reduce_max(p.sb["rstat"][:, 1:2], p.sb["prb"][:, 8:16], axis=AX.X))
        vstep(lambda v, a=m1: v.tensor_scalar(a, p.sb["prb"][:, 8:16], p.sb["rstat"][:, 1:2], None, op0=ALU.is_ge))
        vstep(lambda v: v.tensor_tensor(p.sb["rstat"][:, 2:3], p.sb["rstat"][:, 0:1], p.sb["rstat"][:, 1:2], op=ALU.add))
        vstep(lambda v: v.reciprocal(p.sb["rstat"][:, 3:4], p.sb["rstat"][:, 2:3]))
        vstep(lambda v, s=half_col: v.tensor_scalar_mul(p.sb["rstat"][:, 3:4], p.sb["rstat"][:, 3:4], s))
        vstep(lambda v, a=sb["gts"][:, t * 2:t * 2 + 1]: v.tensor_tensor(a, p.sb["rstat"][:, 0:1], p.sb["rstat"][:, 3:4], op=ALU.mult))
        mvc = vstep(lambda v, a=sb["gts"][:, t * 2 + 1:t * 2 + 2]: v.tensor_tensor(a, p.sb["rstat"][:, 1:2], p.sb["rstat"][:, 3:4], op=ALU.mult))
        p.wait("tensor", "VV", mvc)
        p.wait("tensor", "SS", cnt_mm_ev)
        for k in range(2):
            mk = sb["msk"][:, (t * 2 + k) * 8:(t * 2 + k) * 8 + 8]
            tcn2 = p.inc("tensor", lambda tt, m=mk, oc=one_col: tt.matmul(p.ps["pB"][0:1, 0:8], oc, m, start=True, stop=True), "TT")
            p.wait("scalar", "TT", tcn2)
            cnt_mm_ev = p.inc("scalar", lambda s, t_=t, k_=k: s.activation(
                p.sb["cntrow"][0:1, (t_ * 2 + k_) * 8:(t_ * 2 + k_) * 8 + 8], p.ps["pB"][0:1, 0:8], AF.Copy), "SS")

    p.wait("gpsimd", "SS", cnt_mm_ev)
    stc = p.dma("gpsimd", p.dram["cnt_in"].ap(), sb["cntrow"][:], "ST")
    p.wait("gpsimd", "ST", stc)
    cc = p.collective("AllGather", ALU.bypass, [p.dram["cnt_in"].ap().opt()], [p.dram["cnt_out"].ap().opt()])
    p.wait("sync", "CC", cc)
    p.wait("sync", "SS", p.cnt["SS"])
    ldc = p.dma("sync", sb["sq"][0:8, 0:256], p.dram["cnt_out"].ap(), "LD")
    p.wait("tensor", "LD", ldc)
    p.wait("tensor", "SS", p.cnt["SS"])
    tt1 = p.inc("tensor", lambda t: t.transpose(p.ps["pA"][:, 0:8], p.sb["sq"][0:8, 0:128], p.sb["identf_s"][0:8, 0:8]), "TT")
    tt1 = p.inc("tensor", lambda t: t.transpose(p.ps["pA"][:, 8:16], p.sb["sq"][0:8, 128:256], p.sb["identf_s"][0:8, 0:8]), "TT")
    p.wait("vector", "TT", tt1)
    vcag = p.inc("vector", lambda v: v.tensor_copy(p.sb["cagT"][:], p.ps["pA"][:, 0:16]), "VV")
    p.wait("tensor", "VV", vcag)
    ncc = 0
    for cp in range(8):
        for half in range(2):
            blk = cp * 2 + half
            fn = (lambda t, l=sb["cagT"][:, half * 8 + cp: half * 8 + cp + 1],
                  r=sb["sel_s"][:, blk * 128:(blk + 1) * 128],
                  s0=(blk == 0), s1=(blk == 15): t.matmul(p.ps["pB"][0:1, 0:128], l, r, start=s0, stop=s1))
            if blk == 15:
                ncc = p.inc("tensor", fn, "TT")
            else:
                p.emit("tensor", fn)
    p.wait("scalar", "TT", ncc)
    ofs_ev = p.inc("scalar", lambda s: s.activation(p.sb["cofs"][:], p.ps["pB"][0:1, 0:128], AF.Copy), "SS")

    p.wait("vector", "SS", ofs_ev)
    pos_ev = 0
    coef_last = 0
    for t in range(NTILE):
        j = t // 2
        for k in range(2):
            osl = sb["cofs"][0:1, j * 16 + k * 8: j * 16 + k * 8 + 8]
            if t % 2 == 0:
                rb = p.inc("vector", lambda v, o=osl: v.tensor_copy(p.sb["rwbuf"][:], o), "VV")
            else:
                rb = p.inc("vector", lambda v, o=osl, c=sb["cntrow"][0:1, ((t - 1) * 2 + k) * 8: ((t - 1) * 2 + k) * 8 + 8]:
                           v.tensor_tensor(p.sb["rwbuf"][:], o, c, op=ALU.add), "VV")
            p.wait("tensor", "VV", rb)
            p.wait("tensor", "VV", coef_last)
            mk = sb["msk"][:, (t * 2 + k) * 8:(t * 2 + k) * 8 + 8]
            p.emit("tensor", lambda tt, m=mk: tt.matmul(p.ps["pA"][:, 0:8], p.sb["tri_s"][:], m, start=True, stop=False))
            pos_ev = p.inc("tensor", lambda tt: tt.matmul(p.ps["pA"][:, 0:8], p.sb["onesrow_s"][:], p.sb["rwbuf"][:], start=False, stop=True), "TT")
            p.wait("vector", "TT", pos_ev)

            def vstep2(fn):
                c = p.inc("vector", fn, "VV")
                p.wait("vector", "VV", c)
                return c

            vstep2(lambda v, s=cap_col: v.tensor_scalar(p.sb["keepb"][:], p.ps["pA"][:, 0:8], s, None, op0=ALU.is_lt))
            vstep2(lambda v, m=mk: v.tensor_tensor(p.sb["keepb"][:], p.sb["keepb"][:], m, op=ALU.mult))
            cdst = sb["coef"][:, t * 8:(t + 1) * 8]
            if k == 0:
                coef_last = vstep2(lambda v, a=cdst, g=sb["gts"][:, t * 2:t * 2 + 1]:
                                   v.tensor_scalar_mul(a, p.sb["keepb"][:], g))
            else:
                vstep2(lambda v, g=sb["gts"][:, t * 2 + 1:t * 2 + 2]:
                       v.tensor_scalar_mul(p.sb["keepb"][:], p.sb["keepb"][:], g))
                coef_last = vstep2(lambda v, a=cdst:
                                   v.tensor_tensor(a, a, p.sb["keepb"][:], op=ALU.add))

    p.wait("tensor", "VV", p.cnt["VV"])
    p.wait("sync", "TT", p.cnt["TT"])
    wq_user_tt = [0, 0]
    w2_user_tt = 0
    last_stt = 0
    for e in range(8):
        for qr in range(4):
            wb = (e * 4 + qr) % 2
            w1r = sb["w1qb"][:, wb * 8192:(wb + 1) * 8192]
            w2r = sb["w2qb"][:]
            vw1 = emit_moe_weight_quarter(p, p.dram["w1b"].ap()[e, :, qr * 1024:(qr + 1) * 1024], w1r, wq_user_tt[wb])
            p.wait("tensor", "LD", vw1)
            vw2 = None
            pbank = [pA, pB]
            pev = [p.cnt["SS"], p.cnt["SS"]]
            pc2 = [p.ps["pc00"], p.ps["pc01"]]
            pev2 = [p.cnt["VV"], p.cnt["VV"]]
            for th in range(2):
                p.wait("tensor", "VV", pev2[0])
                p.wait("tensor", "VV", pev2[1])
                for fb in range(8):
                    for tch in range(2):
                        bk = (fb * 2 + tch) % 2
                        p.wait("tensor", "SS", pev[bk])
                        tcn = 0
                        for f in range(8):
                            fn = (lambda tt, o=pbank[bk][:], l=w1r[:, f * 1024 + fb * 128: f * 1024 + (fb + 1) * 128],
                                  r=xlt8[:, f, th * 1024 + tch * 512: th * 1024 + (tch + 1) * 512],
                                  s0=(f == 0), s1=(f == 7): tt.matmul(o, l, r, start=s0, stop=s1))
                            if f == 7:
                                tcn = p.inc("tensor", fn, "TT")
                            else:
                                p.emit("tensor", fn)
                        p.wait("scalar", "TT", tcn)
                        pev[bk] = p.inc("scalar", lambda s, a=sb["hTq"][:, fb * 1024 + tch * 512: fb * 1024 + (tch + 1) * 512],
                                        b=pbank[bk][:]: s.activation(a, b, AF.Gelu), "SS")
                ph1_all = p.cnt["SS"]
                if th == 0:
                    vw2 = emit_moe_weight_quarter(p, p.dram["w2b"].ap()[e, qr * 1024:(qr + 1) * 1024, :], w2r, w2_user_tt)
                p.wait("tensor", "SS", ph1_all)
                if th == 0:
                    p.wait("tensor", "LD", vw2)
                for tk8 in range(8):
                    tk = th * 8 + tk8
                    for ch in range(2):
                        bk = (tk8 * 2 + ch) % 2
                        p.wait("tensor", "VV", pev2[bk])
                        tcn = 0
                        for fb in range(8):
                            fn = (lambda tt, o=pc2[bk][:], l=sb["hTq"][:, fb * 1024 + tk8 * 128: fb * 1024 + (tk8 + 1) * 128],
                                  r=w2r[:, fb * 1024 + ch * 512: fb * 1024 + (ch + 1) * 512],
                                  s0=(fb == 0), s1=(fb == 7): tt.matmul(o, l, r, start=s0, stop=s1))
                            if fb == 7:
                                tcn = p.inc("tensor", fn, "TT")
                            else:
                                p.emit("tensor", fn)
                        p.wait("vector", "TT", tcn)
                        xsl = sb["xres"][:, tk * D + ch * 512: tk * D + (ch + 1) * 512]
                        pev2[bk] = p.inc("vector", lambda v, a=xsl, b=pc2[bk][:], s=sb["coef"][:, tk * 8 + e: tk * 8 + e + 1]:
                                         v.scalar_tensor_tensor(a, b, s, a, op0=ALU.mult, op1=ALU.add), "VV")
                        last_stt = pev2[bk]
            wq_user_tt[wb] = p.cnt["TT"]
            w2_user_tt = p.cnt["TT"]
    return last_stt



L1_L = 64
L1_DEBUG_STOP = 0
L1_NH = 4
L1_DK = 128
L1_DV = 256


def decl_l1attn(p: Prog):
    p.din("xpm", [128, 8], F32)
    p.din("xlw1t", [1024, 32], F32)
    p.din("xlw2t", [32, 160], F32)
    p.din("xlb2", [32, 5], F32)
    p.din("xowt", [160, 1024], F32)
    p.din("xbt", [128, 40], F32)
    p.din("rwt_l", [1024, 512], F32)
    p.din("kwt_l", [1024, 512], F32)
    p.din("vwt_l", [1024, 1024], F32)
    p.din("gwt_l", [1024, 1024], F32)
    p.din("ww1t", [1024, 64], F32)
    p.din("ww2t", [64, 512], F32)
    p.din("wbrow", [1, 512], F32)
    p.din("owt_l", [1024, 1024], F32)
    p.din("bonusrow", [1, 512], F32)
    p.din("gnwrow", [1, 1024], F32)
    p.din("gnbrow", [1, 1024], F32)
    p.din("tri64", [64, 64], F32)
    p.din("sel63", [64, 1], F32)
    p.din("mask64", [64, 64], F32)
    p.din("ident64", [64, 64], BF16)
    p.din("onesrow64", [1, 64], F32)

    p.sbuf("xpm_s", [128, 8], F32)
    p.sbuf("xlw1t_s", [128, 8 * 32], BF16)
    p.sbuf("xlw2t_s", [32, 160], BF16)
    p.sbuf("xlb2_s", [32, 5], F32)
    p.sbuf("xow_s", [32, 1024], BF16)
    p.sbuf("xbt_s", [128, 5 * 8], F32)
    p.sbuf("ww2t_s", [64, 512], BF16)
    p.sbuf("wbrow_s", [1, 512], BF16)
    p.sbuf("bonus_b", [128, 512], BF16)
    p.sbuf("gnw_b", [128, 1024], BF16)
    p.sbuf("gnb_b", [128, 1024], BF16)
    p.sbuf("tri64_s", [64, 64], F32)
    p.sbuf("sel63_s", [64, 1], F32)
    p.sbuf("mask64_s", [64, 64], F32)
    p.sbuf("ident64_s", [64, 64], BF16)
    p.sbuf("onesrow64_s", [1, 64], F32)
    p.sbuf("wbuf_l", [128, 8 * 1024], BF16)
    p.sbuf("lerp_l", [128, 8 * 512], BF16)
    p.sbuf("dlt_l", [128, 8 * 512], BF16)
    p.sbuf("x0t_l", [32, 512], BF16)
    p.sbuf("x1t_l", [32, 512], BF16)
    p.sbuf("hprevT", [128, 8 * 8], BF16)
    p.sbuf("rtmp", [128, 512], BF16)
    p.sbuf("ktmp", [128, 512], BF16)
    p.sbuf("wexpc", [128, 2 * 512], F32)
    p.sbuf("diag_l", [128, 16 * 4], F32)
    p.sbuf("S0", [128, 4 * 256], F32)
    p.sbuf("Stmp", [128, 4 * 256], BF16)
    if "mrow" not in p.sb:
        p.sbuf("mrow", [1, 1024], F32)
    p.sbuf("mt_sb", [64, 512], F32)
    p.sbuf("fac", [64, 3 * 512], BF16)
    p.sbuf("facf", [64, 2 * 512], F32)
    p.sbuf("rt_sc", [128, 512], BF16)
    p.sbuf("at_sb", [64, 4 * 64], BF16)
    p.sbuf("dcol", [128, 8], F32)
    p.sbuf("bchunk", [128, 4 * 256], BF16)
    p.sbuf("arow_st", [1, 512], F32)
    p.din("indc", [128, 8], F32)
    p.din("indc8", [8, 1], BF16)
    p.din("identf1", [1, 1], F32)
    p.din("identf64", [64, 64], F32)
    p.sbuf("indc_s", [128, 8], F32)
    p.sbuf("indc8_s", [8, 1], BF16)
    p.sbuf("scW", [64, 512], F32)
    p.sbuf("scR", [64, 512], BF16)
    p.sbuf("scK", [64, 512], BF16)
    p.sbuf("scV", [64, 1024], BF16)
    p.sbuf("scDg", [64, 4], F32)
    p.sbuf("scDm", [64, 64], BF16)
    p.sbuf("sc1", [64, 1536], BF16)
    p.sbuf("ofin", [128, 1024], BF16)
    p.sbuf("oft", [128, 8 * 128], BF16)
    p.sbuf("sgb", [128, 1024], BF16)
    p.dint("halo_pad", [65, 1024], BF16)
    p.dint("sin_d", [64 * 512, 256], BF16)
    p.dint("vdram", [SHARD, 1024], BF16)
    p.dint("kdram", [SHARD, 512], BF16)
    p.dint("rdram", [SHARD, 512], BF16)
    p.sbuf("mub", [128, 512], BF16)
    p.sbuf("tnh", [128, 64], BF16)
    p.sbuf("tnhT", [64, 128], BF16)
    p.sbuf("wstg", [128, 512], F32)
    p.sbuf("ones1bl", [1, 128], BF16)
    p.sbuf("identf1_s", [1, 1], F32)
    p.sb["bload"] = p.sb["lerp_l"]
    p.sb["sload"] = p.sb["dlt_l"]
    p.sbuf("identf_l", [64, 64], F32)
    p.sbuf("acol", [128, 256], F32)

    p.dint("gdram", [SHARD, 1024], BF16)
    p.dint("wexp_d", [SHARD, 512], F32)
    p.dint("halo_in", [8, 1024], BF16)
    p.dint("halo_out", [64, 1024], BF16)
    p.dint("agA_in", [8, 512], F32)
    p.dint("agA_out", [64, 512], F32)
    p.dint("agB_in", [8 * 512, 256], BF16)
    p.dint("agB_out", [64 * 512, 256], BF16)


def l1attn_host_inputs(z):
    g = lambda k: np.asarray(z["in_L1_attn_" + k], dtype=np.float32)
    d = {}
    d["xpm"] = np.ascontiguousarray(g("x_proj_mu").reshape(8, 128).T)
    d["xlw1t"] = np.ascontiguousarray(g("xl_w1").T)
    d["xlw2t"] = np.ascontiguousarray(g("xl_w2").T)
    d["xlb2"] = np.ascontiguousarray(g("xl_b2").reshape(5, 32).T)
    d["xowt"] = np.ascontiguousarray(g("x_out_w").T)
    d["xbt"] = np.ascontiguousarray(g("x_bias").T.reshape(8, 128, 5).transpose(1, 0, 2).reshape(128, 40))
    d["rwt_l"] = np.ascontiguousarray(g("r_w").T)
    d["kwt_l"] = np.ascontiguousarray(g("k_w").T)
    d["vwt_l"] = np.ascontiguousarray(g("v_w").T)
    d["gwt_l"] = np.ascontiguousarray(g("g_w").T)
    d["ww1t"] = np.ascontiguousarray(g("w_w1").T)
    d["ww2t"] = np.ascontiguousarray(g("w_w2").T)
    d["wbrow"] = g("w_b").reshape(1, 512)
    d["owt_l"] = np.ascontiguousarray(g("o_w").T)
    d["bonusrow"] = g("bonus").reshape(1, 512)
    d["gnwrow"] = g("gn_weight").reshape(1, 1024)
    d["gnbrow"] = g("gn_bias").reshape(1, 1024)
    t_ = np.arange(64)
    d["tri64"] = (t_[:, None] <= t_[None, :]).astype(np.float32)
    s63 = np.zeros((64, 1), dtype=np.float32)
    s63[63, 0] = 1.0
    d["sel63"] = s63
    d["mask64"] = (np.exp(40.0) * (t_[:, None] < t_[None, :])).astype(np.float32)
    d["ident64"] = np.eye(64, dtype=ml_dtypes.bfloat16)
    d["onesrow64"] = np.ones((1, 64), dtype=np.float32)
    return d


def emit_l1_weight(p: Prog, dram_name, ncols, dst_region):
    sb = p.sb
    vc = 0
    for hf in range(4):
        p.wait("sync", "VV", p.cnt["VV"])
        p.wait("sync", "SS", p.cnt["SS"])
        src = p.dram[dram_name].ap()[hf * 256:(hf + 1) * 256, :].rearrange("(f p) c -> p f c", p=128)
        ld = p.dma("sync", sb["sq"][:, 0:2 * ncols].rearrange("p (f c) -> p f c", f=2), src, "LD")
        p.wait("vector", "LD", ld)
        vc = p.inc("vector", lambda v, a=dst_region[:, hf * 2 * ncols:(hf + 1) * 2 * ncols],
                   b=sb["sq"][:, 0:2 * ncols]: v.tensor_copy(a, b), "VV")
    return vc


def emit_l1attn(p: Prog, x_ready_ld):
    sb = p.sb
    nc = p.nc
    xlt8 = sb["xlt"][:].rearrange("p (f t) -> p f t", f=8)
    pA, pB = p.ps["pA"], p.ps["pB"]
    pc00, pc01, pc10, pc11 = p.ps["pc00"], p.ps["pc01"], p.ps["pc10"], p.ps["pc11"]
    ptrs = [p.ps["ptr"], p.ps["ptr2"]]

    def vstep(fn):
        c = p.inc("vector", fn, "VV")
        p.wait("vector", "VV", c)
        return c

    def vfast(fn):
        return p.inc("vector", fn, "VV")

    ld = p.dma("sync", sb["xpm_s"][:], p.dram["xpm"].ap(), "LD")
    ld = p.dma("sync", sb["xbt_s"][:], p.dram["xbt"].ap(), "LD")
    ld = p.dma("sync", sb["xlb2_s"][0:32, :], p.dram["xlb2"].ap(), "LD")
    ld = p.dma("sync", sb["tri64_s"][0:64, :], p.dram["tri64"].ap(), "LD")
    ld = p.dma("sync", sb["sel63_s"][0:64, :], p.dram["sel63"].ap(), "LD")
    ld = p.dma("sync", sb["mask64_s"][0:64, :], p.dram["mask64"].ap(), "LD")
    ld = p.dma("sync", sb["ident64_s"][0:64, :], p.dram["ident64"].ap(), "LD")
    ld = p.dma("sync", sb["onesrow64_s"][:], p.dram["onesrow64"].ap(), "LD")
    ld = p.dma("sync", sb["onesrow_l"][:], p.dram["onesrow"].ap(), "LD")
    ld = p.dma("sync", sb["indc_s"][:], p.dram["indc"].ap(), "LD")
    ld = p.dma("sync", sb["indc8_s"][0:8, :], p.dram["indc8"].ap(), "LD")
    ld = p.dma("sync", sb["identf1_s"][0:1, :], p.dram["identf1"].ap(), "LD")
    ld = p.dma("sync", sb["identf_l"][0:64, :], p.dram["identf64"].ap(), "LD")
    ld = p.dma("sync", sb["mrow"][0:1, 0:512], p.dram["wbrow"].ap(), "LD")
    ld = p.dma("sync", sb["mrow"][0:1, 512:1024], p.dram["bonusrow"].ap(), "LD")
    ld_aux = ld
    ldw = p.dma("sync", sb["sq"][:, 0:256].rearrange("p (f c) -> p f c", f=8),
                p.dram["xlw1t"].ap().rearrange("(f p) c -> p f c", p=128), "LD")
    p.wait("vector", "LD", ldw)
    vc = p.inc("vector", lambda v: v.tensor_copy(p.sb["xlw1t_s"][:], p.sb["sq"][:, 0:256]), "VV")
    p.wait("sync", "VV", vc)
    ldw = p.dma("sync", sb["sq"][0:32, 0:160], p.dram["xlw2t"].ap(), "LD")
    p.wait("vector", "LD", ldw)
    vc = p.inc("vector", lambda v: v.tensor_copy(p.sb["xlw2t_s"][:], p.sb["sq"][0:32, 0:160]), "VV")
    p.wait("sync", "VV", vc)
    ldw = p.dma("sync", sb["sq"][0:64, 0:512], p.dram["ww2t"].ap(), "LD")
    p.wait("vector", "LD", ldw)
    vc = p.inc("vector", lambda v: v.tensor_copy(p.sb["ww2t_s"][:], p.sb["sq"][0:64, 0:512]), "VV")
    p.wait("sync", "VV", vc)
    vc = p.inc("vector", lambda v: v.tensor_copy(p.sb["wbrow_s"][:], p.sb["mrow"][0:1, 0:512]), "VV")
    vc = p.inc("vector", lambda v: v.memset(p.sb["ones1bl"][:], 1.0), "VV")
    vc_small = vc

    p.wait("vector", "LD", x_ready_ld)
    last_sq = [0] * 4
    last_tr = [0] * 4
    last_ec = 0
    for i in range(NTILE):
        slot = i % 2
        if last_sq[slot]:
            p.wait("vector", "SS", last_sq[slot])
        if last_tr[slot]:
            p.wait("vector", "TT", last_tr[slot])
        vc_ln = emit_ln_tile(p, i, "xlnb", slot * D, slot)
        last_sq[slot] = p.cnt["SS"] - 1
        if i % 2 == 1:
            j = i // 2
            p.wait("gpsimd", "VV", vc_ln)
            hs = p.dma("gpsimd", p.dram["halo_in"].ap()[j:j + 1, :],
                       sb["xlnb"][127:128, slot * D:(slot + 1) * D], "ST")
            p.wait("vector", "ST", hs)
        ec = emit_transpose_tile(p, sb["xlnb"][:, slot * D:(slot + 1) * D],
                                 xlt8[:, :, i * 128:(i + 1) * 128], "VV", vc_ln)
        last_tr[slot] = p.cnt["TT"]
        last_ec = ec

    p.wait("gpsimd", "SS", p.cnt["SS"])
    cc = p.collective("AllGather", ALU.bypass, [p.dram["halo_in"].ap().opt()], [p.dram["halo_out"].ap().opt()])
    p.wait("gpsimd", "CC", cc)
    zv = p.inc("vector", lambda v: v.memset(p.sb["lerp_l"][0:1, 0:1024], 0.0), "VV")
    p.wait("gpsimd", "VV", zv)
    st = p.dma("gpsimd", p.dram["halo_pad"].ap()[0:1, :], sb["lerp_l"][0:1, 0:1024], "ST")
    for r_ in range(64):
        c_ = r_ // 8
        j_ = r_ % 8
        m_ = 8 * j_ + c_
        st = p.dma("gpsimd", p.dram["halo_pad"].ap()[1 + m_:2 + m_, :], p.dram["halo_out"].ap()[r_:r_ + 1, :], "ST")
    p.wait("sync", "ST", st)
    hld = 0
    for j in range(8):
        hld = p.dma("sync", sb["bload"][0:8, 0:1024], p.dram["halo_pad"].ap()[8 * j:8 * j + 8, :], "LK")
        p.wait("tensor", "LK", hld)
        p.wait("tensor", "SS", p.cnt["SS"])
        t1 = p.inc("tensor", lambda t: t.matmul(p.ps["pB"][0:1, 0:512], p.sb["indc8_s"][0:8, :], p.sb["bload"][0:8, 0:512], start=True, stop=True), "TT")
        t1 = p.inc("tensor", lambda t: t.matmul(p.ps["pc11"][0:1, 0:512], p.sb["indc8_s"][0:8, :], p.sb["bload"][0:8, 512:1024], start=True, stop=True), "TT")
        p.wait("scalar", "TT", t1)
        e1 = p.inc("scalar", lambda s: s.activation(p.sb["sgb"][0:1, 0:512], p.ps["pB"][0:1, 0:512], AF.Copy), "SS")
        e1 = p.inc("scalar", lambda s: s.activation(p.sb["sgb"][0:1, 512:1024], p.ps["pc11"][0:1, 0:512], AF.Copy), "SS")
        p.wait("tensor", "SS", e1)
        tf = 0
        for f in range(8):
            tf = p.inc("tensor", lambda t, f_=f: t.transpose(p.ps["ptr"][:, 2 * f_:2 * f_ + 1],
                       p.sb["sgb"][0:1, f_ * 128:(f_ + 1) * 128], p.sb["identb_s"][0:1, 0:1]), "TT")
        p.wait("scalar", "TT", tf)
        e2 = p.inc("scalar", lambda s, j_=j: s.activation(
            p.sb["hprevT"][:, j_ * 8:(j_ + 1) * 8].rearrange("p (f o) -> p f o", o=1),
            p.ps["ptr"][:, 0:16].rearrange("p (f t) -> p f t", t=2)[:, :, 0:1], AF.Copy), "SS")
        p.wait("tensor", "SS", e2)
    hprev_ss = p.cnt["SS"]
    p.wait("sync", "SS", p.cnt["SS"])
    ldg = p.dma("sync", sb["sq"][0:1, 0:1024], p.dram["gnwrow"].ap(), "LD")
    ldg = p.dma("sync", sb["sq"][0:1, 1024:2048], p.dram["gnbrow"].ap(), "LD")
    p.wait("tensor", "LD", ldg)
    p.wait("tensor", "SS", hprev_ss)
    tb = p.inc("tensor", lambda t: t.matmul(p.ps["pc00"][:, 0:512], p.sb["onesrow_l"][0:1, 0:128], p.sb["mrow"][0:1, 512:1024], start=True, stop=True), "TT")
    tb = p.inc("tensor", lambda t: t.matmul(p.ps["pc01"][:, 0:512], p.sb["onesrow_l"][0:1, 0:128], p.sb["sq"][0:1, 0:512], start=True, stop=True), "TT")
    tb = p.inc("tensor", lambda t: t.matmul(p.ps["pc10"][:, 0:512], p.sb["onesrow_l"][0:1, 0:128], p.sb["sq"][0:1, 512:1024], start=True, stop=True), "TT")
    tb = p.inc("tensor", lambda t: t.matmul(p.ps["pc11"][:, 0:512], p.sb["onesrow_l"][0:1, 0:128], p.sb["sq"][0:1, 1024:1536], start=True, stop=True), "TT")
    tb = p.inc("tensor", lambda t: t.matmul(p.ps["pA"][:, 0:512], p.sb["onesrow_l"][0:1, 0:128], p.sb["sq"][0:1, 1536:2048], start=True, stop=True), "TT")
    p.wait("scalar", "TT", tb)
    eb = p.inc("scalar", lambda s: s.activation(p.sb["bonus_b"][:], p.ps["pc00"][:, 0:512], AF.Copy), "SS")
    eb = p.inc("scalar", lambda s: s.activation(p.sb["gnw_b"][:, 0:512], p.ps["pc01"][:, 0:512], AF.Copy), "SS")
    eb = p.inc("scalar", lambda s: s.activation(p.sb["gnw_b"][:, 512:1024], p.ps["pc10"][:, 0:512], AF.Copy), "SS")
    eb = p.inc("scalar", lambda s: s.activation(p.sb["gnb_b"][:, 0:512], p.ps["pc11"][:, 0:512], AF.Copy), "SS")
    eb = p.inc("scalar", lambda s: s.activation(p.sb["gnb_b"][:, 512:1024], p.ps["pA"][:, 0:512], AF.Copy), "SS")
    p.wait("tensor", "SS", eb)

    pq = [pA, pB]
    bank_ev = [p.cnt["SS"], p.cnt["SS"]]

    def emit_xchain(g_):
        p.wait("vector", "TT", p.cnt["TT"])
        for f in range(8):
            c0 = 512 * g_
            dsl = sb["dlt_l"][:, f * 512:(f + 1) * 512]
            xf = xlt8[:, f, :]
            j0, j1 = 2 * g_, 2 * g_ + 1
            vfast(lambda v, a=dsl[:, 1:256], i0=xf[:, c0 + 0:c0 + 255], i1=xf[:, c0 + 1:c0 + 256]:
                  v.tensor_tensor(a, i0, i1, op=ALU.subtract))
            vfast(lambda v, a=dsl[:, 257:512], i0=xf[:, c0 + 256:c0 + 511], i1=xf[:, c0 + 257:c0 + 512]:
                  v.tensor_tensor(a, i0, i1, op=ALU.subtract))
            vfast(lambda v, a=dsl[:, 0:1], i0=sb["hprevT"][:, j0 * 8 + f:j0 * 8 + f + 1], i1=xf[:, c0:c0 + 1]:
                  v.tensor_tensor(a, i0, i1, op=ALU.subtract))
            vfast(lambda v, a=dsl[:, 256:257], i0=sb["hprevT"][:, j1 * 8 + f:j1 * 8 + f + 1], i1=xf[:, c0 + 256:c0 + 257]:
                  v.tensor_tensor(a, i0, i1, op=ALU.subtract))
        for f in range(8):
            vfast(lambda v, a=sb["lerp_l"][:, f * 512:(f + 1) * 512], d=sb["dlt_l"][:, f * 512:(f + 1) * 512],
                  s=sb["xpm_s"][:, f:f + 1], x=xlt8[:, f, 512 * g_:512 * (g_ + 1)]:
                  v.scalar_tensor_tensor(a, d, s, x, op0=ALU.mult, op1=ALU.add))
        xl_vv = p.cnt["VV"]
        p.wait("tensor", "VV", xl_vv)
        p.wait("tensor", "SS", bank_ev[0])
        tcn = 0
        for f in range(8):
            fn = (lambda t, o=pA[0:32, 0:512], l=sb["xlw1t_s"][:, f * 32:(f + 1) * 32],
                  r=sb["lerp_l"][:, f * 512:(f + 1) * 512],
                  s0=(f == 0), s1=(f == 7): t.matmul(o, l, r, start=s0, stop=s1))
            if f == 7:
                tcn = p.inc("tensor", fn, "TT")
            else:
                p.emit("tensor", fn)
        p.wait("scalar", "TT", tcn)
        bank_ev[0] = p.inc("scalar", lambda s: s.activation(p.sb["x0t_l"][:], pA[0:32, 0:512], AF.Tanh), "SS")
        return bank_ev[0]

    def emit_mu_lerp(n, g_, x0_ss):
        p.wait("tensor", "SS", x0_ss)
        p.wait("tensor", "SS", bank_ev[1])
        t1 = p.inc("tensor", lambda t, n_=n: t.matmul(pB[0:32, 0:512], p.sb["xlw2t_s"][:, n_ * 32:(n_ + 1) * 32],
                   p.sb["x0t_l"][:], start=True, stop=True), "TT")
        p.wait("scalar", "TT", t1)
        bank_ev[1] = p.inc("scalar", lambda s, n_=n: s.activation(p.sb["x1t_l"][:], pB[0:32, 0:512], AF.Tanh,
                           bias=p.sb["xlb2_s"][:, n_:n_ + 1], scale=np.float32(1.0)), "SS")
        x1_ss = bank_ev[1]
        for f in range(8):
            bk = f % 2
            p.wait("tensor", "SS", x1_ss if f == 0 else 0)
            p.wait("tensor", "SS", bank_ev[bk])
            t2 = p.inc("tensor", lambda t, f_=f, o=pq[bk][:, 0:512]: t.matmul(
                o, p.sb["xow_s"][:, f_ * 128:(f_ + 1) * 128], p.sb["x1t_l"][:], start=True, stop=True), "TT")
            p.wait("scalar", "TT", t2)
            p.wait("scalar", "VV", p.cnt["VV"])
            bank_ev[bk] = p.inc("scalar", lambda s, f_=f, n_=n, o=pq[bk][:, 0:512]: s.activation(
                p.sb["mub"][:], o, AF.Identity, bias=p.sb["xbt_s"][:, f_ * 5 + n_:f_ * 5 + n_ + 1], scale=np.float32(1.0)), "SS")
            p.wait("vector", "SS", bank_ev[bk])
            vfast(lambda v, f_=f: v.tensor_tensor(p.sb["mub"][:], p.sb["dlt_l"][:, f_ * 512:(f_ + 1) * 512],
                  p.sb["mub"][:], op=ALU.mult))
            vfast(lambda v, f_=f, g2=g_: v.tensor_tensor(p.sb["lerp_l"][:, f_ * 512:(f_ + 1) * 512],
                  xlt8[:, f_, 512 * g2:512 * (g2 + 1)], p.sb["mub"][:], op=ALU.add))
        return p.cnt["VV"]

    PROJS = [("r", 0, "rwt_l", 512), ("w", 1, None, None), ("k", 2, "kwt_l", 512),
             ("v", 3, "vwt_l", 1024), ("g", 4, "gwt_l", 1024)]
    st_g = 0
    for pname, n, wdram, ncols in PROJS:
        p.wait("vector", "TT", p.cnt["TT"])
        if pname == "w":
            wv = emit_l1_weight(p, "ww1t", 64, sb["wbuf_l"])
        else:
            wv = emit_l1_weight(p, wdram, ncols, sb["wbuf_l"])
        p.wait("sync", "VV", p.cnt["VV"])
        p.wait("sync", "SS", p.cnt["SS"])
        ldw2 = p.dma("sync", sb["sq"][0:32, 0:1024], p.dram["xowt"].ap()[n * 32:(n + 1) * 32, :], "LD")
        p.wait("vector", "LD", ldw2)
        wv2 = p.inc("vector", lambda v: v.tensor_copy(p.sb["xow_s"][:], p.sb["sq"][0:32, 0:1024]), "VV")
        p.wait("tensor", "VV", wv2)
        for g_ in range(4):
            x0_ss = emit_xchain(g_)
            lerp_vv = emit_mu_lerp(n, g_, x0_ss)
            p.wait("tensor", "VV", lerp_vv)
            for tk in range(4):
                tt = g_ * 4 + tk
                if pname == "w":
                    p.wait("tensor", "SS", bank_ev[0])
                    tcn = 0
                    for f in range(8):
                        fn = (lambda t, o=pA[:, 0:64], l=sb["lerp_l"][:, f * 512 + tk * 128: f * 512 + (tk + 1) * 128],
                              r=sb["wbuf_l"][:, f * 64:(f + 1) * 64],
                              s0=(f == 0), s1=(f == 7): t.matmul(o, l, r, start=s0, stop=s1))
                        if f == 7:
                            tcn = p.inc("tensor", fn, "TT")
                        else:
                            p.emit("tensor", fn)
                    p.wait("scalar", "TT", tcn)
                    bank_ev[0] = p.inc("scalar", lambda s: s.activation(p.sb["tnh"][:], pA[:, 0:64], AF.Tanh), "SS")
                    p.wait("tensor", "SS", bank_ev[0])
                    ttr = p.inc("tensor", lambda t: t.transpose(p.ps["ptr"][0:64, 0:128],
                                p.sb["tnh"][:], p.sb["identb_s"][:]), "TT")
                    p.wait("scalar", "TT", ttr)
                    etr = p.inc("scalar", lambda s: s.activation(p.sb["tnhT"][:], p.ps["ptr"][0:64, 0:128], AF.Copy), "SS")
                    p.wait("tensor", "SS", etr)
                    p.wait("tensor", "SS", bank_ev[1])
                    p.emit("tensor", lambda t: t.matmul(pB[:, 0:512], p.sb["tnhT"][:], p.sb["ww2t_s"][:], start=True, stop=False))
                    tcn = p.inc("tensor", lambda t: t.matmul(pB[:, 0:512], p.sb["ones1bl"][:], p.sb["wbrow_s"][:], start=False, stop=True), "TT")
                    p.wait("scalar", "TT", tcn)
                    p.wait("scalar", "ST", st_g)
                    bank_ev[1] = p.inc("scalar", lambda s: s.activation(p.sb["wstg"][:], pB[:, 0:512], AF.Exp), "SS")
                    p.wait("gpsimd", "SS", bank_ev[1])
                    st_g = p.dma("gpsimd", p.dram["wexp_d"].ap()[tt * 128:(tt + 1) * 128, :], sb["wstg"][:], "ST")
                else:
                    nch = ncols // 512
                    for ch in range(nch):
                        bk = ch % 2
                        p.wait("tensor", "SS", bank_ev[bk])
                        tcn = 0
                        for f in range(8):
                            fn = (lambda t, o=pq[bk][:, 0:512], l=sb["lerp_l"][:, f * 512 + tk * 128: f * 512 + (tk + 1) * 128],
                                  r=sb["wbuf_l"][:, f * ncols + ch * 512: f * ncols + (ch + 1) * 512],
                                  s0=(f == 0), s1=(f == 7): t.matmul(o, l, r, start=s0, stop=s1))
                            if f == 7:
                                tcn = p.inc("tensor", fn, "TT")
                            else:
                                p.emit("tensor", fn)
                        p.wait("scalar", "TT", tcn)
                        dst = sb["sgb"][:, ch * 512:(ch + 1) * 512]
                        if ch == 0:
                            p.wait("scalar", "ST", st_g)
                        bank_ev[bk] = p.inc("scalar", lambda s, a=dst, o=pq[bk][:, 0:512]: s.activation(a, o, AF.Copy), "SS")
                    p.wait("gpsimd", "SS", bank_ev[(nch - 1) % 2])
                    dd = {"v": p.dram["vdram"], "g": p.dram["gdram"], "k": p.dram["kdram"], "r": p.dram["rdram"]}[pname]
                    width = 1024 if pname in ("v", "g") else 512
                    st_g = p.dma("gpsimd", dd.ap()[tt * 128:(tt + 1) * 128, :], sb["sgb"][:, 0:width], "ST")
    proj_done_ss = p.cnt["SS"]
    p.wait("gpsimd", "ST", p.cnt["ST"])

    p.wait("vector", "SS", proj_done_ss)
    for t in range(NTILE):
        p.wait("sync", "VV", p.cnt["VV"])
        lk2 = p.dma("sync", sb["ktmp"][:], p.dram["kdram"].ap()[t * 128:(t + 1) * 128, :], "LK")
        lk2 = p.dma("sync", sb["rtmp"][:], p.dram["rdram"].ap()[t * 128:(t + 1) * 128, :], "LK")
        p.wait("vector", "LK", lk2)
        vfast(lambda v: v.tensor_tensor(p.sb["mub"][:], p.sb["rtmp"][:], p.sb["bonus_b"][:], op=ALU.mult))
        vfast(lambda v: v.tensor_tensor(p.sb["mub"][:], p.sb["mub"][:], p.sb["ktmp"][:], op=ALU.mult))
        for h in range(4):
            vstep(lambda v, t_=t, h_=h: v.reduce_sum(p.sb["diag_l"][:, t_ * 4 + h_:t_ * 4 + h_ + 1],
                  p.sb["mub"][:, h_ * 128:(h_ + 1) * 128], axis=AX.X))
    diag_vv = p.cnt["VV"]

    sc_ld = {"n": 0}
    sc_ss = {"n": p.cnt["SS"]}
    sc_vv = {"n": p.cnt["VV"]}

    def load_subchunk(j, u, with_rk):
        tt = 2 * j + u // 2
        b = (u % 2) * 64
        p.wait("sync", "VV", p.cnt["VV"])
        p.wait("sync", "SS", p.cnt["SS"])
        ll = p.dma("sync", sb["scW"][0:64, :], sb["wexpc"][:, (u // 2) * 512:(u // 2 + 1) * 512][b:b + 64, :], "LL")
        ll = p.dma("sync", sb["scV"][0:64, :], p.dram["vdram"].ap()[tt * 128 + b: tt * 128 + b + 64, :], "LL")
        ll = p.dma("sync", sb["scK"][0:64, :], p.dram["kdram"].ap()[tt * 128 + b: tt * 128 + b + 64, :], "LL")
        if with_rk:
            ll = p.dma("sync", sb["scR"][0:64, :], p.dram["rdram"].ap()[tt * 128 + b: tt * 128 + b + 64, :], "LL")
            ll = p.dma("sync", sb["scDg"][0:64, :], sb["diag_l"][:, tt * 4:(tt + 1) * 4][b:b + 64, :], "LL")
        return ll

    def subchunk_m(ll, parity):
        p.wait("tensor", "LL", ll)
        p.wait("tensor", "SS", sc_ss["n"])
        p.wait("tensor", "VV", sc_vv["n"])
        t1 = p.inc("tensor", lambda t: t.matmul(pA[0:64, 0:512], p.sb["tri64_s"][0:64, :], p.sb["scW"][0:64, :], start=True, stop=True), "TT")
        p.wait("scalar", "TT", t1)
        e1 = p.inc("scalar", lambda s: s.activation(p.sb["mt_sb"][0:64, :], pA[0:64, 0:512], AF.Copy), "SS")
        p.wait("tensor", "SS", e1)
        t2 = p.inc("tensor", lambda t: t.matmul(pB[0:1, 0:512], p.sb["sel63_s"][0:64, :], p.sb["mt_sb"][0:64, :], start=True, stop=True), "TT")
        p.wait("scalar", "TT", t2)
        e2 = p.inc("scalar", lambda s: s.activation(p.sb["mrow"][0:1, 0:512], pB[0:1, 0:512], AF.Copy), "SS")
        e2 = p.inc("scalar", lambda s: s.activation(p.sb["mrow"][0:1, 512:1024], pB[0:1, 0:512], AF.Copy, scale=np.float32(0.5)), "SS")
        p.wait("tensor", "SS", e2)
        t3 = p.inc("tensor", lambda t: t.matmul(p.ps["pc11"][0:64, 0:512], p.sb["onesrow64_s"][:], p.sb["mrow"][0:1, 0:512], start=True, stop=True), "TT")
        t3 = p.inc("tensor", lambda t: t.matmul(p.ps["pc10"][0:64, 0:512], p.sb["onesrow64_s"][:], p.sb["mrow"][0:1, 512:1024], start=True, stop=True), "TT")
        return e2, t3

    def subchunk_factors(bc_tt, need_ar):
        p.wait("vector", "TT", bc_tt)
        vfast(lambda v: v.tensor_tensor(p.sb["facf"][0:64, 0:512], p.sb["mt_sb"][0:64, :], p.sb["scW"][0:64, :], op=ALU.subtract))
        if need_ar:
            vfast(lambda v: v.tensor_tensor(p.sb["facf"][0:64, 512:1024], p.sb["facf"][0:64, 0:512], p.ps["pc10"][0:64, 0:512], op=ALU.subtract))
            svc = vfast(lambda v, s=p.const_col(-40.0, 64): v.tensor_scalar_max(p.sb["facf"][0:64, 512:1024], p.sb["facf"][0:64, 512:1024], s))
            p.wait("scalar", "VV", svc)
            sA = p.inc("scalar", lambda s: s.activation(p.sb["fac"][0:64, 0:512], p.sb["facf"][0:64, 512:1024], AF.Exp, scale=np.float32(-1.0), bias=np.float32(-20.0)), "SS")
            p.wait("vector", "SS", sA)
            vfast(lambda v: v.tensor_tensor(p.sb["facf"][0:64, 512:1024], p.sb["mt_sb"][0:64, :], p.ps["pc10"][0:64, 0:512], op=ALU.subtract))
            svc = vfast(lambda v, s=p.const_col(40.0, 64): v.tensor_scalar_min(p.sb["facf"][0:64, 512:1024], p.sb["facf"][0:64, 512:1024], s))
            p.wait("scalar", "VV", svc)
            sA = p.inc("scalar", lambda s: s.activation(p.sb["fac"][0:64, 512:1024], p.sb["facf"][0:64, 512:1024], AF.Exp, scale=np.float32(1.0), bias=np.float32(-20.0)), "SS")
            p.wait("vector", "SS", sA)
        vfast(lambda v: v.tensor_tensor(p.sb["facf"][0:64, 512:1024], p.ps["pc11"][0:64, 0:512], p.sb["mt_sb"][0:64, :], op=ALU.subtract))
        svc = p.cnt["VV"]
        p.wait("scalar", "VV", svc)
        sA = p.inc("scalar", lambda s: s.activation(p.sb["fac"][0:64, 1024:1536], p.sb["facf"][0:64, 512:1024], AF.Exp, scale=np.float32(-1.0)), "SS")
        p.wait("vector", "SS", sA)
        if need_ar:
            vfast(lambda v: v.tensor_tensor(p.sb["sc1"][0:64, 0:512], p.sb["scR"][0:64, :], p.sb["fac"][0:64, 0:512], op=ALU.mult))
            vfast(lambda v: v.tensor_tensor(p.sb["sc1"][0:64, 512:1024], p.sb["scK"][0:64, :], p.sb["fac"][0:64, 512:1024], op=ALU.mult))
        vfast(lambda v: v.tensor_tensor(p.sb["sc1"][0:64, 1024:1536], p.sb["scK"][0:64, :], p.sb["fac"][0:64, 1024:1536], op=ALU.mult))
        return p.cnt["VV"], sA

    def subchunk_dcols(parity, with_emid):
        tdc = 0
        p.wait("tensor", "SS", sc_ss["n"])
        for h in range(4):
            tdc = p.inc("tensor", lambda t, h_=h: t.transpose(pB[:, 8 + h_:9 + h_],
                        p.sb["mrow"][0:1, h_ * 128:(h_ + 1) * 128], p.sb["identf1_s"][0:1, 0:1]), "TT")
        p.wait("scalar", "TT", tdc)
        edc = 0
        for h in range(4):
            edc = p.inc("scalar", lambda s, h_=h: s.activation(p.sb["dcol"][:, h_:h_ + 1], pB[:, 8 + h_:9 + h_], AF.Exp, scale=np.float32(-1.0)), "SS")
            if with_emid:
                edc = p.inc("scalar", lambda s, h_=h: s.activation(p.sb["dcol"][:, 4 + h_:5 + h_], pB[:, 8 + h_:9 + h_], AF.Exp, scale=np.float32(-0.5), bias=np.float32(20.0)), "SS")
        return edc

    def subchunk_bsub_supdate(prod_vv, edc):
        p.wait("vector", "SS", edc)
        p.wait("tensor", "SS", p.cnt["SS"])
        for h in range(4):
            p.wait("tensor", "VV", p.cnt["VV"])
            tb2 = p.inc("tensor", lambda t, h_=h: t.matmul(p.ps["pc00"][:, 0:256], p.sb["sc1"][0:64, 1024 + h_ * 128:1024 + (h_ + 1) * 128],
                        p.sb["scV"][0:64, h_ * 256:(h_ + 1) * 256], start=True, stop=True), "TT")
            vstep(lambda v, h_=h: v.tensor_scalar_mul(p.sb["S0"][:, h_ * 256:(h_ + 1) * 256],
                  p.sb["S0"][:, h_ * 256:(h_ + 1) * 256], p.sb["dcol"][:, h_:h_ + 1]))
            p.wait("vector", "TT", tb2)
            vstep(lambda v, h_=h: v.tensor_tensor(p.sb["S0"][:, h_ * 256:(h_ + 1) * 256],
                  p.sb["S0"][:, h_ * 256:(h_ + 1) * 256], p.ps["pc00"][:, 0:256], op=ALU.add))
        sc_vv["n"] = p.cnt["VV"]
        sc_ss["n"] = p.cnt["SS"]

    one_col = p.const_col(1.0)
    stA = 0
    for j in range(8):
        p.wait("sync", "TT", p.cnt["TT"])
        lw = p.dma("sync", sb["wexpc"][:, 0:512], p.dram["wexp_d"].ap()[(2 * j) * 128:(2 * j + 1) * 128, :], "LW")
        lw = p.dma("sync", sb["wexpc"][:, 512:1024], p.dram["wexp_d"].ap()[(2 * j + 1) * 128:(2 * j + 2) * 128, :], "LW")
        p.wait("sync", "LW", lw)
        p.wait("tensor", "LW", lw)
        p.wait("tensor", "SS", sc_ss["n"])
        tA = 0
        for tt2 in range(2):
            tA = p.inc("tensor", lambda t, t2=tt2: t.matmul(pB[0:1, 0:512], one_col, p.sb["wexpc"][:, t2 * 512:(t2 + 1) * 512],
                       start=(t2 == 0), stop=(t2 == 1)), "TT")
        p.wait("scalar", "TT", tA)
        p.wait("scalar", "ST", stA)
        eA = p.inc("scalar", lambda s: s.activation(p.sb["arow_st"][:], pB[0:1, 0:512], AF.Exp, scale=np.float32(-1.0)), "SS")
        p.wait("gpsimd", "SS", eA)
        stA = p.dma("gpsimd", p.dram["agA_in"].ap()[j:j + 1, :], sb["arow_st"][:], "ST")
        sc_ss["n"] = p.cnt["SS"]
        zz = vstep(lambda v: v.memset(p.sb["S0"][:], 0.0))
        sc_vv["n"] = zz
        for u in range(4):
            ll = load_subchunk(j, u, with_rk=False)
            mt_ss, bc_tt = subchunk_m(ll, u % 2)
            prod_vv, sA = subchunk_factors(bc_tt, need_ar=False)
            edc = subchunk_dcols(u % 2, with_emid=False)
            p.wait("tensor", "VV", prod_vv)
            subchunk_bsub_supdate(prod_vv, edc)
            p.wait("sync", "TT", p.cnt["TT"])
        bvv = p.cnt["VV"]
        p.wait("scalar", "VV", bvv)
        eB = 0
        for h in range(4):
            eB = p.inc("scalar", lambda s, h_=h: s.activation(p.sb["bchunk"][:, h_ * 256:(h_ + 1) * 256],
                       p.sb["S0"][:, h_ * 256:(h_ + 1) * 256], AF.Copy), "SS")
        p.wait("gpsimd", "SS", eB)
        stA = p.dma("gpsimd", p.dram["agB_in"].ap()[j * 512:(j + 1) * 512, :].rearrange("(h p) v -> p h v", p=128),
                    sb["bchunk"][:].rearrange("p (h v) -> p h v", h=4), "ST")
        sc_ss["n"] = p.cnt["SS"]

    p.wait("gpsimd", "ST", p.cnt["ST"])
    cc = p.collective("AllGather", ALU.bypass, [p.dram["agA_in"].ap().opt()], [p.dram["agA_out"].ap().opt()])
    cc = p.collective("AllGather", ALU.bypass, [p.dram["agB_in"].ap().opt()], [p.dram["agB_out"].ap().opt()])
    p.wait("sync", "CC", cc)
    p.wait("gpsimd", "CC", cc)

    la = p.dma("sync", sb["sq"][0:64, 0:512], p.dram["agA_out"].ap(), "LK")
    p.wait("tensor", "LK", la)
    p.wait("tensor", "SS", p.cnt["SS"])
    ta2 = 0
    for h in range(4):
        ta2 = p.inc("tensor", lambda t, h_=h: t.transpose(pB[:, h_ * 64:(h_ + 1) * 64],
                    p.sb["sq"][0:64, h_ * 128:(h_ + 1) * 128], p.sb["identf_l"][0:64, 0:64]), "TT")
    p.wait("scalar", "TT", ta2)
    ea2 = p.inc("scalar", lambda s: s.activation(p.sb["acol"][:], pB[:, 0:256], AF.Copy), "SS")
    p.wait("tensor", "SS", ea2)
    p.wait("vector", "SS", ea2)

    zz = vstep(lambda v: v.memset(p.sb["S0"][:], 0.0))
    bld_tt = [0, 0]
    stS = 0
    for m in range(64):
        r_ = 8 * (m % 8) + m // 8
        buf = m % 2
        p.wait("sync", "VV", bld_tt[buf])
        lb = p.dma("sync", sb["bload"][:, buf * 1024:(buf + 1) * 1024].rearrange("p (h v) -> p h v", h=4),
                   p.dram["agB_out"].ap()[r_ * 512:(r_ + 1) * 512, :].rearrange("(h p) v -> p h v", p=128), "LK")
        p.wait("scalar", "VV", p.cnt["VV"])
        p.wait("scalar", "ST", stS)
        eS = p.inc("scalar", lambda s: s.activation(p.sb["bchunk"][:], p.sb["S0"][:], AF.Copy), "SS")
        p.wait("gpsimd", "SS", eS)
        stS = p.dma("gpsimd", p.dram["sin_d"].ap()[m * 512:(m + 1) * 512, :].rearrange("(h p) v -> p h v", p=128),
                    sb["bchunk"][:].rearrange("p (h v) -> p h v", h=4), "ST")
        p.wait("vector", "SS", eS)
        p.wait("vector", "LK", lb)
        for h in range(4):
            vstep(lambda v, h_=h, r2=r_: v.tensor_scalar_mul(p.sb["S0"][:, h_ * 256:(h_ + 1) * 256],
                  p.sb["S0"][:, h_ * 256:(h_ + 1) * 256], p.sb["acol"][:, h_ * 64 + r2:h_ * 64 + r2 + 1]))
            vstep(lambda v, h_=h, b_=buf: v.tensor_tensor(p.sb["S0"][:, h_ * 256:(h_ + 1) * 256],
                  p.sb["S0"][:, h_ * 256:(h_ + 1) * 256], p.sb["bload"][:, b_ * 1024 + h_ * 256: b_ * 1024 + (h_ + 1) * 256], op=ALU.add))
        bld_tt[buf] = p.cnt["VV"]

    sld_vv = [0, 0]
    for j in range(8):
        zz = vstep(lambda v: v.memset(p.sb["S0"][:], 0.0))
        for c2 in range(8):
            sbuf_i = c2 % 2
            p.wait("sync", "VV", sld_vv[sbuf_i])
            ls = p.dma("sync", sb["sload"][:, sbuf_i * 1024:(sbuf_i + 1) * 1024].rearrange("p (h v) -> p h v", h=4),
                       p.dram["sin_d"].ap()[(8 * j + c2) * 512:(8 * j + c2 + 1) * 512, :].rearrange("(h p) v -> p h v", p=128), "LK")
            p.wait("vector", "LK", ls)
            sld_vv[sbuf_i] = vstep(lambda v, b_=sbuf_i, c3=c2: v.scalar_tensor_tensor(p.sb["S0"][:],
                                   p.sb["sload"][:, b_ * 1024:(b_ + 1) * 1024], p.sb["indc_s"][:, c3:c3 + 1],
                                   p.sb["S0"][:], op0=ALU.mult, op1=ALU.add))
        sc_vv["n"] = p.cnt["VV"]
        p.wait("sync", "TT", p.cnt["TT"])
        lw = p.dma("sync", sb["wexpc"][:, 0:512], p.dram["wexp_d"].ap()[(2 * j) * 128:(2 * j + 1) * 128, :], "LW")
        lw = p.dma("sync", sb["wexpc"][:, 512:1024], p.dram["wexp_d"].ap()[(2 * j + 1) * 128:(2 * j + 2) * 128, :], "LW")
        for u in range(4):
            tt = 2 * j + u // 2
            b = (u % 2) * 64
            ll = load_subchunk(j, u, with_rk=True)
            mt_ss, bc_tt = subchunk_m(ll, u % 2)
            prod_vv, sA = subchunk_factors(bc_tt, need_ar=True)
            edc = subchunk_dcols(u % 2, with_emid=True)
            ptb = ptrs[u % 2]
            p.wait("tensor", "VV", prod_vv)
            p.wait("tensor", "SS", sc_ss["n"])
            ttr = 0
            for h in range(4):
                ttr = p.inc("tensor", lambda t, h_=h, o=ptb: t.transpose(o[:, h_ * 64:(h_ + 1) * 64],
                            p.sb["sc1"][0:64, h_ * 128:(h_ + 1) * 128], p.sb["ident64_s"][0:64, :]), "TT")
            for h in range(4):
                ttr = p.inc("tensor", lambda t, h_=h, o=ptb: t.transpose(o[:, 256 + h_ * 64:256 + (h_ + 1) * 64],
                            p.sb["sc1"][0:64, 512 + h_ * 128:512 + (h_ + 1) * 128], p.sb["ident64_s"][0:64, :]), "TT")
            p.wait("scalar", "TT", ttr)
            etr = p.inc("scalar", lambda s, o=ptb: s.activation(p.sb["rt_sc"][:, 0:512], o[:, 0:512], AF.Copy), "SS")
            sc_ss["n"] = etr
            p.wait("vector", "SS", edc)
            for h in range(4):
                vstep(lambda v, h_=h: v.tensor_scalar_mul(p.sb["Stmp"][:, h_ * 256:(h_ + 1) * 256],
                      p.sb["S0"][:, h_ * 256:(h_ + 1) * 256], p.sb["dcol"][:, 4 + h_:5 + h_]))
            stmp_vv = p.cnt["VV"]
            p.wait("tensor", "SS", etr)
            p.wait("tensor", "VV", stmp_vv)
            tat = 0
            for h in range(4):
                tat = p.inc("tensor", lambda t, h_=h: t.matmul(pA[0:64, h_ * 64:(h_ + 1) * 64],
                            p.sb["rt_sc"][:, 256 + h_ * 64:256 + (h_ + 1) * 64],
                            p.sb["rt_sc"][:, h_ * 64:(h_ + 1) * 64], start=True, stop=True), "TT")
            p.wait("vector", "TT", tat)
            for h in range(4):
                vstep(lambda v, h_=h: v.tensor_tensor(p.sb["at_sb"][0:64, h_ * 64:(h_ + 1) * 64],
                      pA[0:64, h_ * 64:(h_ + 1) * 64], p.sb["mask64_s"][0:64, :], op=ALU.mult))
                vstep(lambda v, h_=h: v.tensor_scalar_mul(p.sb["scDm"][0:64, :], p.sb["ident64_s"][0:64, :],
                      p.sb["scDg"][0:64, h_:h_ + 1]))
                vstep(lambda v, h_=h: v.tensor_tensor(p.sb["at_sb"][0:64, h_ * 64:(h_ + 1) * 64],
                      p.sb["at_sb"][0:64, h_ * 64:(h_ + 1) * 64], p.sb["scDm"][0:64, :], op=ALU.add))
            at_vv = p.cnt["VV"]
            p.wait("tensor", "VV", at_vv)
            tO = 0
            for h in range(4):
                po = p.ps["pc00"] if h < 2 else p.ps["pc01"]
                off = (h % 2) * 256
                p.emit("tensor", lambda t, h_=h, o=po[0:64, off:off + 256]: t.matmul(
                    o, p.sb["at_sb"][0:64, h_ * 64:(h_ + 1) * 64], p.sb["scV"][0:64, h_ * 256:(h_ + 1) * 256],
                    start=True, stop=False))
                tO = p.inc("tensor", lambda t, h_=h, o=po[0:64, off:off + 256]: t.matmul(
                    o, p.sb["rt_sc"][:, h_ * 64:(h_ + 1) * 64], p.sb["Stmp"][:, h_ * 256:(h_ + 1) * 256],
                    start=False, stop=True), "TT")
            p.wait("scalar", "TT", tO)
            eO = p.inc("scalar", lambda s: s.activation(p.sb["ofin"][0:64, 0:512], p.ps["pc00"][0:64, 0:512], AF.Copy), "SS")
            eO = p.inc("scalar", lambda s: s.activation(p.sb["ofin"][0:64, 512:1024], p.ps["pc01"][0:64, 0:512], AF.Copy), "SS")
            p.wait("gpsimd", "SS", eO)
            stO = p.dma("gpsimd", sb["xlt"][:, tt * 1024:(tt + 1) * 1024][b:b + 64, :], sb["ofin"][0:64, :], "ST")
            sc_ss["n"] = p.cnt["SS"]
            subchunk_bsub_supdate(prod_vv, edc)
            p.wait("sync", "TT", p.cnt["TT"])
            p.wait("sync", "ST", stO)
            if L1_DEBUG_STOP == 1 and j == 0 and u == 0:
                p.wait("gpsimd", "ST", p.cnt["ST"])
                return p.cnt["VV"]
    o_done_st = p.cnt["ST"]
    if L1_DEBUG_STOP == 2:
        p.wait("gpsimd", "ST", p.cnt["ST"])
        return p.cnt["VV"]

    p.wait("vector", "TT", p.cnt["TT"])
    wvo = emit_l1_weight(p, "owt_l", 1024, sb["wbuf_l"])
    p.wait("tensor", "VV", wvo)
    st_fin = 0
    gb_ld = 0
    gb_guard = 0
    for t in range(NTILE):
        p.wait("sync", "VV", p.cnt["VV"])
        p.wait("sync", "SS", p.cnt["SS"])
        lg = p.dma("sync", sb["sgb"][:], p.dram["gdram"].ap()[t * 128:(t + 1) * 128, :], "LK")
        p.wait("scalar", "LK", lg)
        p.wait("scalar", "ST", o_done_st)
        sg_ss = p.inc("scalar", lambda s: s.activation(p.sb["sgb"][:], p.sb["sgb"][:], AF.Silu), "SS")
        gb_ld = sg_ss
        osl = sb["xlt"][:, t * 1024:(t + 1) * 1024]
        p.wait("vector", "ST", o_done_st)
        for h in range(4):
            p.wait("vector", "SS", p.cnt["SS"])
            oh = osl[:, h * 256:(h + 1) * 256]
            negm = sb["stat"][:, 0:1]
            sd = sb["stat"][:, 1:2]
            rstd = sb["stat"][:, 2:3]
            vstep(lambda v, a=negm, b=oh: v.reduce_sum(a, b, axis=AX.X, negate=True))
            nv = vstep(lambda v, a=negm, s=p.const_col(1.0 / 256.0): v.tensor_scalar_mul(a, a, s))
            p.wait("scalar", "VV", nv)
            sqs = p.inc("scalar", lambda s, a=sb["mub"][:, 0:256], b=oh, m=negm:
                        s.activation(a, b, AF.Square, bias=m, scale=np.float32(1.0)), "SS")
            p.wait("vector", "SS", sqs)
            vstep(lambda v, a=sd: v.reduce_sum(a, p.sb["mub"][:, 0:256], axis=AX.X))
            nv = vstep(lambda v, a=sd, s=p.const_col(1.0 / 256.0): v.tensor_scalar_mul(a, a, s))
            p.wait("scalar", "VV", nv)
            sqs = p.inc("scalar", lambda s, a=sd: s.activation(a, a, AF.Sqrt, bias=np.float32(1e-5), scale=np.float32(1.0)), "SS")
            p.wait("vector", "SS", sqs)
            vstep(lambda v, a=rstd, b=sd: v.reciprocal(a, b))
            vstep(lambda v, a=sb["ofin"][:, h * 256:(h + 1) * 256], b=oh, m=negm, r=rstd:
                  v.tensor_scalar(a, b, m, r, op0=ALU.add, op1=ALU.mult))
        vfast(lambda v: v.tensor_tensor(p.sb["ofin"][:], p.sb["ofin"][:], p.sb["gnw_b"][:], op=ALU.mult))
        vfast(lambda v: v.tensor_tensor(p.sb["ofin"][:], p.sb["ofin"][:], p.sb["gnb_b"][:], op=ALU.add))
        p.wait("vector", "SS", sg_ss)
        fvv = vstep(lambda v: v.tensor_tensor(p.sb["ofin"][:], p.sb["ofin"][:], p.sb["sgb"][:], op=ALU.mult))
        gb_guard = fvv
        ec2 = emit_transpose_tile(p, sb["ofin"][:], sb["oft"][:].rearrange("p (f c) -> p f c", f=8),
                                  "VV", fvv)
        p.wait("tensor", "SS", ec2)
        for ch in range(2):
            bk = ch % 2
            p.wait("tensor", "VV", p.cnt["VV"])
            tcn = 0
            for f in range(8):
                fn = (lambda t, o=pq[bk][:, 0:512], l=sb["oft"][:, f * 128:(f + 1) * 128],
                      r=sb["wbuf_l"][:, f * 1024 + ch * 512: f * 1024 + (ch + 1) * 512],
                      s0=(f == 0), s1=(f == 7): t.matmul(o, l, r, start=s0, stop=s1))
                if f == 7:
                    tcn = p.inc("tensor", fn, "TT")
                else:
                    p.emit("tensor", fn)
            p.wait("vector", "TT", tcn)
            if L1_DEBUG_STOP == 3:
                last = vstep(lambda v, a=sb["mub"][:], b=pq[bk][:, 0:512]: v.tensor_copy(a, b))
            else:
                last = vstep(lambda v, a=sb["xres"][:, t * D + ch * 512: t * D + (ch + 1) * 512], b=pq[bk][:, 0:512]:
                             v.tensor_tensor(a, a, b, op=ALU.add))
        if L1_DEBUG_STOP == 4 and t == 0:
            dxr = p.dram["dxr"].ap()
            p.wait("gpsimd", "VV", last)
            std = p.dma("gpsimd", dxr[0:128, :], sb["xres"][:, 0:1024], "ST")
            std = p.dma("gpsimd", dxr[128:256, :], sb["ofin"][:], "ST")
            std = p.dma("gpsimd", dxr[256:384, :], sb["sgb"][:], "ST")
            p.wait("sync", "ST", std)
    return last



def _barrier(p):
    for eng in ENGINES:
        for s2 in ("VV", "SS", "TT", "LD", "LK", "LV", "ST", "CC"):
            if s2 in p.cnt and p.cnt[s2] > 0:
                p.wait(eng, s2, p.cnt[s2])


def build_fused(nphases=6):
    p = Prog("mkfused")
    p.din("xin", [SHARD, D], F32)
    p.dout("xout", [SHARD, D], F32)
    p.din("identb", [128, 128], BF16)
    p.din("ones1", [1, 128], BF16)
    p.din("onesrow", [1, 128], F32)
    p.sbuf("xres", [128, NTILE * D], F32)
    p.sbuf("stat", [128, 16], F32)
    p.sbuf("xlt", [128, 16 * D], BF16)
    p.sbuf("identb_s", [128, 128], BF16)
    p.sbuf("ones1b", [1, 128], BF16)
    p.sbuf("onesrow_l", [1, 128], F32)
    p.psum("ptr", [128, 1024], BF16)
    p.psum("ptr2", [128, 1024], BF16)
    p.psum("pA", [128, 512], F32)
    p.psum("pB", [128, 512], F32)
    p.psum("pc00", [128, 512], F32)
    p.psum("pc01", [128, 512], F32)
    p.psum("pc10", [128, 512], F32)
    p.psum("pc11", [128, 512], F32)
    p.ps["pwa"] = p.ps["pc00"]
    p.ps["pwb"] = p.ps["pc01"]
    p.dma("sync", p.sb["cpool_s"][:], p.dram["cpool"].ap(), "LD")
    p.dma("sync", p.sb["identb_s"][:], p.dram["identb"].ap(), "LD")
    p.dma("sync", p.sb["ones1b"][:], p.dram["ones1"].ap(), "LD")
    p.dma("sync", p.sb["onesrow_l"][:], p.dram["onesrow"].ap(), "LD")
    ld_x = emit_load_xres(p, p.dram["xin"].ap())
    def scope4(p):
        p.sbuf("xlnb", [128, 4 * D], BF16)
        p.sbuf("sq", [128, 4 * D], F32)
    last = 0
    p.push_scope(); decl_l0ffn_w(p)
    p.push_scope(); scope4(p); decl_l0ffn(p)
    sc_w = emit_build_w(p)
    last = emit_l0ffn(p, sc_w, ld_x)
    _barrier(p); p.pop_scope()
    if nphases >= 2:
        p.push_scope()
        p.sbuf("xlnb", [128, 2 * D], BF16)
        p.sbuf("sq", [128, 2 * D], F32)
        decl_l0attn(p)
        last = emit_l0attn(p, ld_x)
        _barrier(p); p.pop_scope()
    if nphases >= 3:
        p.push_scope(); scope4(p)
        p.sbuf("brows", [1, 1024], BF16)
        ldb = p.dma("sync", p.sb["brows"][:], p.dram["brow"].ap(), "LD")
        p.wait("tensor", "LD", ldb)
        last = emit_l0ffn(p, sc_w, ld_x)
        _barrier(p); p.pop_scope()
    p.pop_scope()
    def scope_moe(p):
        p.sbuf("xlnb", [128, 2 * D], BF16)
        p.sbuf("sqpad", [128, 1024], F32)
        p.sbuf("sq", [128, 2 * D], F32)
    if nphases >= 4:
        p.push_scope(); scope_moe(p); decl_moe(p)
        last = emit_moe(p, ld_x)
        _barrier(p); p.pop_scope()
    if nphases >= 5:
        p.push_scope()
        p.sbuf("xlnb", [128, 2 * D], BF16)
        p.sbuf("mrow", [1, 1024], F32)
        p.sbuf("sq", [128, 2 * D], F32)
        decl_l1attn(p)
        last = emit_l1attn(p, ld_x)
        _barrier(p); p.pop_scope()
    if nphases >= 6:
        p.push_scope(); scope_moe(p); decl_moe(p)
        last = emit_moe(p, ld_x)
        _barrier(p); p.pop_scope()
    p.wait("gpsimd", "VV", last)
    st = emit_store_xres(p, p.dram["xout"].ap())
    p.wait("gpsimd", "ST", st)
    return p



_CACHE = {}


def _program():
    if "prog" not in _CACHE:
        pf = build_fused()
        _CACHE["prog"] = (pf, pf.finalize())
    return _CACHE["prog"]


def kernel(**inputs):
    from concourse.bass_utils import run_bass_kernel_spmd
    g = lambda k2: np.asarray(inputs[k2])
    pf, nc = _program()

    host = {}
    host.update(l0ffn_host_inputs(g("L0_ffn_G1_R"), g("L0_ffn_G2_R"), g("L0_ffn_G1_I"), g("L0_ffn_G2_I"),
                                  g("L0_ffn_bias_real"), g("L0_ffn_bias_imag")))
    host.update(l1attn_host_inputs({("in_L1_attn_" + k2): v for k2, v in
                                    [(k3[len("L1_attn_"):], g(k3)) for k3 in inputs if k3.startswith("L1_attn_")]}))
    host["identb"] = np.eye(128, dtype=ml_dtypes.bfloat16)
    host["ones1"] = np.ones((1, 128), dtype=ml_dtypes.bfloat16)
    host["onesrow"] = np.ones((1, 128), dtype=np.float32)
    host["identf1"] = np.ones((1, 1), dtype=np.float32)
    host["identf64"] = np.eye(64, dtype=np.float32)
    host["cpool"] = pf.cpool_host()

    x = np.ascontiguousarray(g("L0_attn_X")[0], dtype=np.float32)
    in_maps = []
    for c in range(N_CORES):
        m = dict(host)
        m.update(l0attn_host_inputs(g("L0_attn_W_q"), g("L0_attn_W_k"), g("L0_attn_W_v"), c))
        m.update(moe_host_inputs(g("L1_ffn_router_w"), g("L1_ffn_w1"), g("L1_ffn_w2"), c))
        m["identb"] = host["identb"]
        m["onesrow"] = host["onesrow"]
        ind = np.zeros((128, 8), dtype=np.float32)
        ind[:, c] = 1.0
        m["indc"] = ind
        i8 = np.zeros((8, 1), dtype=ml_dtypes.bfloat16)
        i8[c, 0] = 1.0
        m["indc8"] = i8
        m["xin"] = shard_rows(x, c)
        in_maps.append(m)
    res = run_bass_kernel_spmd(nc, in_maps, core_ids=list(range(N_CORES)))
    out = unshard_rows([np.asarray(res.results[c]["xout"]) for c in range(N_CORES)])
    return np.ascontiguousarray(out, dtype=np.float32).reshape(1, 16384, 1024)
